# version 06
# baseline (speedup 1.0000x reference)
import math
import numpy as np
import ml_dtypes as mld
import concourse.bass as bass
import concourse.mybir as mybir
from concourse.bass_utils import run_bass_kernel_spmd
from contextlib import ExitStack

F32 = mybir.dt.float32
BF16 = mybir.dt.bfloat16
FP8 = mybir.dt.float8e4
AF = mybir.ActivationFunctionType
ALU = mybir.AluOpType
AX = mybir.AxisListType
U32 = mybir.dt.uint32

B, S, D = 2, 4096, 2048
NCH, CHUNK = 16, 256
NOWN = 1024
PADT = [8, 16, 24, 32]
PADL0_ALL = [4, 32, 8, 28, 12, 24, 16, 20]
PADL0_HALF = [PADL0_ALL[0:4], PADL0_ALL[4:8]]
SC0 = 1.0 / math.sqrt(192.0)
SC1 = 1.0 / math.sqrt(64.0)
COFF = 40.0
EPS = 1e-5
ENGS = ("sync", "gpsimd", "vector", "scalar", "tensor")


def own_chunks(j):
    return [j, 7 - j, 8 + j, 15 - j]


class Sched:

    def __init__(self, nc):
        self.nc = nc
        self.items = []
        self.val = {}
        self.sems = {}

    ALIAS = {"km": "pm", "kre": "pe", "krp": "rkr", "nm": "nmm", "qw": "wq",
             "qm": "pm2", "qre": "rqe", "qrp": "rqr", "wm": "wom", "we": "woe",
             "wtr": "wotr", "wte": "wote", "uevm": "uev", "cbm": "vm",
             "cbe": "ve", "sdw": "dww", "sm": "dm", "ytr": "ftr", "yte": "fte",
             "glm": "rkm", "gmx": "rke", "gex": "rsq", "gch": "Ls2",
             "gcb": "Lbv", "gtr": "Lsd", "gte": "Lxn"}

    def _n(self, name):
        return self.ALIAS.get(name, name)

    def sem(self, name):
        name = self._n(name)
        if name not in self.sems:
            self.sems[name] = self.nc.alloc_semaphore(name)
            self.val[name] = 0
        return self.sems[name]

    def op(self, eng, fn, wait=(), inc=None, inc_amt=1):
        wait = [(self._n(w[0]), w[1]) for w in wait if w is not None]
        for s, _ in wait:
            self.sem(s)
        v = None
        if inc is not None:
            inc = self._n(inc)
            self.sem(inc)
            self.val[inc] += inc_amt
            v = self.val[inc]
        self.items.append((eng, fn, tuple(wait), (inc, inc_amt) if inc else None))
        return v

    def dma(self, eng, out_ap, in_ap, inc, wait=(), **kw):
        def fn(e, o=out_ap, i=in_ap, kw=kw):
            return e.dma_start(o, i, **kw)
        return self.op(eng, fn, wait=wait, inc=inc, inc_amt=16)

    def cur(self, name):
        return self.val.get(self._n(name), 0)

    def flush(self, name=None):
        items, self.items = self.items, []
        if not items:
            return
        self._blk = getattr(self, "_blk", 0) + 1
        with self.nc.Block(f"{name or 'blk'}_b{self._blk}") as blk:
            for eng_name in ENGS:
                mine = [(fn, w, i) for (en, fn, w, i) in items if en == eng_name]
                if not mine:
                    continue

                def body(eng, mine=mine):
                    for fn, waits, inc in mine:
                        for s, v in waits:
                            if v > 0:
                                eng.wait_ge(self.sems[s], v)
                        inst = fn(eng)
                        if inc is not None:
                            inst.then_inc(self.sems[inc[0]], inc[1])
                getattr(blk, eng_name)(body)


class Box:
    def __init__(self, d, k):
        self._d = d
        self._k = k

    def __getitem__(self, sl):
        return self._d[self._k][sl]

    def __getattr__(self, a):
        return getattr(self._d[self._k], a)


class RR:

    def __init__(self, sc, n):
        self.sc = sc
        self.n = n
        self.rel = [None] * n

    def use(self, i):
        return self.rel[i % self.n]

    def done(self, i, semname):
        self.rel[i % self.n] = (semname, self.sc.cur(semname))


def emit_ln(sc, xc, xntok, sq_junk, st, wait_x, tag, epsb=None):
    NT = xc.shape[1]

    def vch(fn, extra=(), name=None):
        w = [(tag + "ch", sc.cur(tag + "ch"))] + list(extra)
        sc.op("vector", fn, wait=w, inc=(name or (tag + "ch")))

    for tt in range(NT):
        w0 = [wait_x] if tt == 0 else ()
        sc.op("vector", lambda e, tt=tt: e.reduce_sum(
            st[:, 0:1], xc[:, tt, :], axis=AX.X, negate=True),
            wait=list(w0) + [(tag + "ch", sc.cur(tag + "ch"))],
            inc=tag + "ch")
        sc.op("scalar", lambda e, tt=tt: e.activation(
            sq_junk[:, :], xc[:, tt, :], AF.Square, accum_out=st[:, 1:2]),
            wait=w0, inc=tag + "s2")
        vch(lambda e: e.tensor_scalar(
            st[:, 2:3], st[:, 0:1], 1.0 / D, None, ALU.mult),
            extra=[(tag + "s2", sc.cur(tag + "s2"))])
        vch(lambda e: e.tensor_mul(st[:, 3:4], st[:, 0:1], st[:, 2:3]))
        vch(lambda e: e.tensor_sub(st[:, 4:5], st[:, 1:2], st[:, 3:4]),
            name=tag + "bv")
        sc.op("scalar", lambda e: e.activation(
            st[:, 5:6], st[:, 4:5], AF.Sqrt, bias=epsb[:, 0:1], scale=1.0 / D),
            wait=[(tag + "bv", sc.cur(tag + "bv"))], inc=tag + "sd")
        sc.op("vector", lambda e: e.reciprocal(st[:, 6:7], st[:, 5:6]),
              wait=[(tag + "sd", sc.cur(tag + "sd")),
                    (tag + "ch", sc.cur(tag + "ch"))], inc=tag + "ch")
        vch(lambda e: e.tensor_scalar(
            st[:, 7:8], st[:, 2:3], 1.0, st[:, 6:7], ALU.mult, ALU.mult))
        vch(lambda e, tt=tt: e.tensor_scalar(
            xntok[:, tt, :], xc[:, tt, :], st[:, 6:7], st[:, 7:8],
            ALU.mult, ALU.add), name=tag + "xn")
    sc.op("vector", lambda e: e.engine_nop(),
          wait=[(tag + "xn", sc.cur(tag + "xn"))], inc=tag + "ch")


def emit_tposes(sc, xntok, dst_fn, idb, ptp, rr_tp, tag, n_f=16):
    NT = xntok.shape[1]
    xn_done = sc.cur(tag + "xn")
    n_base = sc.cur(tag + "tr")
    for tt in range(NT):
        for f in range(n_f):
            i = sc.cur(tag + "tr")
            w = [(tag + "xn", xn_done), rr_tp.use(i)]
            sc.op("tensor", lambda e, tt=tt, f=f, i=i: e.transpose(
                ptp[i % 2][:, :], xntok[:, tt, f * 128:(f + 1) * 128], idb[:, :]),
                wait=w, inc=tag + "tr")
            sc.op("scalar", lambda e, tt=tt, f=f, i=i: e.copy(
                dst_fn(f, tt), ptp[i % 2][:, :]),
                wait=[(tag + "tr", sc.cur(tag + "tr"))], inc=tag + "te")
            rr_tp.done(i, tag + "te")


def emit_rope(sc, tmp, x1b, x2b, out1, out2, cos_ap, sin_ap, tag, wait):
    t1, t2 = tmp
    sc.op("vector", lambda e: e.tensor_mul(t1[:, :], x1b, cos_ap), wait=wait)
    sc.op("vector", lambda e: e.tensor_mul(t2[:, :], x2b, sin_ap))
    sc.op("vector", lambda e: e.tensor_sub(out1, t1[:, :], t2[:, :]))
    sc.op("vector", lambda e: e.tensor_mul(t1[:, :], x1b, sin_ap))
    sc.op("vector", lambda e: e.tensor_mul(t2[:, :], x2b, cos_ap))
    sc.op("vector", lambda e: e.tensor_add(out2, t1[:, :], t2[:, :]), inc=tag)


def emit_attention(sc, nh, padt, kf_fn, qx_fn, v_fn, mask_fn, ptsb, psc, pout, pl,
                   pbc, lsb, rbsb, ones_l, ones1_f, out_fn, l_in_out,
                   head_hook=None, nmask=8, rr_sc=None, rr_pt=None, rr_slot=None):
    as_list = lambda x: x if isinstance(x, list) else [x]
    pout, pl = as_list(pout), as_list(pl)
    nbk = len(pout)
    if rr_sc is None:
        rr_sc = RR(sc, 2)
    if rr_pt is None:
        rr_pt = RR(sc, 2)
    if rr_slot is None:
        rr_slot = RR(sc, nbk)
    for h in range(nh):
        if head_hook:
            head_hook(h)
        for s_ in range(4):
            ntile = padt[s_]
            so = sc.cur("aev")
            ob = so % nbk
            rdy = {}

            def scores(t, h=h, s_=s_):
                i = sc.cur("scm")
                w = [rr_sc.use(i)]
                if head_hook:
                    w.append(head_hook(h, t, s_))
                sc.op("tensor", lambda e, t=t, h=h, s_=s_, i=i: kf_fn(
                    e, psc[i % 2], h, s_, t), wait=w, inc="scm")

            def expmask(t, s_=s_, h=h):
                i = sc.cur("scm") - 1
                j = sc.cur("ex")
                w = [("scm", sc.cur("scm")), rr_pt.use(j)]
                sc.op("scalar", lambda e, i=i, j=j: qx_fn(
                    e, ptsb[j % 2], psc[i % 2]), wait=w, inc="ex")
                rr_sc.done(i, "ex")
                if t >= padt[s_] - nmask:
                    rt = t - (padt[s_] - nmask)
                    sc.op("vector", lambda e, j=j, rt=rt, s_=s_: e.tensor_mul(
                        ptsb[j % 2][:, :], ptsb[j % 2][:, :], mask_fn(s_, rt)),
                        wait=[("ex", sc.cur("ex"))], inc="mk")
                    return j, ("mk", sc.cur("mk"))
                return j, ("ex", sc.cur("ex"))

            def pvl(t, h=h, s_=s_, ntile=ntile, so=so, ob=ob):
                j, rd = rdy[t]
                w = [rd]
                if t == 0:
                    w.append(rr_slot.use(so))
                sc.op("tensor", lambda e, t=t, h=h, j=j, ntile=ntile, ob=ob: v_fn(
                    e, pout[ob], h, t, j % 2, t == 0, t == ntile - 1),
                    wait=w, inc="pvd" if l_in_out else None)
                if not l_in_out:
                    sc.op("tensor", lambda e, t=t, j=j, ntile=ntile, ob=ob: e.matmul(
                        pl[ob][:, :], ones_l[:, :], ptsb[j % 2][:, :],
                        start=(t == 0), stop=(t == ntile - 1)), inc="pvd")
                rr_pt.done(j, "pvd")

            scores(0)
            rdy[0] = expmask(0)
            if ntile > 1:
                scores(1)
                rdy[1] = expmask(1)
            for t in range(ntile):
                pvl(t)
                if t + 2 < ntile:
                    scores(t + 2)
                    rdy[t + 2] = expmask(t + 2)
            lsrc = pout[ob][64:65, :] if l_in_out else pl[ob][0:1, :]
            sc.op("vector", lambda e, lsrc=lsrc: e.tensor_copy(lsb[0:1, :], lsrc),
                  wait=[("pvd", sc.cur("pvd"))], inc="lcp")
            sc.op("vector", lambda e: e.reciprocal(lsb[0:1, :], lsb[0:1, :]),
                  wait=[("lcp", sc.cur("lcp"))], inc="rc")
            sc.op("tensor", lambda e: e.matmul(
                pbc[:, :], ones1_f[:, :], lsb[0:1, :], start=True, stop=True),
                wait=[("rc", sc.cur("rc")), ("rbu", sc.cur("rbu"))], inc="bc")
            sc.op("vector", lambda e: e.tensor_copy(rbsb[:, :], pbc[:, :]),
                  wait=[("bc", sc.cur("bc"))], inc="rbu")
            sc.op("vector", lambda e, h=h, s_=s_, ob=ob: out_fn(e, h, s_, ob),
                  wait=[("rbu", sc.cur("rbu"))], inc="aev")
            rr_slot.done(so, "aev")


def nb(x):
    return np.ascontiguousarray(np.asarray(x, np.float32))


def to128(v):
    v = np.asarray(v, np.float32)
    return np.ascontiguousarray(v.reshape(-1, 128).T)


PERM64 = np.concatenate([np.arange(0, 64, 2), np.arange(1, 64, 2)])


def rope_tables_L0():
    pos = np.arange(S, dtype=np.float64)[:, None]
    inv = np.exp(np.arange(0, 128, 2, dtype=np.float64) * (-math.log(1e4) / 128))
    ang = pos * inv[None, :]
    return (np.ascontiguousarray(np.cos(ang)[:, :32].astype(np.float32).T),
            np.ascontiguousarray(np.sin(ang)[:, :32].astype(np.float32).T))


def rope_tables_L1():
    pos = np.arange(S, dtype=np.float64)[:, None]
    inv = 1.0 / (1e4 ** (np.arange(0, 64, 2, dtype=np.float64) / 64))
    ang = pos * inv[None, :]
    return (np.ascontiguousarray(np.cos(ang).astype(np.float32).T),
            np.ascontiguousarray(np.sin(ang).astype(np.float32).T))


def own_token_index(j):
    idx = []
    for c in own_chunks(j):
        idx.extend(range(c * CHUNK, (c + 1) * CHUNK))
    return np.asarray(idx)


def build_masks(j):
    m = np.zeros((4, 8, 128, CHUNK), np.float32)
    ch = own_chunks(j)
    kk = np.arange(128)[:, None]
    qq = np.arange(CHUNK)[None, :]
    for s in range(4):
        for rt in range(8):
            t = 8 * s + rt
            m[s, rt] = ((128 * t + kk) <= (CHUNK * ch[s] + qq)).astype(np.float32)
    return m.astype(mld.bfloat16)


def kview(w_ap, p=128):
    return w_ap.rearrange("(k p) o -> p k o", p=p)


def build_l0():
    nc = bass.Bass("TRN2", target_bir_lowering=False)
    EI, EO = "ExternalInput", "ExternalOutput"
    DT = lambda n, s, d, k="Internal": nc.dram_tensor(n, s, d, kind=k).ap()
    x_full = DT("x_full", [S, D], F32, EI)
    x_own = DT("x_own", [NOWN, D], F32, EI)
    ident = DT("ident", [128, 128], BF16, EI)
    cmask = DT("cmask", [4, 8, 128, CHUNK], BF16, EI)
    cos0 = DT("cos0", [32, S], F32, EI)
    sin0 = DT("sin0", [32, S], F32, EI)
    cos0q = DT("cos0q", [32, NOWN], F32, EI)
    sin0q = DT("sin0q", [32, NOWN], F32, EI)
    w_dkv = DT("w_dkv", [D, 512], F32, EI)
    b_dkv = DT("b_dkv", [128, 4], F32, EI)
    w_rk = DT("w_rk", [D, 64], F32, EI)
    b_rk = DT("b_rk", [32, 2], F32, EI)
    w_dq = DT("w_dq", [D, 1536], F32, EI)
    b_dq = DT("b_dq", [128, 12], F32, EI)
    w_rq = DT("w_rq", [1536, 1024], F32, EI)
    b_rq = DT("b_rq", [32, 32], F32, EI)
    w_uk = DT("w_uk", [512, D], F32, EI)
    b_uk = DT("b_uk", [128, 16], F32, EI)
    w_uv = DT("w_uv", [512, D], F32, EI)
    b_uv = DT("b_uv", [1, D], F32, EI)
    w_uq = DT("w_uq", [1536, D], F32, EI)
    b_uq = DT("b_uq", [128, 16], F32, EI)
    w_o = DT("w_o", [D, D], F32, EI)
    b_o = DT("b_o", [128, 16], F32, EI)
    gw = DT("gw", [D, 6144], F32, EI)
    uw = DT("uw", [D, 6144], F32, EI)
    dw = DT("dw", [6144, D], F32, EI)
    gsc = DT("gsc", [128, 48], F32, EI)
    usc = DT("usc", [128, 48], F32, EI)
    dsc = DT("dsc", [128, 16], F32, EI)
    x1_out = DT("x1_out", [NOWN, D], F32, EO)
    k_sp = DT("k_sp", [16, 128, S], BF16)
    v_sp = DT("v_sp", [128, 32, 16, 128], BF16)

    es = ExitStack()
    esR = ExitStack()
    T = lambda n, s, d: es.enter_context(nc.sbuf_tensor(n, s, d))
    TR = lambda n, s, d: esR.enter_context(nc.sbuf_tensor(n, s, d, side="right"))
    sc = Sched(nc)

    idb = T("idb", [128, 128], BF16)
    idf = T("idf", [128, 128], F32)
    msk = T("msk", [128, 4 * 8 * CHUNK], BF16)
    kmax = T("kmax", [1, NCH * 16], F32)
    Km = T("Km", [1, 16 * 4], F32)
    onesc = T("onesc", [128, 1], BF16)
    epsb = T("epsb", [128, 1], F32)
    ones1b = T("ones1b", [1, 128], BF16)
    ones1f = T("ones1f", [1, 128], F32)
    bias_dkv = T("bias_dkv", [128, 4], F32)
    bias_rk = T("bias_rk", [32, 2], F32)
    bias_dq = T("bias_dq", [128, 12], F32)
    bias_rq = T("bias_rq", [32, 32], F32)
    bias_uk = T("bias_uk", [128, 16], F32)
    bias_uv = T("bias_uv", [1, D], BF16)
    bias_uq = T("bias_uq", [128, 16], F32)
    bias_o = T("bias_o", [128, 16], F32)
    gsc_sb = T("gsc_sb", [128, 48], F32)
    usc_sb = T("usc_sb", [128, 48], F32)
    dsc_sb = T("dsc_sb", [128, 16], F32)
    xnT_own = T("xnT_own", [128, 16, NOWN], BF16)
    ropek = TR("ropek", [128, S], BF16)
    c0q = TR("c0q", [32, NOWN], F32)
    s0q = TR("s0q", [32, NOWN], F32)

    for dst, src in [(idb, ident), (bias_dkv, b_dkv), (bias_rk, b_rk),
                     (bias_dq, b_dq), (bias_rq, b_rq), (bias_uk, b_uk),
                     (bias_uq, b_uq), (bias_o, b_o), (gsc_sb, gsc),
                     (usc_sb, usc), (dsc_sb, dsc), (c0q, cos0q),
                     (s0q, sin0q)]:
        sc.dma("sync", dst[:, :], src, "cld")
    sc.dma("sync", msk.ap().rearrange("p (a q) -> p a q", q=CHUNK),
           cmask.rearrange("s r p q -> p (s r) q"), "cld")
    sc.dma("gpsimd", bias_uv[:, :], b_uv, "cld")
    sc.dma("gpsimd", idf[:, :], ident, "cld")
    sc.dma("sync", x1_out[:, :], x_own, "x1c")
    sc.op("vector", lambda e: e.memset(onesc[:, :], 1.0))
    sc.op("vector", lambda e: e.memset(epsb[:, :], EPS))
    sc.op("vector", lambda e: e.memset(ones1b[:, :], 1.0))
    sc.op("vector", lambda e: e.memset(ones1f[:, :], 1.0))
    sc.op("vector", lambda e: e.memset(ropek[64:65, :], 1.0))
    sc.flush("p0")

    NT = 2
    esL = ExitStack()
    TL = lambda n, s, d: esL.enter_context(nc.sbuf_tensor(n, s, d))
    PL = lambda n, s, d: esL.enter_context(nc.psum_tensor(n, s, d))
    es1 = ExitStack()
    T1 = lambda n, s, d: es1.enter_context(nc.sbuf_tensor(n, s, d))
    xc = [TL("xc0", [128, NT, D], F32)]
    xntok = [TL("xntok0", [128, NT, D], BF16)]
    sq_junk = TL("sq_junk", [128, D], BF16)
    stt = TL("stt", [128, 8], F32)
    tr1 = TL("tr1", [32, CHUNK], F32)
    tr2 = TL("tr2", [32, CHUNK], F32)
    sqbuf = TL("sqbuf", [128, CHUNK], BF16)
    x1b = TL("x1b", [32, CHUNK], F32)
    x2b = TL("x2b", [32, CHUNK], F32)
    sqrope = TL("sqrope", [64, CHUNK], BF16)
    qrow = TL("qrow", [1, 512], F32)
    c0c = T1("c0c", [32, CHUNK], F32)
    s0c = T1("s0c", [32, CHUNK], F32)
    xnT_c = [T1(f"xnTc{i}", [128, 16, CHUNK], BF16) for i in range(2)]
    lkvT_c = [T1(f"lkvTc{i}", [128, 4, CHUNK], BF16) for i in range(2)]
    kc_sb = T1("kc_sb", [128, 16, CHUNK], BF16)
    vc_sb = T1("vc_sb", [128, NT, D], BF16)
    ptp = [PL(f"ptp{i}", [128, 128], BF16) for i in range(2)]
    pproj = [PL(f"pproj{i}", [128, CHUNK], F32) for i in range(2)]
    prk = PL("prk", [64, CHUNK], F32)
    pv = PL("pvv", [128, 512], F32)
    pnrm = PL("pnrm", [1, 512], F32)
    rr_tp = RR(sc, 2)
    rr_pp = RR(sc, 2)
    rr_xc = RR(sc, 1)
    rr_lkv = RR(sc, 2)
    rr_prk = RR(sc, 1)
    rr_pv = RR(sc, 1)
    rr_pn = RR(sc, 1)
    rr_kc = RR(sc, 1)
    rr_vc = RR(sc, 1)

    wdkv_sb = T1("wdkv_sb", [128, 16, 512], BF16)
    wrk_sb = T1("wrk_sb", [128, 16, 64], BF16)
    wuk_sb = T1("wuk_sb", [128, 4, D], BF16)
    wuv_sb = T1("wuv_sb", [128, 4, D], BF16)
    sc.dma("gpsimd", wdkv_sb[:, :, :], kview(w_dkv), "wld")
    sc.dma("gpsimd", wrk_sb[:, :, :], kview(w_rk), "wld")
    sc.dma("gpsimd", wuk_sb[:, :, :], kview(w_uk), "wld")
    sc.dma("gpsimd", wuv_sb[:, :, :], kview(w_uv), "wld")
    wait_start = [("cld", sc.cur("cld")), ("wld", sc.cur("wld"))]

    def l0_chunk(c, x_src, is_own, oc=None):
        bi = 0
        sc.dma("sync", xc[bi][:, :, :],
               x_src.rearrange("(n t p) d -> n p t d", t=NT, p=128)[c if not is_own else oc],
               "xld", wait=(wait_start if (c == 0 and not is_own) else [rr_xc.use(c)]))
        emit_ln(sc, xc[bi], xntok[bi], sq_junk, stt, ("xld", sc.cur("xld")), "L", epsb)
        rr_xc.done(c, "Lxn")

    for c in range(NCH):
        bi = 0
        l0_chunk(c, x_full, False)
        emit_tposes(sc, xntok[bi],
                    lambda f, tt, bi=bi: xnT_c[bi][:, f, tt * 128:(tt + 1) * 128],
                    idb, ptp, rr_tp, "L")
        xt_done = sc.cur("Lte")
        for ot in range(4):
            i = sc.cur("pm")
            for kt in range(16):
                w = [rr_pp.use(i)] if kt == 0 else []
                if kt == 0 and ot == 0:
                    w.append(("Lte", xt_done))
                sc.op("tensor", lambda e, ot=ot, kt=kt, i=i, bi=bi: e.matmul(
                    pproj[i % 2][:, :], wdkv_sb[:, kt, ot * 128:(ot + 1) * 128],
                    xnT_c[bi][:, kt, :], start=(kt == 0), stop=(kt == 15)),
                    wait=w, inc="pm" if kt == 15 else None)
            lw = [("pm", sc.cur("pm")), rr_lkv.use(c)] if ot == 0 else \
                 [("pm", sc.cur("pm"))]
            sc.op("vector", lambda e, ot=ot, i=i, bi=bi: e.tensor_scalar(
                lkvT_c[bi][:, ot, :], pproj[i % 2][:, :],
                bias_dkv[:, ot:ot + 1], None, ALU.add), wait=lw, inc="pe")
            rr_pp.done(i, "pe")
        for kt in range(16):
            w = [rr_prk.use(0)] if kt == 0 else ()
            sc.op("tensor", lambda e, kt=kt, bi=bi: e.matmul(
                prk[:, :], wrk_sb[:, kt, :], xnT_c[bi][:, kt, :],
                start=(kt == 0), stop=(kt == 15)),
                wait=w, inc="rkm" if kt == 15 else None)
        sc.op("vector", lambda e: e.tensor_scalar(
            x1b[:, :], prk[0:32, :], bias_rk[:, 0:1], None, ALU.add),
            wait=[("rkm", sc.cur("rkm")), ("rkr", sc.cur("rkr"))])
        sc.op("vector", lambda e: e.tensor_scalar(
            x2b[:, :], prk[32:64, :], bias_rk[:, 1:2], None, ALU.add),
            inc="rke")
        rr_prk.done(0, "rke")
        sc.dma("sync", c0c[:, :], cos0[:, c * CHUNK:(c + 1) * CHUNK], "cs",
               wait=[("rkr", sc.cur("rkr"))])
        sc.dma("sync", s0c[:, :], sin0[:, c * CHUNK:(c + 1) * CHUNK], "cs")
        emit_rope(sc, (tr1, tr2), x1b[:, :], x2b[:, :],
                  ropek[0:32, c * CHUNK:(c + 1) * CHUNK],
                  ropek[32:64, c * CHUNK:(c + 1) * CHUNK],
                  c0c[0:32, :], s0c[0:32, :], "rkr",
                  wait=[("rke", sc.cur("rke")), ("cs", sc.cur("cs"))])
        lkv_done = sc.cur("pe")
        for h in range(16):
            i = sc.cur("pm")
            for kt in range(4):
                w = []
                if kt == 0:
                    w = [rr_pp.use(i), ("pe", lkv_done)]
                    if h == 0:
                        w.append(rr_kc.use(0))
                sc.op("tensor", lambda e, h=h, kt=kt, i=i, bi=bi: e.matmul(
                    pproj[i % 2][:, :], wuk_sb[:, kt, h * 128:(h + 1) * 128],
                    lkvT_c[bi][:, kt, :], start=(kt == 0), stop=(kt == 3)),
                    wait=w, inc="pm" if kt == 3 else None)
            sc.op("vector", lambda e, h=h, i=i: e.tensor_scalar(
                kc_sb[:, h, :], pproj[i % 2][:, :], bias_uk[:, h:h + 1],
                None, ALU.add), wait=[("pm", sc.cur("pm"))], inc="pe")
            rr_pp.done(i, "pe")
        sc.op("scalar", lambda e, c=c: e.activation(
            sqrope[:, :], ropek[0:64, c * CHUNK:(c + 1) * CHUNK], AF.Square),
            wait=[("rkr", sc.cur("rkr")), ("nmm", sc.cur("nmm"))], inc="rsq")
        for h in range(16):
            sc.op("scalar", lambda e, h=h: e.activation(
                sqbuf[:, :], kc_sb[:, h, :], AF.Square),
                wait=[("pe", sc.cur("pe")), ("nmm", sc.cur("nmm"))], inc="nsq")
            sc.op("tensor", lambda e: e.matmul(
                pnrm[0:1, 0:CHUNK], onesc[:, :], sqbuf[:, :], start=True, stop=False),
                wait=[("nsq", sc.cur("nsq")), rr_pn.use(0),
                      ("rsq", sc.cur("rsq"))])
            sc.op("tensor", lambda e: e.matmul(
                pnrm[0:1, 0:CHUNK], onesc[0:64, :], sqrope[:, :], start=False, stop=True),
                inc="nmm")
            sc.op("vector", lambda e, h=h, c=c: e.tensor_reduce(
                kmax[0:1, c * 16 + h:c * 16 + h + 1], pnrm[0:1, 0:CHUNK],
                axis=AX.X, op=ALU.max),
                wait=[("nmm", sc.cur("nmm"))], inc="nrd")
            rr_pn.done(0, "nrd")
        sc.dma("sync", k_sp.rearrange("h p (n t) -> p h n t", t=CHUNK)[:, :, c, :],
               kc_sb[:, :, :], "ksp", wait=[("nsq", sc.cur("nsq"))])
        rr_kc.done(0, "ksp")
        for g in range(4):
            for ktk in range(NT):
                for lt in range(4):
                    w = []
                    if lt == 0:
                        w = [rr_pv.use(0)]
                        if g == 0 and ktk == 0:
                            w.append(rr_vc.use(0))
                    sc.op("tensor", lambda e, g=g, ktk=ktk, lt=lt, bi=bi: e.matmul(
                        pv[:, :], lkvT_c[bi][:, lt, ktk * 128:(ktk + 1) * 128],
                        wuv_sb[:, lt, g * 512:(g + 1) * 512],
                        start=(lt == 0), stop=False), wait=w)
                sc.op("tensor", lambda e, g=g: e.matmul(
                    pv[:, :], ones1b[:, :], bias_uv[0:1, g * 512:(g + 1) * 512],
                    start=False, stop=True), inc="vm")
                sc.op("vector", lambda e, g=g, ktk=ktk: e.tensor_copy(
                    vc_sb[:, ktk, g * 512:(g + 1) * 512], pv[:, :]),
                    wait=[("vm", sc.cur("vm"))], inc="ve")
                rr_pv.done(0, "ve")
        sc.dma("sync",
               v_sp.rearrange("p t h d -> p t (h d)")[:, 2 * c:2 * c + 2, :],
               vc_sb[:, :, :], "vsp", wait=[("ve", sc.cur("ve"))])
        rr_vc.done(0, "vsp")
        rr_lkv.done(c, "ve")
    sc.flush("p1_end")
    for h in range(16):
        for s_ in range(4):
            npre = 4 * (s_ + 1)
            sc.op("vector", lambda e, h=h, s_=s_, npre=npre: e.tensor_reduce(
                Km[0:1, h * 4 + s_: h * 4 + s_ + 1],
                kmax.ap().rearrange("p (c h) -> p h c", h=16)[0:1, h, 0:npre],
                axis=AX.X, op=ALU.max), inc="kmf")
    sc.op("scalar", lambda e: e.activation(Km[0:1, :], Km[0:1, :], AF.Sqrt),
          wait=[("kmf", sc.cur("kmf"))], inc="kms")
    sc.op("vector", lambda e: e.tensor_scalar(
        Km[0:1, :], Km[0:1, :], -1.0, None, ALU.mult),
          wait=[("kms", sc.cur("kms"))], inc="kmn")
    sc.flush("p1_km")
    es1.close()

    qx = TR("qx", [128, 16, NOWN], BF16)
    rqx = TR("rqx", [128, 16 * NOWN], BF16)
    es2 = ExitStack()
    T2 = lambda n, s, d: es2.enter_context(nc.sbuf_tensor(n, s, d))
    for oc in range(4):
        bi = 0
        l0_chunk(oc, x_own, True, oc)
        emit_tposes(sc, xntok[bi],
                    lambda f, tt, oc=oc: xnT_own[:, f, oc * CHUNK + tt * 128:
                                                 oc * CHUNK + (tt + 1) * 128],
                    idb, ptp, rr_tp, "L")
    sc.flush("p1b_ln")

    wq_sb = [T2(f"wq_sb{i}", [128, 16, 128], BF16) for i in range(2)]
    lqT = T2("lqT", [128, 12, NOWN], BF16)
    rr_wq = RR(sc, 2)
    xt_own = sc.cur("Lte")
    for ot in range(12):
        bw = ot % 2
        sc.dma("gpsimd", wq_sb[bw][:, :, :],
               kview(w_dq)[:, :, ot * 128:(ot + 1) * 128], "wq", wait=[rr_wq.use(ot)])
        for cc in range(2):
            i = sc.cur("pm2")
            for kt in range(16):
                w = []
                if kt == 0:
                    w = [("wq", sc.cur("wq")), rr_pv.use(0)]
                    if ot == 0 and cc == 0:
                        w.append(("Lte", xt_own))
                sc.op("tensor", lambda e, ot=ot, cc=cc, kt=kt, bw=bw: e.matmul(
                    pv[:, 0:512], wq_sb[bw][:, kt, :],
                    xnT_own[:, kt, cc * 512:(cc + 1) * 512],
                    start=(kt == 0), stop=(kt == 15)),
                    wait=w, inc="pm2" if kt == 15 else None)
            sc.op("vector", lambda e, ot=ot, cc=cc: e.tensor_scalar(
                lqT[:, ot, cc * 512:(cc + 1) * 512], pv[:, 0:512],
                bias_dq[:, ot:ot + 1], None, ALU.add),
                wait=[("pm2", sc.cur("pm2"))], inc="pe2")
            rr_pv.done(0, "pe2")
        rr_wq.done(ot, "pm2")
    sc.flush("p1b_lq")
    for h in range(16):
        bw = (h // 2) % 2
        if h % 2 == 0:
            sc.dma("gpsimd", wq_sb[bw][:, 0:12, :],
                   kview(w_rq)[:, :, h * 64:(h + 2) * 64], "wq",
                   wait=[rr_wq.use(h // 2)])
        for oc in range(4):
            i = sc.cur("pm2")
            for kt in range(12):
                w = [("wq", sc.cur("wq")), rr_pp.use(i)] if kt == 0 else []
                sc.op("tensor", lambda e, h=h, oc=oc, kt=kt, bw=bw, i=i: e.matmul(
                    pproj[i % 2][0:64, :],
                    wq_sb[bw][:, kt, (h % 2) * 64:(h % 2) * 64 + 64],
                    lqT[:, kt, oc * CHUNK:(oc + 1) * CHUNK],
                    start=(kt == 0), stop=(kt == 11)),
                    wait=w, inc="pm2" if kt == 11 else None)
            sc.op("vector", lambda e, h=h, i=i: e.tensor_scalar(
                x1b[:, :], pproj[i % 2][0:32, :], bias_rq[:, 2 * h:2 * h + 1],
                None, ALU.add),
                wait=[("pm2", sc.cur("pm2")), ("rqr", sc.cur("rqr"))])
            sc.op("vector", lambda e, h=h, i=i: e.tensor_scalar(
                x2b[:, :], pproj[i % 2][32:64, :], bias_rq[:, 2 * h + 1:2 * h + 2],
                None, ALU.add), inc="rqe")
            rr_pp.done(i, "rqe")
            qc = h * NOWN + oc * CHUNK
            emit_rope(sc, (tr1, tr2), x1b[:, :], x2b[:, :],
                      rqx[0:32, qc:qc + CHUNK], rqx[32:64, qc:qc + CHUNK],
                      c0q[0:32, oc * CHUNK:(oc + 1) * CHUNK],
                      s0q[0:32, oc * CHUNK:(oc + 1) * CHUNK], "rqr",
                      wait=[("rqe", sc.cur("rqe"))])
        if h % 2 == 1:
            rr_wq.done(h // 2, "pm2")
    sc.flush("p1b_rq")
    for ot in range(16):
        bw = ot % 2
        h = ot
        sc.dma("gpsimd", wq_sb[bw][:, 0:12, :],
               kview(w_uq)[:, :, ot * 128:(ot + 1) * 128], "wq", wait=[rr_wq.use(ot)])
        for cc in range(2):
            for kt in range(12):
                w = [("wq", sc.cur("wq")), rr_pv.use(0)] if kt == 0 else []
                sc.op("tensor", lambda e, ot=ot, cc=cc, kt=kt, bw=bw: e.matmul(
                    pv[:, 0:512], wq_sb[bw][:, kt, :],
                    lqT[:, kt, cc * 512:(cc + 1) * 512],
                    start=(kt == 0), stop=(kt == 11)),
                    wait=w, inc="pm2" if kt == 11 else None)
            sc.op("vector", lambda e, ot=ot, cc=cc: e.tensor_scalar(
                qx[:, ot, cc * 512:(cc + 1) * 512], pv[:, 0:512],
                bias_uq[:, ot:ot + 1], None, ALU.add),
                wait=[("pm2", sc.cur("pm2"))], inc="pe2")
            rr_pv.done(0, "pe2")
            qc = h * NOWN + cc * 512
            sc.op("scalar", lambda e, h=h, cc=cc: e.activation(
                sq_junk[:, 0:512], qx[:, h, cc * 512:(cc + 1) * 512], AF.Square),
                wait=[("pe2", sc.cur("pe2")), ("qnm", sc.cur("qnm"))], inc="qsq")
            sc.op("scalar", lambda e, qc=qc: e.activation(
                sq_junk[0:64, 512:1024], rqx[0:64, qc:qc + 512], AF.Square),
                inc="qsq")
            sc.op("tensor", lambda e: e.matmul(
                pnrm[0:1, 0:512], onesc[:, :], sq_junk[:, 0:512],
                start=True, stop=False),
                wait=[("qsq", sc.cur("qsq")), ("qrd", sc.cur("qrd"))])
            sc.op("tensor", lambda e: e.matmul(
                pnrm[0:1, 0:512], onesc[0:64, :], sq_junk[0:64, 512:1024],
                start=False, stop=True), inc="qnm")
            sc.op("scalar", lambda e: e.activation(
                qrow[0:1, :], pnrm[0:1, 0:512], AF.Sqrt),
                wait=[("qnm", sc.cur("qnm"))], inc="qrt")
            for sh in range(2):
                s_ = cc * 2 + sh
                qc2 = h * NOWN + s_ * CHUNK
                sc.op("vector", lambda e, h=h, s_=s_, qc2=qc2, sh=sh: e.tensor_scalar(
                    rqx[64:65, qc2:qc2 + CHUNK],
                    qrow[0:1, sh * CHUNK:(sh + 1) * CHUNK],
                    Km[0:1, h * 4 + s_:h * 4 + s_ + 1], COFF / SC0,
                    ALU.mult, ALU.add),
                    wait=[("qrt", sc.cur("qrt"))], inc="qrd")
        rr_wq.done(ot, "pm2")
    sc.flush("p1b_uq")
    es2.close()
    esL.close()

    es25 = ExitStack()
    attnT = es25.enter_context(nc.sbuf_tensor("attnT", [128, 16, NOWN], BF16))
    es3 = ExitStack()
    T3 = lambda n, s, d: es3.enter_context(nc.sbuf_tensor(n, s, d))
    P3s = lambda n, s, d: es3.enter_context(nc.psum_tensor(n, s, d))
    kbuf = [T3(f"kbuf{i}", [128, S], BF16) for i in range(2)]
    vbuf = [T3(f"vbuf{i}", [128, 32, 128], BF16) for i in range(2)]
    ptsb = [T3(f"ptsb{i}", [128, CHUNK], BF16) for i in range(2)]
    lsb = T3("lsb", [1, CHUNK], F32)
    rbsb = T3("rbsb", [128, CHUNK], F32)
    psc = [P3s(f"psc{i}", [128, CHUNK], F32) for i in range(2)]
    pout = P3s("pout", [128, CHUNK], F32)
    pl = P3s("pl", [1, CHUNK], F32)
    pbc = P3s("pbc", [128, CHUNK], F32)
    rr_kv = RR(sc, 2)
    kvl_at = {}

    for h in range(16):
        hb = h % 2
        sc.dma("sync", kbuf[hb][:, :], k_sp[h, :, :], "kvl", wait=[rr_kv.use(h)])
        sc.dma("sync", vbuf[hb][:, :, :], v_sp[:, :, h, :], "kvl")
        kvl_at[h] = sc.cur("kvl")

        def kf(e, bank, hh, s_, t, hb=hb, h=h):
            e.matmul(bank[:, :], kbuf[hb][:, t * 128:(t + 1) * 128],
                     qx[:, h, s_ * CHUNK:(s_ + 1) * CHUNK],
                     start=True, stop=False)
            return e.matmul(bank[:, :], ropek[0:65, t * 128:(t + 1) * 128],
                            rqx[0:65, h * NOWN + s_ * CHUNK: h * NOWN + (s_ + 1) * CHUNK],
                            start=False, stop=True)

        def qxf(e, dst, srcbank):
            return e.activation(dst[:, :], srcbank[:, :], AF.Exp, scale=SC0)

        def vf(e, bank, hh, t, pb, st_, sp_, hb=hb):
            return e.matmul(bank[:, :], vbuf[hb][:, t, :], ptsb[pb][:, :],
                            start=st_, stop=sp_)

        def outf(e, hh, s_, ob=0, h=h):
            return e.tensor_mul(attnT[:, h, s_ * CHUNK:(s_ + 1) * CHUNK],
                                pout[:, :], rbsb[:, :])

        def hook(hh, t=None, s_=None, h=h):
            if t is None:
                return None
            if s_ == 0 and t < 2:
                return ("kvl", kvl_at[h])
            return None

        emit_attention(sc, 1, PADT, kf, qxf, vf,
                       lambda s_, rt: msk[:, (s_ * 8 + rt) * CHUNK:(s_ * 8 + rt + 1) * CHUNK],
                       ptsb, psc, pout, pl,
                       pbc, lsb, rbsb, onesc, ones1f, outf, l_in_out=False,
                       head_hook=hook)
        rr_kv.done(h, "pvd")
        sc.flush(f"p2_{h}")

    es3.close()
    esR.close()
    es3b = ExitStack()
    T3b = lambda n, s, d: es3b.enter_context(nc.sbuf_tensor(n, s, d))
    P3b = lambda n, s, d: es3b.enter_context(nc.psum_tensor(n, s, d))
    wo_sb = [T3b(f"wo_sb{i}", [128, 16, 128], BF16) for i in range(2)]
    woT = T3b("woT", [128, 512], F32)
    atok = [T3b(f"atok{i}", [128, D], F32) for i in range(4)]
    ptrf = [P3b(f"ptrf{i}", [128, 128], F32) for i in range(2)]
    pw = P3b("pw", [128, 512], F32)
    rr_wo = RR(sc, 2)
    rr_pw = RR(sc, 1)
    rr_trf = RR(sc, 2)
    rr_woT = RR(sc, 1)

    def proj_transpose_add(sc, w_src, n_kt, rhs_fn, out_dram, tag, scale_sb=None,
                           bias_sb=None, add_wait=None, wdt=BF16, h_kt=16):
        for cc in range(2):
            atok_guard = ("rda", sc.cur("rda"))
            for ot in range(16):
                bw = ot % 2
                sc.dma("gpsimd", wo_sb[bw][:, 0:h_kt, :] if wdt == BF16 else None,
                       kview(w_src)[:, :, ot * 128:(ot + 1) * 128], tag + "w",
                       wait=[rr_wo.use(cc * 16 + ot)])
                i = cc * 16 + ot
                for kt in range(n_kt):
                    w = [(tag + "w", sc.cur(tag + "w")), rr_pw.use(0)] if kt == 0 else []
                    sc.op("tensor", lambda e, cc=cc, ot=ot, kt=kt, bw=bw: e.matmul(
                        pw[:, :], wo_sb[bw][:, kt, :], rhs_fn(kt, cc),
                        start=(kt == 0), stop=(kt == n_kt - 1)),
                        wait=w, inc=tag + "m" if kt == n_kt - 1 else None)
                rr_wo.done(i, tag + "m")
                if bias_sb is not None:
                    sc.op("vector", lambda e, ot=ot: e.tensor_scalar(
                        woT[:, :], pw[:, :], bias_sb[:, ot:ot + 1], None, ALU.add),
                        wait=[(tag + "m", sc.cur(tag + "m")), rr_woT.use(0)],
                        inc=tag + "e")
                else:
                    sc.op("vector", lambda e, ot=ot: e.tensor_scalar(
                        woT[:, :], pw[:, :], scale_sb[:, ot:ot + 1], None, ALU.mult),
                        wait=[(tag + "m", sc.cur(tag + "m")), rr_woT.use(0)],
                        inc=tag + "e")
                rr_pw.done(0, tag + "e")
                for tt in range(4):
                    i2 = sc.cur(tag + "tr")
                    w3 = [(tag + "e", sc.cur(tag + "e")), rr_trf.use(i2)]
                    if ot == 0 and tt == 0:
                        w3.append(atok_guard)
                    sc.op("tensor", lambda e, tt=tt, i2=i2: e.transpose(
                        ptrf[i2 % 2][:, :], woT[:, tt * 128:(tt + 1) * 128],
                        idf[:, :]), wait=w3, inc=tag + "tr")
                    sc.op("scalar", lambda e, tt=tt, ot=ot, i2=i2: e.copy(
                        atok[tt][:, ot * 128:(ot + 1) * 128], ptrf[i2 % 2][:, :]),
                        wait=[(tag + "tr", sc.cur(tag + "tr"))], inc=tag + "te")
                    rr_trf.done(i2, tag + "te")
                rr_woT.done(0, tag + "te")
            te_all = sc.cur(tag + "te")
            for tt in range(4):
                sc.dma("gpsimd",
                       out_dram[cc * 512 + tt * 128: cc * 512 + (tt + 1) * 128, :],
                       atok[tt][:, :], "rda",
                       wait=([(tag + "te", te_all)] + (add_wait or [])),
                       accum_op=ALU.add)
            sc.flush(tag + f"_{cc}")

    proj_transpose_add(sc, w_o, 16,
                       lambda kt, cc: attnT[:, kt, cc * 512:(cc + 1) * 512],
                       x1_out, "wo", bias_sb=bias_o, add_wait=[("x1c", 16)])

    es3b.close()
    es25.close()
    es4 = ExitStack()
    T4 = lambda n, s, d: es4.enter_context(nc.sbuf_tensor(n, s, d))
    P4s = lambda n, s, d: es4.enter_context(nc.psum_tensor(n, s, d))
    woT = T4("woT4", [128, 512], F32)
    atok = [T4(f"atok4_{i}", [128, D], F32) for i in range(4)]
    ptrf = [P4s(f"ptrf4_{i}", [128, 128], F32) for i in range(2)]
    pw = P4s("pw4", [128, 512], F32)
    rr_pw = RR(sc, 1)
    rr_trf = RR(sc, 2)
    rr_woT = RR(sc, 1)
    gub = [T4(f"gub{i}", [128, 16, 128], FP8) for i in range(4)]
    h_sb = T4("h_sb", [128, 48, NOWN], BF16)
    gtmp = T4("gtmp", [128, 512], BF16)
    utmp = T4("utmp", [128, 512], BF16)
    pg = P4s("pg", [128, 512], F32)
    pu = P4s("pu", [128, 512], F32)
    rr_gu = RR(sc, 2)
    rr_pg = RR(sc, 1)
    rr_pu = RR(sc, 1)
    rr_gt = RR(sc, 1)
    rr_ut = RR(sc, 1)
    for j in range(48):
        bw = j % 2
        sc.dma("gpsimd", gub[bw][:, :, :], kview(gw)[:, :, j * 128:(j + 1) * 128],
               "guw", wait=[rr_gu.use(j)], max_dma_last_dim=4096)
        sc.dma("gpsimd", gub[2 + bw][:, :, :], kview(uw)[:, :, j * 128:(j + 1) * 128],
               "guw", max_dma_last_dim=4096)
        for cc in range(2):
            for kt in range(16):
                w = [("guw", sc.cur("guw")), rr_pg.use(0)] if kt == 0 else []
                sc.op("tensor", lambda e, j=j, cc=cc, kt=kt, bw=bw: e.matmul(
                    pg[:, :], gub[bw][:, kt, :],
                    xnT_own[:, kt, cc * 512:(cc + 1) * 512],
                    start=(kt == 0), stop=(kt == 15)),
                    wait=w, inc="gm" if kt == 15 else None)
            for kt in range(16):
                w = [rr_pu.use(0)] if kt == 0 else []
                sc.op("tensor", lambda e, j=j, cc=cc, kt=kt, bw=bw: e.matmul(
                    pu[:, :], gub[2 + bw][:, kt, :],
                    xnT_own[:, kt, cc * 512:(cc + 1) * 512],
                    start=(kt == 0), stop=(kt == 15)),
                    wait=w, inc="um" if kt == 15 else None)
            sc.op("scalar", lambda e, j=j: e.activation(
                gtmp[:, :], pg[:, :], AF.Silu, scale=gsc_sb[:, j:j + 1]),
                wait=[("gm", sc.cur("gm")), rr_gt.use(0)], inc="gsl")
            rr_pg.done(0, "gsl")
            sc.op("vector", lambda e, j=j: e.tensor_scalar(
                utmp[:, :], pu[:, :], usc_sb[:, j:j + 1], None, ALU.mult),
                wait=[("um", sc.cur("um")), rr_ut.use(0)], inc="uev")
            rr_pu.done(0, "uev")
            sc.op("vector", lambda e, j=j, cc=cc: e.tensor_mul(
                h_sb[:, j, cc * 512:(cc + 1) * 512], gtmp[:, :], utmp[:, :]),
                wait=[("gsl", sc.cur("gsl"))], inc="hc")
            rr_gt.done(0, "hc")
            rr_ut.done(0, "hc")
        rr_gu.done(j, "um")
        if j % 8 == 7:
            sc.flush(f"p4g_{j}")
    sc.flush("p4_gu")
    dwb = [T4(f"dwb{i}", [128, 48, 128], FP8) for i in range(2)]
    rr_dw = RR(sc, 2)
    for cc in range(2):
        atok_guard = ("rda", sc.cur("rda"))
        for ot in range(16):
            bw = ot % 2
            i = cc * 16 + ot
            sc.dma("gpsimd", dwb[bw][:, :, :], kview(dw)[:, :, ot * 128:(ot + 1) * 128],
                   "dww", wait=[rr_dw.use(i)], max_dma_last_dim=4096)
            for kt in range(48):
                w = [("dww", sc.cur("dww")), rr_pw.use(0)] if kt == 0 else []
                sc.op("tensor", lambda e, cc=cc, ot=ot, kt=kt, bw=bw: e.matmul(
                    pw[:, :], dwb[bw][:, kt, :],
                    h_sb[:, kt, cc * 512:(cc + 1) * 512],
                    start=(kt == 0), stop=(kt == 47)),
                    wait=w, inc="dm" if kt == 47 else None)
            rr_dw.done(i, "dm")
            sc.op("vector", lambda e, ot=ot: e.tensor_scalar(
                woT[:, :], pw[:, :], dsc_sb[:, ot:ot + 1], None, ALU.mult),
                wait=[("dm", sc.cur("dm")), rr_woT.use(0)], inc="de")
            rr_pw.done(0, "de")
            for tt in range(4):
                i2 = sc.cur("ftr")
                w3 = [("de", sc.cur("de")), rr_trf.use(i2)]
                if ot == 0 and tt == 0:
                    w3.append(atok_guard)
                sc.op("tensor", lambda e, tt=tt, i2=i2: e.transpose(
                    ptrf[i2 % 2][:, :], woT[:, tt * 128:(tt + 1) * 128],
                    idf[:, :]), wait=w3, inc="ftr")
                sc.op("scalar", lambda e, tt=tt, ot=ot, i2=i2: e.copy(
                    atok[tt][:, ot * 128:(ot + 1) * 128], ptrf[i2 % 2][:, :]),
                    wait=[("ftr", sc.cur("ftr"))], inc="fte")
                rr_trf.done(i2, "fte")
            rr_woT.done(0, "fte")
        te_all = sc.cur("fte")
        for tt in range(4):
            sc.dma("gpsimd",
                   x1_out[cc * 512 + tt * 128: cc * 512 + (tt + 1) * 128, :],
                   atok[tt][:, :], "rda", wait=[("fte", te_all)],
                   accum_op=ALU.add)
        sc.flush(f"p4d_{cc}")

    sc.op("gpsimd", lambda e: e.engine_nop(),
          wait=[("rda", sc.cur("rda")), ("x1c", 16),
                ("ksp", sc.cur("ksp")), ("vsp", sc.cur("vsp"))])
    sc.flush("fin")
    es4.close()
    es.close()
    return nc


def build_l1(dbg=False):
    nc = bass.Bass("TRN2", target_bir_lowering=False)

    def tap(sc, name, sb_ap, shape, dtype):
        if not dbg:
            return
        o = nc.dram_tensor("dbg_" + name, shape, dtype, kind="ExternalOutput").ap()
        sc.dma("gpsimd", o, sb_ap, "dbgs")
        sc.op("gpsimd", lambda e: e.engine_nop(), wait=[("dbgs", sc.cur("dbgs"))])
        sc.flush("tap_" + name)
    EI, EO = "ExternalInput", "ExternalOutput"
    DT = lambda n, s, d, k="Internal": nc.dram_tensor(n, s, d, kind=k).ap()
    x1_full = DT("x1_full", [S, D], F32, EI)
    x1_own = DT("x1_own", [NOWN, D], F32, EI)
    ident = DT("ident", [128, 128], BF16, EI)
    cmask = DT("cmask", [4, 8, 128, CHUNK], BF16, EI)
    cos1 = DT("cos1", [32, S], F32, EI)
    sin1 = DT("sin1", [32, S], F32, EI)
    cos1q = DT("cos1q", [32, NOWN], F32, EI)
    sin1q = DT("sin1q", [32, NOWN], F32, EI)
    w_qkvT = DT("w_qkvT", [D, 2560], F32, EI)
    w_oT = DT("w_oT", [D, D], F32, EI)
    gate_wT = DT("gate_wT", [D, 8], F32, EI)
    w_gate = DT("w_gate", [8, D, 1408], F32, EI)
    w_up = DT("w_up", [8, D, 1408], F32, EI)
    w_down = DT("w_down", [8, 1408, D], F32, EI)
    sw_gate = DT("sw_gate", [D, 2816], F32, EI)
    sw_up = DT("sw_up", [D, 2816], F32, EI)
    sw_down = DT("sw_down", [2816, D], F32, EI)
    out_own = DT("out_own", [NOWN, D], F32, EO)

    es = ExitStack()
    esR = ExitStack()
    T = lambda n, s, d: es.enter_context(nc.sbuf_tensor(n, s, d))
    TR = lambda n, s, d: esR.enter_context(nc.sbuf_tensor(n, s, d, side="right"))
    sc = Sched(nc)

    idb = T("idb", [128, 128], BF16)
    idf = T("idf", [128, 128], F32)
    epsb = T("epsb", [128, 1], F32)
    kmax = T("kmax", [1, NCH * 4], F32)
    Km = T("Km", [1, 4 * 4], F32)
    onesc = T("onesc", [128, 1], BF16)
    ones1f = T("ones1f", [1, 128], F32)
    xn1T_own = T("xn1T_own", [128, 16, NOWN], BF16)
    c_row = T("c_row", [1, 8 * NOWN], BF16)
    ones1b = T("ones1b", [1, 128], BF16)
    msk = TR("msk", [128, 4 * 8 * CHUNK], BF16)
    k1T = TR("k1T", [128, 4, S], BF16)
    v1 = TR("v1", [128, 32, 4 * 65], BF16)

    sc.dma("sync", idb[:, :], ident, "cld")
    sc.dma("gpsimd", idf[:, :], ident, "cld")
    sc.dma("sync", msk.ap().rearrange("p (a q) -> p a q", q=CHUNK),
           cmask.rearrange("s r p q -> p (s r) q"), "cld")
    sc.dma("sync", out_own[:, :], x1_own, "x2c")
    sc.op("vector", lambda e: e.memset(onesc[:, :], 1.0))
    sc.op("vector", lambda e: e.memset(epsb[:, :], EPS))
    sc.op("vector", lambda e: e.memset(ones1f[:, :], 1.0))
    sc.op("vector", lambda e: e.memset(ones1b[:, :], 1.0))
    sc.op("vector", lambda e: e.memset(k1T[64:65, :, :], 1.0))
    sc.op("vector", lambda e: e.memset(v1[:, :, :], 0.0))
    for g in range(4):
        sc.op("vector", lambda e, g=g: e.memset(
            v1[:, :, g * 65 + 64:g * 65 + 65], 1.0), inc="vset")
    sc.flush("q0")

    NT = 2
    esL = ExitStack()
    TL = lambda n, s, d: esL.enter_context(nc.sbuf_tensor(n, s, d))
    PL = lambda n, s, d: esL.enter_context(nc.psum_tensor(n, s, d))
    es1 = ExitStack()
    T1 = lambda n, s, d: es1.enter_context(nc.sbuf_tensor(n, s, d))
    P1s = lambda n, s, d: es1.enter_context(nc.psum_tensor(n, s, d))
    sq_junk = TL("sq_junk", [128, D], BF16)
    tr1 = TL("tr1", [32, CHUNK], F32)
    tr2 = TL("tr2", [32, CHUNK], F32)
    sqbuf = TL("sqbuf", [32, 2 * CHUNK], BF16)
    x1b = TL("x1b", [32, CHUNK], F32)
    x2b = TL("x2b", [32, CHUNK], F32)
    qrow = TL("qrow", [1, 512], F32)
    pproj = [PL(f"pproj{i}", [128, CHUNK], F32) for i in range(2)]
    pnrm = PL("pnrm", [1, 512], F32)
    xc = [T1("xc0", [128, NT, D], F32)]
    xntok = [T1("xntok0", [128, NT, D], BF16)]
    stt = T1("stt", [128, 8], F32)
    c1c = T1("c1c", [32, CHUNK], F32)
    s1c = T1("s1c", [32, CHUNK], F32)
    wkv_sb = T1("wkv_sb", [128, 16, 512], BF16)
    xnT_c = [T1(f"xnTc{i}", [128, 16, CHUNK], BF16) for i in range(2)]
    ptp = [P1s(f"ptp{i}", [128, 128], BF16) for i in range(2)]
    pv = P1s("pvv", [128, 512], F32)
    rr_tp = RR(sc, 2)
    rr_pp = RR(sc, 2)
    rr_xc = RR(sc, 1)
    rr_pv = RR(sc, 1)
    rr_pn = RR(sc, 1)
    rr_sqb = RR(sc, 1)
    sc.dma("gpsimd", wkv_sb[:, :, :], kview(w_qkvT)[:, :, 2048:2560], "wld")

    wait_start = [("cld", sc.cur("cld")), ("wld", sc.cur("wld")),
                  ("vset", sc.cur("vset"))]
    for c in range(NCH):
        bi = 0
        sc.dma("sync", xc[bi][:, :, :],
               x1_full.rearrange("(n t p) d -> n p t d", t=NT, p=128)[c], "xld",
               wait=(wait_start if c == 0 else [rr_xc.use(c)]))
        emit_ln(sc, xc[bi], xntok[bi], sq_junk, stt, ("xld", sc.cur("xld")), "L", epsb)
        rr_xc.done(c, "Lxn")
        emit_tposes(sc, xntok[bi],
                    lambda f, tt, bi=bi: xnT_c[bi][:, f, tt * 128:(tt + 1) * 128],
                    idb, ptp, rr_tp, "L")
        xt_done = sc.cur("Lte")
        sc.dma("sync", c1c[:, :], cos1[:, c * CHUNK:(c + 1) * CHUNK], "cs",
               wait=[("krp", sc.cur("krp"))])
        sc.dma("sync", s1c[:, :], sin1[:, c * CHUNK:(c + 1) * CHUNK], "cs")
        for g in range(4):
            i = sc.cur("km")
            for kt in range(16):
                w = [rr_pp.use(i)] if kt == 0 else []
                if kt == 0 and g == 0:
                    w.append(("Lte", xt_done))
                sc.op("tensor", lambda e, g=g, kt=kt, i=i, bi=bi: e.matmul(
                    pproj[i % 2][0:64, :], wkv_sb[:, kt, g * 64:(g + 1) * 64],
                    xnT_c[bi][:, kt, :], start=(kt == 0), stop=(kt == 15)),
                    wait=w, inc="km" if kt == 15 else None)
            sc.op("vector", lambda e, i=i: e.tensor_copy(x1b[:, :], pproj[i % 2][0:32, :]),
                  wait=[("km", sc.cur("km")), ("krp", sc.cur("krp"))])
            sc.op("vector", lambda e, i=i: e.tensor_copy(x2b[:, :], pproj[i % 2][32:64, :]),
                  inc="kre")
            rr_pp.done(i, "kre")
            sc.op("scalar", lambda e: e.activation(
                sqbuf[:, 0:CHUNK], x1b[:, :], AF.Square),
                wait=[("kre", sc.cur("kre")), rr_sqb.use(0)], inc="ksq")
            sc.op("scalar", lambda e: e.activation(
                sqbuf[:, CHUNK:2 * CHUNK], x2b[:, :], AF.Square), inc="ksq")
            sc.op("tensor", lambda e: e.matmul(
                pnrm[0:1, 0:CHUNK], onesc[0:32, :], sqbuf[:, 0:CHUNK],
                start=True, stop=False),
                wait=[("ksq", sc.cur("ksq")), rr_pn.use(0)])
            sc.op("tensor", lambda e: e.matmul(
                pnrm[0:1, 0:CHUNK], onesc[0:32, :], sqbuf[:, CHUNK:2 * CHUNK],
                start=False, stop=True), inc="nm")
            rr_sqb.done(0, "nm")
            sc.op("vector", lambda e, g=g, c=c: e.tensor_reduce(
                kmax[0:1, c * 4 + g:c * 4 + g + 1], pnrm[0:1, 0:CHUNK],
                axis=AX.X, op=ALU.max), wait=[("nm", sc.cur("nm"))], inc="nrd")
            rr_pn.done(0, "nrd")
            emit_rope(sc, (tr1, tr2), x1b[:, :], x2b[:, :],
                      k1T[0:32, g, c * CHUNK:(c + 1) * CHUNK],
                      k1T[32:64, g, c * CHUNK:(c + 1) * CHUNK],
                      c1c[0:32, :], s1c[0:32, :], "krp",
                      wait=[("kre", sc.cur("kre")), ("cs", sc.cur("cs"))])
        for ktk in range(NT):
            for kt in range(16):
                w = [rr_pv.use(0)] if kt == 0 else []
                sc.op("tensor", lambda e, ktk=ktk, kt=kt, bi=bi: e.matmul(
                    pv[:, 0:256], xnT_c[bi][:, kt, ktk * 128:(ktk + 1) * 128],
                    wkv_sb[:, kt, 256:512], start=(kt == 0), stop=(kt == 15)),
                    wait=w, inc="vm" if kt == 15 else None)
            sc.op("vector", lambda e, ktk=ktk, c=c: e.tensor_copy(
                v1.ap().rearrange("p t (g d) -> p t g d", d=65)[:, 2 * c + ktk, :, 0:64],
                pv.ap()[:, 0:256].rearrange("p (g d) -> p g d", d=64)),
                wait=[("vm", sc.cur("vm"))], inc="ve")
            rr_pv.done(0, "ve")
    sc.flush("q1_end")
    for g in range(4):
        for s_ in range(4):
            npre = 4 * (s_ + 1)
            sc.op("vector", lambda e, g=g, s_=s_, npre=npre: e.tensor_reduce(
                Km[0:1, g * 4 + s_: g * 4 + s_ + 1],
                kmax.ap().rearrange("p (c h) -> p h c", h=4)[0:1, g, 0:npre],
                axis=AX.X, op=ALU.max), inc="kmf")
    sc.op("scalar", lambda e: e.activation(Km[0:1, :], Km[0:1, :], AF.Sqrt),
          wait=[("kmf", sc.cur("kmf"))], inc="kms")
    sc.op("vector", lambda e: e.tensor_scalar(
        Km[0:1, :], Km[0:1, :], -1.0, None, ALU.mult),
          wait=[("kms", sc.cur("kms"))], inc="kmn")
    sc.flush("q1_km")
    tap(sc, "k1T", k1T[:, :, :], [128, 4, S], BF16)
    tap(sc, "v1", v1[:, :, :], [128, 32, 4 * 65], BF16)
    tap(sc, "kmax", kmax[0:1, :], [1, NCH * 4], F32)
    tap(sc, "Km", Km[0:1, :], [1, 16], F32)

    es2 = ExitStack()
    T2 = lambda n, s, d: es2.enter_context(nc.sbuf_tensor(n, s, d))
    P2s = lambda n, s, d: es2.enter_context(nc.psum_tensor(n, s, d))
    gwT_sb = T2("gwT_sb", [128, 16, 8], BF16)
    glog = T2("glog", [128, 8], F32)
    gst = T2("gst", [128, 4], F32)
    comb = T2("comb", [128, 8], F32)
    ptiny = P2s("ptiny", [1, 128], F32)
    pgl = P2s("pgl", [128, 8], F32)
    rr_pgl = RR(sc, 1)
    rr_pti = RR(sc, 1)
    sc.dma("gpsimd", gwT_sb[:, :, :], kview(gate_wT), "wld")
    wait_q1b = [("wld", sc.cur("wld"))]
    for oc in range(4):
        bi = 0
        sc.dma("sync", xc[bi][:, :, :],
               x1_own.rearrange("(n t p) d -> n p t d", t=NT, p=128)[oc], "xld",
               wait=[rr_xc.use(oc)] + (wait_q1b if oc == 0 else []))
        emit_ln(sc, xc[bi], xntok[bi], sq_junk, stt, ("xld", sc.cur("xld")), "L", epsb)
        rr_xc.done(oc, "Lxn")
        emit_tposes(sc, xntok[bi],
                    lambda f, tt, oc=oc: xn1T_own[:, f, oc * CHUNK + tt * 128:
                                                  oc * CHUNK + (tt + 1) * 128],
                    idb, ptp, rr_tp, "L")
    sc.flush("q1b_ln")
    xt_own = sc.cur("Lte")
    for tt in range(8):
        for kt in range(16):
            w = [rr_pgl.use(0)] if kt == 0 else []
            if kt == 0 and tt == 0:
                w.append(("Lte", xt_own))
            sc.op("tensor", lambda e, tt=tt, kt=kt: e.matmul(
                pgl[:, :], xn1T_own[:, kt, tt * 128:(tt + 1) * 128],
                gwT_sb[:, kt, :], start=(kt == 0), stop=(kt == 15)),
                wait=w, inc="glm" if kt == 15 else None)
        def gch(fn, extra=(), name=None):
            w = [("gch", sc.cur("gch"))] + list(extra)
            sc.op("vector", fn, wait=w, inc=(name or "gch"))
        gch(lambda e: e.reduce_max(
            gst[:, 1:2], pgl[:, :], axis=AX.X, negate=True),
            extra=[("glm", sc.cur("glm"))], name="gmx")
        sc.op("scalar", lambda e: e.activation(
            glog[:, :], pgl[:, :], AF.Exp, bias=gst[:, 1:2]),
            wait=[("gmx", sc.cur("gmx"))], inc="gex")
        rr_pgl.done(0, "gex")
        gch(lambda e: e.reduce_sum(gst[:, 2:3], glog[:, :], axis=AX.X),
            extra=[("gex", sc.cur("gex")), ("gmx", sc.cur("gmx"))])
        gch(lambda e: e.reciprocal(gst[:, 3:4], gst[:, 2:3]))
        gch(lambda e: e.tensor_scalar(
            glog[:, :], glog[:, :], gst[:, 3:4], None, ALU.mult))
        gch(lambda e: e.reduce_max(gst[:, 0:1], glog[:, :], axis=AX.X))
        gch(lambda e: e.tensor_scalar(
            comb[:, :], glog[:, :], gst[:, 0:1], None, ALU.is_ge))
        gch(lambda e: e.tensor_scalar(
            comb[:, :], comb[:, :], -2.0, None, ALU.mult))
        gch(lambda e: e.tensor_add(comb[:, :], comb[:, :], glog[:, :]))
        gch(lambda e: e.reduce_max(gst[:, 1:2], comb[:, :], axis=AX.X))
        gch(lambda e: e.tensor_scalar(
            comb[:, :], glog[:, :], gst[:, 1:2], None, ALU.is_ge))
        gch(lambda e: e.tensor_mul(comb[:, :], comb[:, :], glog[:, :]),
            name="gcb")
        sc.op("vector", lambda e: e.engine_nop(),
              wait=[("gcb", sc.cur("gcb"))], inc="gch")
        for e_ in range(8):
            sc.op("tensor", lambda e, e_=e_: e.transpose(
                ptiny[0:1, :], comb[:, e_:e_ + 1], idf[:, :]),
                wait=[("gcb", sc.cur("gcb")), rr_pti.use(0)], inc="gtr")
            sc.op("vector", lambda e, e_=e_, tt=tt: e.tensor_copy(
                c_row[0:1, e_ * NOWN + tt * 128: e_ * NOWN + (tt + 1) * 128],
                ptiny[0:1, :]), wait=[("gtr", sc.cur("gtr"))], inc="gte")
            rr_pti.done(0, "gte")
    sc.flush("q1c")
    tap(sc, "xn1T", xn1T_own[:, :, :], [128, 16, NOWN], BF16)
    tap(sc, "c_row", c_row[0:1, :], [1, 8 * NOWN], BF16)
    es2.close()
    es1.close()

    esR2 = ExitStack()
    attnT = esR2.enter_context(
        nc.sbuf_tensor("attnT", [128, 16, NOWN], BF16, side="right"))
    esR3 = ExitStack()
    q1x = esR3.enter_context(
        nc.sbuf_tensor("q1x", [128, 16 * NOWN], BF16, side="right"))
    es2b = ExitStack()
    T2b = lambda n, s, d: es2b.enter_context(nc.sbuf_tensor(n, s, d))
    P2b = lambda n, s, d: es2b.enter_context(nc.psum_tensor(n, s, d))
    c1q = T2b("c1q", [32, NOWN], F32)
    s1q = T2b("s1q", [32, NOWN], F32)
    wq_sb = [T2b("wq_sb0", [128, 16, 128], BF16)]
    ptsb = [T2b(f"ptsb{i}", [128, CHUNK], BF16) for i in range(2)]
    lsb = T2b("lsb", [1, CHUNK], F32)
    rbsb = T2b("rbsb", [128, CHUNK], F32)
    psc = [P2b(f"psc{i}", [128, CHUNK], F32) for i in range(2)]
    pout = P2b("pout", [65, CHUNK], F32)
    pbc = P2b("pbc", [128, CHUNK], F32)
    rr_wq = RR(sc, 1)
    sc.dma("sync", c1q[:, :], cos1q, "cld")
    sc.dma("sync", s1q[:, :], sin1q, "cld")

    def q2_qproj(gh):
        g = gh // 8
        hl = gh % 16
        bw = 0
        if gh % 2 == 0:
            sc.dma("gpsimd", wq_sb[bw][:, :, :],
                   kview(w_qkvT)[:, :, gh * 64:(gh + 2) * 64], "qw",
                   wait=[rr_wq.use(0)])
        for oc in range(4):
            i = sc.cur("qm")
            for kt in range(16):
                w = [("qw", sc.cur("qw")), rr_pp.use(i)] if kt == 0 else []
                sc.op("tensor", lambda e, gh=gh, oc=oc, kt=kt, bw=bw, i=i: e.matmul(
                    pproj[i % 2][0:64, :],
                    wq_sb[bw][:, kt, (gh % 2) * 64:(gh % 2) * 64 + 64],
                    xn1T_own[:, kt, oc * CHUNK:(oc + 1) * CHUNK],
                    start=(kt == 0), stop=(kt == 15)),
                    wait=w, inc="qm" if kt == 15 else None)
            sc.op("vector", lambda e, i=i: e.tensor_copy(x1b[:, :], pproj[i % 2][0:32, :]),
                  wait=[("qm", sc.cur("qm")), ("qrp", sc.cur("qrp"))])
            sc.op("vector", lambda e, i=i: e.tensor_copy(x2b[:, :], pproj[i % 2][32:64, :]),
                  inc="qre")
            rr_pp.done(i, "qre")
            qc = hl * NOWN + oc * CHUNK
            sc.op("scalar", lambda e: e.activation(
                sqbuf[:, 0:CHUNK], x1b[:, :], AF.Square),
                wait=[("qre", sc.cur("qre")), rr_sqb.use(0)], inc="qsq")
            sc.op("scalar", lambda e: e.activation(
                sqbuf[:, CHUNK:2 * CHUNK], x2b[:, :], AF.Square), inc="qsq")
            sc.op("tensor", lambda e: e.matmul(
                pnrm[0:1, 0:CHUNK], onesc[0:32, :], sqbuf[:, 0:CHUNK],
                start=True, stop=False),
                wait=[("qsq", sc.cur("qsq")), rr_pn.use(0)])
            sc.op("tensor", lambda e: e.matmul(
                pnrm[0:1, 0:CHUNK], onesc[0:32, :], sqbuf[:, CHUNK:2 * CHUNK],
                start=False, stop=True), inc="qnm")
            rr_sqb.done(0, "qnm")
            sc.op("scalar", lambda e: e.activation(
                qrow[0:1, 0:CHUNK], pnrm[0:1, 0:CHUNK], AF.Sqrt),
                wait=[("qnm", sc.cur("qnm"))], inc="qrt")
            rr_pn.done(0, "qrt")
            s_ = oc
            sc.op("vector", lambda e, g=g, s_=s_, qc=qc: e.tensor_scalar(
                q1x[64:65, qc:qc + CHUNK], qrow[0:1, 0:CHUNK],
                Km[0:1, g * 4 + s_:g * 4 + s_ + 1], COFF / SC1,
                ALU.mult, ALU.add),
                wait=[("qrt", sc.cur("qrt")), ("kmn", sc.cur("kmn"))],
                inc="brow")
            emit_rope(sc, (tr1, tr2), x1b[:, :], x2b[:, :],
                      q1x[0:32, qc:qc + CHUNK], q1x[32:64, qc:qc + CHUNK],
                      c1q[0:32, oc * CHUNK:(oc + 1) * CHUNK],
                      s1q[0:32, oc * CHUNK:(oc + 1) * CHUNK], "qrp",
                      wait=[("qre", sc.cur("qre"))])
        if gh % 2 == 1:
            rr_wq.done(0, "qm")

    def qxf1(e, dst, srcbank):
        return e.activation(dst[:, :], srcbank[:, :], AF.Exp, scale=SC1)

    rr_scQ, rr_ptQ, rr_slotQ = RR(sc, 2), RR(sc, 2), RR(sc, 2)

    def q2_att(gh):
        g = gh // 8
        hl = gh % 16
        hh = gh % 2

        def kf1(e, bank, _h, s_, t):
            return e.matmul(bank[:, :], k1T[0:65, g, t * 128:(t + 1) * 128],
                            q1x[0:65, hl * NOWN + s_ * CHUNK:
                                hl * NOWN + (s_ + 1) * CHUNK],
                            start=True, stop=True)

        def vf1(e, bank, _h, t, pb, st_, sp_):
            return e.matmul(bank[:, :], v1[:, t, g * 65:(g + 1) * 65],
                            ptsb[pb][:, :], start=st_, stop=sp_)

        def outf1(e, _h, s_, ob):
            return e.tensor_mul(
                attnT[64 * hh:64 * hh + 64, gh // 2, s_ * CHUNK:(s_ + 1) * CHUNK],
                pout[ob][0:64, :], rbsb[0:64, :])

        emit_attention(sc, 1, PADT, kf1, qxf1, vf1,
                       lambda s_, rt: msk[:, (s_ * 8 + rt) * CHUNK:
                                          (s_ * 8 + rt + 1) * CHUNK],
                       ptsb, psc, pout, None,
                       pbc, lsb, rbsb, onesc, ones1f, outf1, l_in_out=True,
                       rr_sc=rr_scQ, rr_pt=rr_ptQ, rr_slot=rr_slotQ)

    for gh in range(32):
        q2_qproj(gh)
        if gh >= 1:
            q2_att(gh - 1)
    q2_att(31)
    sc.flush("q2")
    es2b.close()
    esR3.close()
    esL.close()

    es3 = ExitStack()
    T3 = lambda n, s, d: es3.enter_context(nc.sbuf_tensor(n, s, d))
    P3s = lambda n, s, d: es3.enter_context(nc.psum_tensor(n, s, d))
    wo_sb = [T3(f"wo_sb{i}", [128, 16, 128], BF16) for i in range(2)]
    woT = T3("woT", [128, 512], F32)
    atok = [T3(f"atok{i}", [128, D], F32) for i in range(4)]
    ptrf = [P3s(f"ptrf{i}", [128, 128], F32) for i in range(2)]
    pw3 = P3s("pw3", [128, 512], F32)
    rr_wo = RR(sc, 2)
    rr_pw = RR(sc, 1)
    rr_trf = RR(sc, 2)
    rr_woT = RR(sc, 1)
    for cc in range(2):
        atok_guard = ("rda", sc.cur("rda"))
        for ot in range(16):
            bw = ot % 2
            i = cc * 16 + ot
            sc.dma("gpsimd", wo_sb[bw][:, :, :],
                   kview(w_oT)[:, :, ot * 128:(ot + 1) * 128], "wow",
                   wait=[rr_wo.use(i)])
            for kt in range(16):
                w = [("wow", sc.cur("wow")), rr_pw.use(0)] if kt == 0 else []
                sc.op("tensor", lambda e, cc=cc, ot=ot, kt=kt, bw=bw: e.matmul(
                    pw3[:, :], wo_sb[bw][:, kt, :],
                    attnT[:, kt, cc * 512:(cc + 1) * 512],
                    start=(kt == 0), stop=(kt == 15)),
                    wait=w, inc="wm" if kt == 15 else None)
            rr_wo.done(i, "wm")
            sc.op("vector", lambda e: e.tensor_copy(woT[:, :], pw3[:, :]),
                  wait=[("wm", sc.cur("wm")), rr_woT.use(0)], inc="we")
            rr_pw.done(0, "we")
            for tt in range(4):
                i2 = sc.cur("wtr")
                w3 = [("we", sc.cur("we")), rr_trf.use(i2)]
                if ot == 0 and tt == 0:
                    w3.append(atok_guard)
                sc.op("tensor", lambda e, tt=tt, i2=i2: e.transpose(
                    ptrf[i2 % 2][:, :], woT[:, tt * 128:(tt + 1) * 128],
                    idf[:, :]), wait=w3, inc="wtr")
                sc.op("scalar", lambda e, tt=tt, ot=ot, i2=i2: e.copy(
                    atok[tt][:, ot * 128:(ot + 1) * 128], ptrf[i2 % 2][:, :]),
                    wait=[("wtr", sc.cur("wtr"))], inc="wte")
                rr_trf.done(i2, "wte")
            rr_woT.done(0, "wte")
        te_all = sc.cur("wte")
        for tt in range(4):
            sc.dma("gpsimd",
                   out_own[cc * 512 + tt * 128: cc * 512 + (tt + 1) * 128, :],
                   atok[tt][:, :], "rda",
                   wait=[("wte", te_all), ("x2c", 16)], accum_op=ALU.add)
        sc.flush(f"q3_{cc}")
    es3.close()
    esR2.close()
    esR.close()

    es4 = ExitStack()
    T4 = lambda n, s, d: es4.enter_context(nc.sbuf_tensor(n, s, d))
    P4s = lambda n, s, d: es4.enter_context(nc.psum_tensor(n, s, d))
    ySB = T4("ySB", [128, 16, NOWN], F32)
    es5 = ExitStack()
    T5 = lambda n, s, d: es5.enter_context(nc.sbuf_tensor(n, s, d))
    P5s = lambda n, s, d: es5.enter_context(nc.psum_tensor(n, s, d))
    hbig = T5("hbig", [128, 22, NOWN], BF16)
    web = [T5(f"web{i}", [128, 16, 128], BF16) for i in range(4)]
    cb = T5("cb", [128, NOWN], F32)
    gtmp = T5("gtmp", [128, 512], BF16)
    pg = P5s("pg", [128, 512], F32)
    pu = P5s("pu", [128, 512], F32)
    pcb = P5s("pcb", [128, 512], F32)
    pw4 = P5s("pw4", [128, 512], F32)
    h_e = [hbig[:, 0:11, :], hbig[:, 11:22, :]]
    rr_we = RR(sc, 2)
    rr_pg = RR(sc, 1)
    rr_pu = RR(sc, 1)
    rr_gt = RR(sc, 1)
    rr_pcb = RR(sc, 1)
    rr_pw = RR(sc, 1)
    rr_dwe = RR(sc, 2)
    gucnt = [0]
    for pair in range(4):
        for ei in range(2):
            e_ = pair * 2 + ei
            for hf in range(2):
                sc.op("tensor", lambda e, e_=e_, hf=hf: e.matmul(
                    pcb[:, :], ones1b[:, :],
                    c_row[0:1, e_ * NOWN + hf * 512: e_ * NOWN + (hf + 1) * 512],
                    start=True, stop=True),
                    wait=[rr_pcb.use(0), ("uevm", sc.cur("uevm"))], inc="cbm")
                sc.op("vector", lambda e, hf=hf: e.tensor_copy(
                    cb[:, hf * 512:(hf + 1) * 512], pcb[:, :]),
                    wait=[("cbm", sc.cur("cbm"))], inc="cbe")
                rr_pcb.done(0, "cbe")
            for j in range(11):
                bw = gucnt[0] % 2
                sc.dma("gpsimd", web[bw][:, :, :],
                       kview(w_gate[e_])[:, :, j * 128:(j + 1) * 128], "guw",
                       wait=[rr_we.use(gucnt[0])])
                sc.dma("gpsimd", web[2 + bw][:, :, :],
                       kview(w_up[e_])[:, :, j * 128:(j + 1) * 128], "guw")
                for cc in range(2):
                    for kt in range(16):
                        w = [("guw", sc.cur("guw")), rr_pg.use(0)] if kt == 0 else []
                        sc.op("tensor", lambda e, kt=kt, cc=cc, bw=bw: e.matmul(
                            pg[:, :], web[bw][:, kt, :],
                            xn1T_own[:, kt, cc * 512:(cc + 1) * 512],
                            start=(kt == 0), stop=(kt == 15)),
                            wait=w, inc="gm" if kt == 15 else None)
                    for kt in range(16):
                        w = [rr_pu.use(0)] if kt == 0 else []
                        sc.op("tensor", lambda e, kt=kt, cc=cc, bw=bw: e.matmul(
                            pu[:, :], web[2 + bw][:, kt, :],
                            xn1T_own[:, kt, cc * 512:(cc + 1) * 512],
                            start=(kt == 0), stop=(kt == 15)),
                            wait=w, inc="um" if kt == 15 else None)
                    sc.op("scalar", lambda e: e.activation(
                        gtmp[:, :], pg[:, :], AF.Silu),
                        wait=[("gm", sc.cur("gm")), rr_gt.use(0),
                              ("cbe", sc.cur("cbe"))], inc="gsl")
                    rr_pg.done(0, "gsl")
                    sc.op("vector", lambda e, cc=cc, j=j, ei=ei: e.tensor_tensor(
                        h_e[ei][:, j, cc * 512:(cc + 1) * 512], pu[:, :],
                        cb[:, cc * 512:(cc + 1) * 512], ALU.mult),
                        wait=[("um", sc.cur("um"))], inc="uevm")
                    rr_pu.done(0, "uevm")
                    sc.op("vector", lambda e, cc=cc, j=j, ei=ei: e.tensor_mul(
                        h_e[ei][:, j, cc * 512:(cc + 1) * 512], gtmp[:, :],
                        h_e[ei][:, j, cc * 512:(cc + 1) * 512]),
                        wait=[("gsl", sc.cur("gsl"))], inc="hc")
                    rr_gt.done(0, "hc")
                rr_we.done(gucnt[0], "um")
                gucnt[0] += 1
        sc.flush(f"q4g_{pair}")
        if pair == 0:
            tap(sc, "hbig0", hbig[:, :, :], [128, 22, NOWN], BF16)
            tap(sc, "cb0", cb[:, :], [128, NOWN], F32)
        for ot in range(16):
            for ei in range(2):
                e_ = pair * 2 + ei
                sc.dma("gpsimd", web[ei][:, 0:11, :],
                       kview(w_down[e_])[:, :, ot * 128:(ot + 1) * 128], "dww",
                       wait=[rr_dwe.use(ot * 2 + ei)])
            for cc in range(2):
                for ei in range(2):
                    for kt in range(11):
                        w = []
                        if ei == 0 and kt == 0:
                            w = [("dww", sc.cur("dww")), rr_pw.use(0)]
                        sc.op("tensor", lambda e, kt=kt, cc=cc, ei=ei: e.matmul(
                            pw4[:, :], web[ei][:, kt, :],
                            h_e[ei][:, kt, cc * 512:(cc + 1) * 512],
                            start=(kt == 0 and ei == 0),
                            stop=(kt == 10 and ei == 1)),
                            wait=w, inc="dm" if (kt == 10 and ei == 1) else None)
                if pair == 0:
                    sc.op("vector", lambda e, ot=ot, cc=cc: e.tensor_copy(
                        ySB[:, ot, cc * 512:(cc + 1) * 512], pw4[:, :]),
                        wait=[("dm", sc.cur("dm"))], inc="de")
                else:
                    sc.op("vector", lambda e, ot=ot, cc=cc: e.tensor_add(
                        ySB[:, ot, cc * 512:(cc + 1) * 512],
                        ySB[:, ot, cc * 512:(cc + 1) * 512], pw4[:, :]),
                        wait=[("dm", sc.cur("dm"))], inc="de")
                rr_pw.done(0, "de")
            rr_dwe.done(ot * 2, "dm")
            rr_dwe.done(ot * 2 + 1, "dm")
        sc.flush(f"q4d_{pair}")
        if pair == 0:
            tap(sc, "ySBp0", ySB[:, :, :], [128, 16, NOWN], F32)

    for j in range(22):
        bw = j % 2
        sc.dma("gpsimd", web[bw][:, :, :],
               kview(sw_gate)[:, :, j * 128:(j + 1) * 128], "guw",
               wait=[rr_we.use(j)])
        sc.dma("gpsimd", web[2 + bw][:, :, :],
               kview(sw_up)[:, :, j * 128:(j + 1) * 128], "guw")
        for cc in range(2):
            for kt in range(16):
                w = [("guw", sc.cur("guw")), rr_pg.use(0)] if kt == 0 else []
                sc.op("tensor", lambda e, kt=kt, cc=cc, bw=bw: e.matmul(
                    pg[:, :], web[bw][:, kt, :],
                    xn1T_own[:, kt, cc * 512:(cc + 1) * 512],
                    start=(kt == 0), stop=(kt == 15)),
                    wait=w, inc="gm" if kt == 15 else None)
            for kt in range(16):
                w = [rr_pu.use(0)] if kt == 0 else []
                sc.op("tensor", lambda e, kt=kt, cc=cc, bw=bw: e.matmul(
                    pu[:, :], web[2 + bw][:, kt, :],
                    xn1T_own[:, kt, cc * 512:(cc + 1) * 512],
                    start=(kt == 0), stop=(kt == 15)),
                    wait=w, inc="um" if kt == 15 else None)
            sc.op("scalar", lambda e: e.activation(gtmp[:, :], pg[:, :], AF.Silu),
                  wait=[("gm", sc.cur("gm")), rr_gt.use(0)], inc="gsl")
            rr_pg.done(0, "gsl")
            sc.op("vector", lambda e, cc=cc, j=j: e.tensor_copy(
                hbig[:, j, cc * 512:(cc + 1) * 512], pu[:, :]),
                wait=[("um", sc.cur("um"))], inc="uevm")
            rr_pu.done(0, "uevm")
            sc.op("vector", lambda e, cc=cc, j=j: e.tensor_mul(
                hbig[:, j, cc * 512:(cc + 1) * 512], gtmp[:, :],
                hbig[:, j, cc * 512:(cc + 1) * 512]),
                wait=[("gsl", sc.cur("gsl"))], inc="hc")
            rr_gt.done(0, "hc")
        rr_we.done(j, "um")
        if j % 8 == 7:
            sc.flush(f"q4s_{j}")
    sc.flush("q4_sh")
    rr_sd1 = RR(sc, 2)
    rr_sd2 = RR(sc, 2)
    for cc in range(2):
        for ot in range(16):
            bw = (cc * 16 + ot) % 2
            i = cc * 16 + ot
            sc.dma("gpsimd", web[bw][:, :, :],
                   kview(sw_down)[:, 0:16, ot * 128:(ot + 1) * 128], "sdw",
                   wait=[rr_sd1.use(i)])
            sc.dma("gpsimd", web[2 + bw][:, 0:6, :],
                   kview(sw_down)[:, 16:22, ot * 128:(ot + 1) * 128], "sdw",
                   wait=[rr_sd2.use(i)])
            for kt in range(22):
                w = [("sdw", sc.cur("sdw")), rr_pw.use(0)] if kt == 0 else []
                wsrc = web[bw] if kt < 16 else web[2 + bw]
                ktl = kt if kt < 16 else kt - 16
                sc.op("tensor", lambda e, kt=kt, cc=cc, wsrc=wsrc, ktl=ktl: e.matmul(
                    pw4[:, :], wsrc[:, ktl, :], hbig[:, kt, cc * 512:(cc + 1) * 512],
                    start=(kt == 0), stop=(kt == 21)),
                    wait=w, inc="sm" if kt == 21 else None)
            rr_sd1.done(i, "sm")
            rr_sd2.done(i, "sm")
            sc.op("vector", lambda e, ot=ot, cc=cc: e.tensor_add(
                ySB[:, ot, cc * 512:(cc + 1) * 512],
                ySB[:, ot, cc * 512:(cc + 1) * 512], pw4[:, :]),
                wait=[("sm", sc.cur("sm"))], inc="de")
            rr_pw.done(0, "de")
    sc.flush("q4_sd")
    tap(sc, "ySB", ySB[:, :, :], [128, 16, NOWN], F32)
    es5.close()
    es6 = ExitStack()
    T6 = lambda n, s, d: es6.enter_context(
        nc.sbuf_tensor(n, s, d, side="right"))
    P6s = lambda n, s, d: es6.enter_context(nc.psum_tensor(n, s, d))
    atok = [T6(f"atok2_{i}", [128, D], F32) for i in range(4)]
    ptrf = [P6s(f"ptrf2_{i}", [128, 128], F32) for i in range(2)]
    rr_trf = RR(sc, 2)
    for cc in range(2):
        atok_guard = ("rda", sc.cur("rda"))
        for ot in range(16):
            for tt in range(4):
                i2 = sc.cur("ytr")
                w3 = [rr_trf.use(i2)]
                if ot == 0 and tt == 0:
                    w3.append(atok_guard)
                sc.op("tensor", lambda e, ot=ot, tt=tt, cc=cc, i2=i2: e.transpose(
                    ptrf[i2 % 2][:, :],
                    ySB[:, ot, cc * 512 + tt * 128: cc * 512 + (tt + 1) * 128],
                    idf[:, :]), wait=w3, inc="ytr")
                sc.op("scalar", lambda e, tt=tt, ot=ot, i2=i2: e.copy(
                    atok[tt][:, ot * 128:(ot + 1) * 128], ptrf[i2 % 2][:, :]),
                    wait=[("ytr", sc.cur("ytr"))], inc="yte")
                rr_trf.done(i2, "yte")
        te_all = sc.cur("yte")
        for tt in range(4):
            sc.dma("gpsimd",
                   out_own[cc * 512 + tt * 128: cc * 512 + (tt + 1) * 128, :],
                   atok[tt][:, :], "rda", wait=[("yte", te_all)],
                   accum_op=ALU.add)
        sc.flush(f"q5_{cc}")
    sc.op("gpsimd", lambda e: e.engine_nop(),
          wait=[("rda", sc.cur("rda")), ("x2c", 16)])
    sc.flush("fin")
    es6.close()
    es4.close()
    es.close()
    return nc


def build_fused():
    nc = bass.Bass("TRN2", target_bir_lowering=False)
    EI, EO = "ExternalInput", "ExternalOutput"
    DT = lambda n, s, d, k="Internal": nc.dram_tensor(n, s, d, kind=k).ap()
    x1_sh = nc.dram_tensor("x1_sh", [S, D], F32, addr_space="Shared").ap()
    shf = nc.dram_tensor("shf", [4, 1], U32, addr_space="Shared").ap()
    scat_idx = DT("scat_idx", [128, 16], U32, EI)
    gath_idx = DT("gath_idx", [128, 8], U32, EI)
    flag_idx = DT("flag_idx", [2, 1], U32, EI)
    nonce_in = DT("nonce_in", [2, 1], U32, EI)
    out_own = DT("out_own", [NOWN, D], F32, EO)
    out_own_rd = out_own
    x_full = DT("x_full", [S, D], F32, EI)
    x_own = DT("x_own", [2 * NOWN, D], F32, EI)
    ident = DT("ident", [128, 128], BF16, EI)
    cmask0 = DT("cmask0", [8, 4, 128, CHUNK], BF16, EI)
    cmask = DT("cmask", [4, 8, 128, CHUNK], BF16, EI)
    cos0 = DT("cos0", [32, S], F32, EI)
    sin0 = DT("sin0", [32, S], F32, EI)
    cos0q = DT("cos0q", [32, 2 * NOWN], F32, EI)
    sin0q = DT("sin0q", [32, 2 * NOWN], F32, EI)
    w_dkv = DT("w_dkv", [D, 512], F32, EI)
    b_dkv = DT("b_dkv", [128, 4], F32, EI)
    w_rk = DT("w_rk", [D, 64], F32, EI)
    b_rk = DT("b_rk", [32, 2], F32, EI)
    w_dq = DT("w_dq", [D, 1536], F32, EI)
    b_dq = DT("b_dq", [128, 12], F32, EI)
    w_rq = DT("w_rq", [1536, 1024], F32, EI)
    b_rq = DT("b_rq", [32, 32], F32, EI)
    w_uk = DT("w_uk", [512, D], F32, EI)
    b_uk = DT("b_uk", [128, 16], F32, EI)
    w_uv = DT("w_uv", [512, D], F32, EI)
    b_uv = DT("b_uv", [1, D], F32, EI)
    w_uq = DT("w_uq", [1536, D], F32, EI)
    b_uq = DT("b_uq", [128, 16], F32, EI)
    w_o = DT("w_o", [D, D], F32, EI)
    b_o = DT("b_o", [128, 16], F32, EI)
    gw = DT("gw", [D, 6144], F32, EI)
    uw = DT("uw", [D, 6144], F32, EI)
    dw = DT("dw", [6144, D], F32, EI)
    gsc = DT("gsc", [128, 48], F32, EI)
    usc = DT("usc", [128, 48], F32, EI)
    dsc = DT("dsc", [128, 16], F32, EI)
    x1_out = DT("x1_local", [2 * NOWN, D], F32)
    k_sp = DT("k_sp", [16, 128, S], BF16)
    v_sp = DT("v_sp", [128, 32, 16, 128], BF16)

    esG = ExitStack()
    TG = lambda n, s, d: esG.enter_context(nc.sbuf_tensor(n, s, d))
    es = ExitStack()
    esR = ExitStack()
    T = lambda n, s, d: es.enter_context(nc.sbuf_tensor(n, s, d))
    TR = lambda n, s, d: esR.enter_context(nc.sbuf_tensor(n, s, d, side="right"))
    sc = Sched(nc)

    idb = TG("idb", [128, 128], BF16)
    idf = TG("idf", [128, 128], F32)
    onesc = TG("onesc", [128, 1], BF16)
    epsb = TG("epsb", [128, 1], F32)
    ones1b = TG("ones1b", [1, 128], BF16)
    ones1f = TG("ones1f", [1, 128], F32)
    kmax = T("kmax", [1, NCH * 16], F32)
    Km = T("Km", [1, 16 * 8], F32)
    bias_dkv = T("bias_dkv", [128, 4], F32)
    bias_rk = T("bias_rk", [32, 2], F32)
    bias_dq = T("bias_dq", [128, 12], F32)
    bias_rq = T("bias_rq", [32, 32], F32)
    bias_uk = T("bias_uk", [128, 16], F32)

    bias_uq = T("bias_uq", [128, 16], F32)
    bias_o = T("bias_o", [128, 16], F32)
    gsc_sb = T("gsc_sb", [128, 48], F32)
    usc_sb = T("usc_sb", [128, 48], F32)
    dsc_sb = T("dsc_sb", [128, 16], F32)
    xnT_own = T("xnT_own", [128, 16, NOWN], BF16)
    ropek = TR("ropek", [128, S], BF16)
    c0q = TR("c0q", [32, NOWN], F32)
    s0q = TR("s0q", [32, NOWN], F32)

    for dst, src in [(idb, ident), (bias_dkv, b_dkv), (bias_rk, b_rk),
                     (bias_dq, b_dq), (bias_rq, b_rq), (bias_uk, b_uk),
                     (bias_uq, b_uq), (bias_o, b_o), (gsc_sb, gsc),
                     (usc_sb, usc), (dsc_sb, dsc)]:
        sc.dma("sync", dst[:, :], src, "cld")
    sc.dma("gpsimd", idf[:, :], ident, "cld")
    sc.dma("sync", x1_out[0:NOWN, :], x_own[0:NOWN, :], "x1c")
    sc.dma("sync", x1_out[NOWN:2 * NOWN, :], x_own[NOWN:2 * NOWN, :], "x1c")
    sc.op("vector", lambda e: e.memset(onesc[:, :], 1.0))
    sc.op("vector", lambda e: e.memset(epsb[:, :], EPS))
    sc.op("vector", lambda e: e.memset(ones1b[:, :], 1.0))
    sc.op("vector", lambda e: e.memset(ones1f[:, :], 1.0))
    sc.op("vector", lambda e: e.memset(ropek[64:65, :], 1.0))
    sc.flush("p0")

    NT = 2
    esL = ExitStack()
    TL = lambda n, s, d: esL.enter_context(nc.sbuf_tensor(n, s, d))
    TLR = lambda n, s, d: esL.enter_context(nc.sbuf_tensor(n, s, d, side="right"))
    PL = lambda n, s, d: esL.enter_context(nc.psum_tensor(n, s, d))
    es1 = ExitStack()
    T1 = lambda n, s, d: es1.enter_context(nc.sbuf_tensor(n, s, d))
    _scr = {}
    c0c = T1("c0c", [32, CHUNK], F32)
    s0c = T1("s0c", [32, CHUNK], F32)
    xnT_c = [T1(f"xnTc{i}", [128, 16, CHUNK], BF16) for i in range(2)]
    lkvT_c = [T1(f"lkvTc{i}", [128, 4, CHUNK], BF16) for i in range(2)]
    kc_sb = T1("kc_sb", [128, 16, CHUNK], BF16)
    vc_sb = T1("vc_sb", [128, NT, D], BF16)
    P1p = lambda n, s, d: es1.enter_context(nc.psum_tensor(n, s, d))
    ptp = [P1p(f"ptp{i}", [128, 128], BF16) for i in range(2)]
    pproj = [P1p(f"pproj{i}", [128, CHUNK], F32) for i in range(2)]
    prk = P1p("prk", [64, CHUNK], F32)
    pv = P1p("pvv", [128, 512], F32)
    pnrm = P1p("pnrm", [1, 512], F32)
    rr_tp = RR(sc, 2)
    rr_pp = RR(sc, 2)
    rr_xc = RR(sc, 2)
    rr_lkv = RR(sc, 2)
    rr_prk = RR(sc, 1)
    rr_pv = RR(sc, 1)
    rr_pn = RR(sc, 1)
    rr_kc = RR(sc, 1)
    rr_vc = RR(sc, 1)

    for _n, _s, _d in [("sq_junk", [128, D], BF16), ("tr1", [32, CHUNK], F32),
                       ("tr2", [32, CHUNK], F32), ("x1b", [32, CHUNK], F32),
                       ("x2b", [32, CHUNK], F32), ("qrow", [1, 512], F32)]:
        _scr[_n] = es1.enter_context(nc.sbuf_tensor(_n, _s, _d, allow_name_mangling=True))
    sq_junk = Box(_scr, "sq_junk"); tr1 = Box(_scr, "tr1"); tr2 = Box(_scr, "tr2")
    x1b = Box(_scr, "x1b"); x2b = Box(_scr, "x2b"); qrow = Box(_scr, "qrow")
    sqbuf = T1("sqbuf", [128, CHUNK], BF16)
    sqrope = T1("sqrope", [64, CHUNK], BF16)
    xc = [T1(f"xc{i}", [128, NT, D], F32) for i in range(2)]
    xntok = [T1(f"xntok{i}", [128, NT, D], BF16) for i in range(2)]
    sttb = [T1("stt", [128, 8], F32)]
    bias_uv = T1("bias_uv", [1, D], BF16)
    sc.dma("gpsimd", bias_uv[:, :], b_uv, "cld")
    wdkv_sb = T1("wdkv_sb", [128, 16, 512], BF16)
    wrk_sb = T1("wrk_sb", [128, 16, 64], BF16)
    wuk_sb = T1("wuk_sb", [128, 4, D], BF16)
    wuv_sb = T1("wuv_sb", [128, 4, D], BF16)
    sc.dma("gpsimd", wdkv_sb[:, :, :], kview(w_dkv), "wld")
    sc.dma("gpsimd", wrk_sb[:, :, :], kview(w_rk), "wld")
    sc.dma("gpsimd", wuk_sb[:, :, :], kview(w_uk), "wld")
    sc.dma("gpsimd", wuv_sb[:, :, :], kview(w_uv), "wld")
    wait_start = [("cld", sc.cur("cld")), ("wld", sc.cur("wld"))]

    def l0_chunk(c, x_src, is_own, oc=None):
        bi = c % 2
        sc.dma("sync", xc[bi][:, :, :],
               x_src.rearrange("(n t p) d -> n p t d", t=NT, p=128)[c if not is_own else oc],
               "xld", wait=(wait_start if (c == 0 and not is_own) else [rr_xc.use(c)]))
        emit_ln(sc, xc[bi], xntok[bi], sq_junk, sttb[0], ("xld", sc.cur("xld")), "L", epsb)
        rr_xc.done(c, "Lxn")

    for c in range(NCH):
        bi = c % 2
        l0_chunk(c, x_full, False)
        emit_tposes(sc, xntok[bi],
                    lambda f, tt, bi=bi: xnT_c[bi][:, f, tt * 128:(tt + 1) * 128],
                    idb, ptp, rr_tp, "L")
        xt_done = sc.cur("Lte")
        for ot in range(4):
            i = sc.cur("pm")
            for kt in range(16):
                w = [rr_pp.use(i)] if kt == 0 else []
                if kt == 0 and ot == 0:
                    w.append(("Lte", xt_done))
                sc.op("tensor", lambda e, ot=ot, kt=kt, i=i, bi=bi: e.matmul(
                    pproj[i % 2][:, :], wdkv_sb[:, kt, ot * 128:(ot + 1) * 128],
                    xnT_c[bi][:, kt, :], start=(kt == 0), stop=(kt == 15)),
                    wait=w, inc="pm" if kt == 15 else None)
            lw = [("pm", sc.cur("pm")), rr_lkv.use(c)] if ot == 0 else \
                 [("pm", sc.cur("pm"))]
            sc.op("vector", lambda e, ot=ot, i=i, bi=bi: e.tensor_scalar(
                lkvT_c[bi][:, ot, :], pproj[i % 2][:, :],
                bias_dkv[:, ot:ot + 1], None, ALU.add), wait=lw, inc="pe")
            rr_pp.done(i, "pe")
        for kt in range(16):
            w = [rr_prk.use(0)] if kt == 0 else ()
            sc.op("tensor", lambda e, kt=kt, bi=bi: e.matmul(
                prk[:, :], wrk_sb[:, kt, :], xnT_c[bi][:, kt, :],
                start=(kt == 0), stop=(kt == 15)),
                wait=w, inc="rkm" if kt == 15 else None)
        sc.op("vector", lambda e: e.tensor_scalar(
            x1b[:, :], prk[0:32, :], bias_rk[:, 0:1], None, ALU.add),
            wait=[("rkm", sc.cur("rkm")), ("rkr", sc.cur("rkr"))])
        sc.op("vector", lambda e: e.tensor_scalar(
            x2b[:, :], prk[32:64, :], bias_rk[:, 1:2], None, ALU.add),
            inc="rke")
        rr_prk.done(0, "rke")
        sc.dma("sync", c0c[:, :], cos0[:, c * CHUNK:(c + 1) * CHUNK], "cs",
               wait=[("rkr", sc.cur("rkr"))])
        sc.dma("sync", s0c[:, :], sin0[:, c * CHUNK:(c + 1) * CHUNK], "cs")
        emit_rope(sc, (tr1, tr2), x1b[:, :], x2b[:, :],
                  ropek[0:32, c * CHUNK:(c + 1) * CHUNK],
                  ropek[32:64, c * CHUNK:(c + 1) * CHUNK],
                  c0c[0:32, :], s0c[0:32, :], "rkr",
                  wait=[("rke", sc.cur("rke")), ("cs", sc.cur("cs"))])
        lkv_done = sc.cur("pe")
        for h in range(16):
            i = sc.cur("pm")
            for kt in range(4):
                w = []
                if kt == 0:
                    w = [rr_pp.use(i), ("pe", lkv_done)]
                    if h == 0:
                        w.append(rr_kc.use(0))
                sc.op("tensor", lambda e, h=h, kt=kt, i=i, bi=bi: e.matmul(
                    pproj[i % 2][:, :], wuk_sb[:, kt, h * 128:(h + 1) * 128],
                    lkvT_c[bi][:, kt, :], start=(kt == 0), stop=(kt == 3)),
                    wait=w, inc="pm" if kt == 3 else None)
            sc.op("vector", lambda e, h=h, i=i: e.tensor_scalar(
                kc_sb[:, h, :], pproj[i % 2][:, :], bias_uk[:, h:h + 1],
                None, ALU.add), wait=[("pm", sc.cur("pm"))], inc="pe")
            rr_pp.done(i, "pe")
        sc.op("scalar", lambda e, c=c: e.activation(
            sqrope[:, :], ropek[0:64, c * CHUNK:(c + 1) * CHUNK], AF.Square),
            wait=[("rkr", sc.cur("rkr")), ("nmm", sc.cur("nmm"))], inc="rsq")
        for h in range(16):
            sc.op("scalar", lambda e, h=h: e.activation(
                sqbuf[:, :], kc_sb[:, h, :], AF.Square),
                wait=[("pe", sc.cur("pe")), ("nmm", sc.cur("nmm"))], inc="nsq")
            sc.op("tensor", lambda e: e.matmul(
                pnrm[0:1, 0:CHUNK], onesc[:, :], sqbuf[:, :], start=True, stop=False),
                wait=[("nsq", sc.cur("nsq")), rr_pn.use(0),
                      ("rsq", sc.cur("rsq"))])
            sc.op("tensor", lambda e: e.matmul(
                pnrm[0:1, 0:CHUNK], onesc[0:64, :], sqrope[:, :], start=False, stop=True),
                inc="nmm")
            sc.op("vector", lambda e, h=h, c=c: e.tensor_reduce(
                kmax[0:1, c * 16 + h:c * 16 + h + 1], pnrm[0:1, 0:CHUNK],
                axis=AX.X, op=ALU.max),
                wait=[("nmm", sc.cur("nmm"))], inc="nrd")
            rr_pn.done(0, "nrd")
        sc.dma("sync", k_sp.rearrange("h p (n t) -> p h n t", t=CHUNK)[:, :, c, :],
               kc_sb[:, :, :], "ksp", wait=[("nsq", sc.cur("nsq"))])
        rr_kc.done(0, "ksp")
        for g in range(4):
            for ktk in range(NT):
                for lt in range(4):
                    w = []
                    if lt == 0:
                        w = [rr_pv.use(0)]
                        if g == 0 and ktk == 0:
                            w.append(rr_vc.use(0))
                    sc.op("tensor", lambda e, g=g, ktk=ktk, lt=lt, bi=bi: e.matmul(
                        pv[:, :], lkvT_c[bi][:, lt, ktk * 128:(ktk + 1) * 128],
                        wuv_sb[:, lt, g * 512:(g + 1) * 512],
                        start=(lt == 0), stop=False), wait=w)
                sc.op("tensor", lambda e, g=g: e.matmul(
                    pv[:, :], ones1b[:, :], bias_uv[0:1, g * 512:(g + 1) * 512],
                    start=False, stop=True), inc="vm")
                sc.op("vector", lambda e, g=g, ktk=ktk: e.tensor_copy(
                    vc_sb[:, ktk, g * 512:(g + 1) * 512], pv[:, :]),
                    wait=[("vm", sc.cur("vm"))], inc="ve")
                rr_pv.done(0, "ve")
        sc.dma("sync",
               v_sp.rearrange("p t h d -> p t (h d)")[:, 2 * c:2 * c + 2, :],
               vc_sb[:, :, :], "vsp", wait=[("ve", sc.cur("ve"))])
        rr_vc.done(0, "vsp")
        rr_lkv.done(c, "ve")
    sc.flush("p1_end")
    for h in range(16):
        for s_ in range(8):
            npre = PADL0_ALL[s_] // 2
            sc.op("vector", lambda e, h=h, s_=s_, npre=npre: e.tensor_reduce(
                Km[0:1, h * 8 + s_: h * 8 + s_ + 1],
                kmax.ap().rearrange("p (c h) -> p h c", h=16)[0:1, h, 0:npre],
                axis=AX.X, op=ALU.max), inc="kmf")
    sc.op("scalar", lambda e: e.activation(Km[0:1, :], Km[0:1, :], AF.Sqrt),
          wait=[("kmf", sc.cur("kmf"))], inc="kms")
    sc.op("vector", lambda e: e.tensor_scalar(
        Km[0:1, :], Km[0:1, :], -1.0, None, ALU.mult),
          wait=[("kms", sc.cur("kms"))], inc="kmn")
    sc.flush("p1_km")
    es1.close()

    for half in range(2):
        sc.dma("sync", c0q[:, :], cos0q[:, half * NOWN:(half + 1) * NOWN], "csq")
        sc.dma("sync", s0q[:, :], sin0q[:, half * NOWN:(half + 1) * NOWN], "csq")
        esRH = ExitStack()
        TRH = lambda n, s, d: esRH.enter_context(
            nc.sbuf_tensor(n, s, d, side="right", allow_name_mangling=True))
        qx = TRH("qx", [128, 16, NOWN], BF16)
        rqx = TRH("rqx", [128, 16 * NOWN], BF16)
        es2 = ExitStack()
        T2 = lambda n, s, d: es2.enter_context(nc.sbuf_tensor(n, s, d, allow_name_mangling=True))
        P2p = lambda n, s, d: es2.enter_context(
            nc.psum_tensor(n, s, d, allow_name_mangling=True))
        ptp = [P2p(f"ptpH{i}", [128, 128], BF16) for i in range(2)]
        pproj = [P2p(f"pprojH{i}", [128, CHUNK], F32) for i in range(2)]
        pv = P2p("pvvH", [128, 512], F32)
        pnrm = P2p("pnrmH", [1, 512], F32)
        xc[0] = T2("xcH0", [128, NT, D], F32)
        xc[1] = T2("xcH1", [128, NT, D], F32)
        xntok[0] = T2("xntokH0", [128, NT, D], BF16)
        xntok[1] = T2("xntokH1", [128, NT, D], BF16)
        sttb[0] = T2("sttH", [128, 8], F32)
        for _n, _s, _d in [("sq_junk", [128, D], BF16), ("tr1", [32, CHUNK], F32),
                           ("tr2", [32, CHUNK], F32), ("x1b", [32, CHUNK], F32),
                           ("x2b", [32, CHUNK], F32), ("qrow", [1, 512], F32)]:
            _scr[_n] = es2.enter_context(nc.sbuf_tensor(_n + "H", _s, _d, allow_name_mangling=True))
        for oc in range(4):
            bi = oc % 2
            l0_chunk(oc, x_own, True, half * 4 + oc)
            emit_tposes(sc, xntok[bi],
                        lambda f, tt, oc=oc: xnT_own[:, f, oc * CHUNK + tt * 128:
                                                     oc * CHUNK + (tt + 1) * 128],
                        idb, ptp, rr_tp, "L")
        sc.flush("p1b_ln_h" + str(half))

        wq_sb = [T2(f"wq_sb{i}", [128, 16, 128], BF16) for i in range(2)]
        lqT = T2("lqT", [128, 12, NOWN], BF16)
        rr_wq = RR(sc, 2)
        xt_own = sc.cur("Lte")
        for ot in range(12):
            bw = ot % 2
            sc.dma("gpsimd", wq_sb[bw][:, :, :],
                   kview(w_dq)[:, :, ot * 128:(ot + 1) * 128], "wq", wait=[rr_wq.use(ot)])
            for cc in range(2):
                i = sc.cur("pm2")
                for kt in range(16):
                    w = []
                    if kt == 0:
                        w = [("wq", sc.cur("wq")), rr_pv.use(0)]
                        if ot == 0 and cc == 0:
                            w.append(("Lte", xt_own))
                    sc.op("tensor", lambda e, ot=ot, cc=cc, kt=kt, bw=bw: e.matmul(
                        pv[:, 0:512], wq_sb[bw][:, kt, :],
                        xnT_own[:, kt, cc * 512:(cc + 1) * 512],
                        start=(kt == 0), stop=(kt == 15)),
                        wait=w, inc="pm2" if kt == 15 else None)
                sc.op("vector", lambda e, ot=ot, cc=cc: e.tensor_scalar(
                    lqT[:, ot, cc * 512:(cc + 1) * 512], pv[:, 0:512],
                    bias_dq[:, ot:ot + 1], None, ALU.add),
                    wait=[("pm2", sc.cur("pm2"))], inc="pe2")
                rr_pv.done(0, "pe2")
            rr_wq.done(ot, "pm2")
        sc.flush("p1b_lq_h" + str(half))
        for h in range(16):
            bw = (h // 2) % 2
            if h % 2 == 0:
                sc.dma("gpsimd", wq_sb[bw][:, 0:12, :],
                       kview(w_rq)[:, :, h * 64:(h + 2) * 64], "wq",
                       wait=[rr_wq.use(h // 2)])
            for oc in range(4):
                i = sc.cur("pm2")
                for kt in range(12):
                    w = [("wq", sc.cur("wq")), rr_pp.use(i)] if kt == 0 else []
                    sc.op("tensor", lambda e, h=h, oc=oc, kt=kt, bw=bw, i=i: e.matmul(
                        pproj[i % 2][0:64, :],
                        wq_sb[bw][:, kt, (h % 2) * 64:(h % 2) * 64 + 64],
                        lqT[:, kt, oc * CHUNK:(oc + 1) * CHUNK],
                        start=(kt == 0), stop=(kt == 11)),
                        wait=w, inc="pm2" if kt == 11 else None)
                sc.op("vector", lambda e, h=h, i=i: e.tensor_scalar(
                    x1b[:, :], pproj[i % 2][0:32, :], bias_rq[:, 2 * h:2 * h + 1],
                    None, ALU.add),
                    wait=[("pm2", sc.cur("pm2")), ("rqr", sc.cur("rqr")),
                          ("csq", sc.cur("csq"))])
                sc.op("vector", lambda e, h=h, i=i: e.tensor_scalar(
                    x2b[:, :], pproj[i % 2][32:64, :], bias_rq[:, 2 * h + 1:2 * h + 2],
                    None, ALU.add), inc="rqe")
                rr_pp.done(i, "rqe")
                qc = h * NOWN + oc * CHUNK
                emit_rope(sc, (tr1, tr2), x1b[:, :], x2b[:, :],
                          rqx[0:32, qc:qc + CHUNK], rqx[32:64, qc:qc + CHUNK],
                          c0q[0:32, oc * CHUNK:(oc + 1) * CHUNK],
                          s0q[0:32, oc * CHUNK:(oc + 1) * CHUNK], "rqr",
                          wait=[("rqe", sc.cur("rqe"))])
            if h % 2 == 1:
                rr_wq.done(h // 2, "pm2")
        sc.flush("p1b_rq_h" + str(half))
        for ot in range(16):
            bw = ot % 2
            h = ot
            sc.dma("gpsimd", wq_sb[bw][:, 0:12, :],
                   kview(w_uq)[:, :, ot * 128:(ot + 1) * 128], "wq", wait=[rr_wq.use(ot)])
            for cc in range(2):
                for kt in range(12):
                    w = [("wq", sc.cur("wq")), rr_pv.use(0)] if kt == 0 else []
                    sc.op("tensor", lambda e, ot=ot, cc=cc, kt=kt, bw=bw: e.matmul(
                        pv[:, 0:512], wq_sb[bw][:, kt, :],
                        lqT[:, kt, cc * 512:(cc + 1) * 512],
                        start=(kt == 0), stop=(kt == 11)),
                        wait=w, inc="pm2" if kt == 11 else None)
                sc.op("vector", lambda e, ot=ot, cc=cc: e.tensor_scalar(
                    qx[:, ot, cc * 512:(cc + 1) * 512], pv[:, 0:512],
                    bias_uq[:, ot:ot + 1], None, ALU.add),
                    wait=[("pm2", sc.cur("pm2"))], inc="pe2")
                rr_pv.done(0, "pe2")
                qc = h * NOWN + cc * 512
                sc.op("scalar", lambda e, h=h, cc=cc: e.activation(
                    sq_junk[:, 0:512], qx[:, h, cc * 512:(cc + 1) * 512], AF.Square),
                    wait=[("pe2", sc.cur("pe2")), ("qnm", sc.cur("qnm"))], inc="qsq")
                sc.op("scalar", lambda e, qc=qc: e.activation(
                    sq_junk[0:64, 512:1024], rqx[0:64, qc:qc + 512], AF.Square),
                    inc="qsq")
                sc.op("tensor", lambda e: e.matmul(
                    pnrm[0:1, 0:512], onesc[:, :], sq_junk[:, 0:512],
                    start=True, stop=False),
                    wait=[("qsq", sc.cur("qsq")), ("qrd", sc.cur("qrd"))])
                sc.op("tensor", lambda e: e.matmul(
                    pnrm[0:1, 0:512], onesc[0:64, :], sq_junk[0:64, 512:1024],
                    start=False, stop=True), inc="qnm")
                sc.op("scalar", lambda e: e.activation(
                    qrow[0:1, :], pnrm[0:1, 0:512], AF.Sqrt),
                    wait=[("qnm", sc.cur("qnm"))], inc="qrt")
                for sh in range(2):
                    s_ = cc * 2 + sh
                    qc2 = h * NOWN + s_ * CHUNK
                    ks = h * 8 + half * 4 + s_
                    sc.op("vector", lambda e, ks=ks, qc2=qc2, sh=sh: e.tensor_scalar(
                        rqx[64:65, qc2:qc2 + CHUNK],
                        qrow[0:1, sh * CHUNK:(sh + 1) * CHUNK],
                        Km[0:1, ks:ks + 1], COFF / SC0,
                        ALU.mult, ALU.add),
                        wait=[("qrt", sc.cur("qrt"))], inc="qrd")
            rr_wq.done(ot, "pm2")
        sc.flush("p1b_uq_h" + str(half))
        es2.close()

        es25 = ExitStack()
        attnT = es25.enter_context(nc.sbuf_tensor("attnT", [128, 16, NOWN], BF16, allow_name_mangling=True))
        msk0 = es25.enter_context(nc.sbuf_tensor("msk0", [128, 8 * 4 * CHUNK], BF16, allow_name_mangling=True))
        sc.dma("sync", msk0.ap().rearrange("p (a q) -> p a q", q=CHUNK),
               cmask0.rearrange("s r p q -> p (s r) q"), "mskl")
        es3 = ExitStack()
        T3 = lambda n, s, d: es3.enter_context(nc.sbuf_tensor(n, s, d, side="right", allow_name_mangling=True))
        P3s = lambda n, s, d: es3.enter_context(nc.psum_tensor(n, s, d, allow_name_mangling=True))
        kbuf = [T3(f"kbuf{i}", [128, S], BF16) for i in range(2)]
        vbuf = [T3(f"vbuf{i}", [128, 32, 128], BF16) for i in range(2)]
        ptsb = [T3(f"ptsb{i}", [128, CHUNK], BF16) for i in range(2)]
        lsb = T3("lsb", [1, CHUNK], F32)
        rbsb = T3("rbsb", [128, CHUNK], F32)
        psc = [P3s(f"psc{i}", [128, CHUNK], F32) for i in range(2)]
        pout = [P3s(f"pout{i}", [128, CHUNK], F32) for i in range(2)]
        pl = [P3s(f"pl{i}", [1, CHUNK], F32) for i in range(2)]
        pbc = P3s("pbc", [128, CHUNK], F32)
        rr_kv = RR(sc, 2)
        rr_scB = RR(sc, 2)
        rr_ptB = RR(sc, 2)
        kvl_at = {}
        sc.op("vector", lambda e: e.engine_nop(), wait=[("mskl", sc.cur("mskl"))],
              inc="mskb")

        for h in range(16):
            hb = h % 2
            sc.dma("sync", kbuf[hb][:, :], k_sp[h, :, :], "kvl", wait=[rr_kv.use(h)])
            sc.dma("sync", vbuf[hb][:, :, :], v_sp[:, :, h, :], "kvl")
            kvl_at[h] = sc.cur("kvl")

            def kf(e, bank, hh, s_, t, hb=hb, h=h):
                e.matmul(bank[:, :], kbuf[hb][:, t * 128:(t + 1) * 128],
                         qx[:, h, s_ * CHUNK:(s_ + 1) * CHUNK],
                         start=True, stop=False)
                return e.matmul(bank[:, :], ropek[0:65, t * 128:(t + 1) * 128],
                                rqx[0:65, h * NOWN + s_ * CHUNK: h * NOWN + (s_ + 1) * CHUNK],
                                start=False, stop=True)

            def qxf(e, dst, srcbank):
                return e.activation(dst[:, :], srcbank[:, :], AF.Exp, scale=SC0)

            def vf(e, bank, hh, t, pb, st_, sp_, hb=hb):
                return e.matmul(bank[:, :], vbuf[hb][:, t, :], ptsb[pb][:, :],
                                start=st_, stop=sp_)

            def outf(e, hh, s_, ob, h=h):
                return e.tensor_mul(attnT[:, h, s_ * CHUNK:(s_ + 1) * CHUNK],
                                    pout[ob][:, :], rbsb[:, :])

            def hook(hh, t=None, s_=None, h=h):
                if t is None:
                    return None
                if s_ == 0 and t < 2:
                    return ("kvl", kvl_at[h])
                return None

            emit_attention(sc, 1, PADL0_HALF[half], kf, qxf, vf,
                           lambda s_, rt: msk0[:, ((half * 4 + s_) * 4 + rt) * CHUNK:
                                               ((half * 4 + s_) * 4 + rt + 1) * CHUNK],
                           ptsb, psc, pout, pl,
                           pbc, lsb, rbsb, onesc, ones1f, outf, l_in_out=False,
                           head_hook=hook, nmask=4, rr_sc=rr_scB, rr_pt=rr_ptB)
            rr_kv.done(h, "pvd")
        sc.flush(f"p2_h{half}")

        es3.close()
        esRH.close()
        es3b = ExitStack()
        T3b = lambda n, s, d: es3b.enter_context(nc.sbuf_tensor(n, s, d, allow_name_mangling=True))
        P3b = lambda n, s, d: es3b.enter_context(nc.psum_tensor(n, s, d, allow_name_mangling=True))
        wo_sb = [T3b(f"wo_sb{i}", [128, 16, 128], BF16) for i in range(2)]
        woT = T3b("woT", [128, 512], F32)
        atok = [T3b(f"atok{i}", [128, D], F32) for i in range(4)]
        ptrf = [P3b(f"ptrf{i}", [128, 128], F32) for i in range(2)]
        pw = P3b("pw", [128, 512], F32)
        rr_wo = RR(sc, 2)
        rr_pw = RR(sc, 1)
        rr_trf = RR(sc, 2)
        rr_woT = RR(sc, 1)

        def proj_transpose_add(sc, w_src, n_kt, rhs_fn, out_dram, tag, scale_sb=None,
                               bias_sb=None, add_wait=None, wdt=BF16, h_kt=16):
            for cc in range(2):
                atok_guard = ("rda", sc.cur("rda"))
                for ot in range(16):
                    bw = ot % 2
                    sc.dma("gpsimd", wo_sb[bw][:, 0:h_kt, :] if wdt == BF16 else None,
                           kview(w_src)[:, :, ot * 128:(ot + 1) * 128], tag + "w",
                           wait=[rr_wo.use(cc * 16 + ot)])
                    i = cc * 16 + ot
                    for kt in range(n_kt):
                        w = [(tag + "w", sc.cur(tag + "w")), rr_pw.use(0)] if kt == 0 else []
                        sc.op("tensor", lambda e, cc=cc, ot=ot, kt=kt, bw=bw: e.matmul(
                            pw[:, :], wo_sb[bw][:, kt, :], rhs_fn(kt, cc),
                            start=(kt == 0), stop=(kt == n_kt - 1)),
                            wait=w, inc=tag + "m" if kt == n_kt - 1 else None)
                    rr_wo.done(i, tag + "m")
                    if bias_sb is not None:
                        sc.op("vector", lambda e, ot=ot: e.tensor_scalar(
                            woT[:, :], pw[:, :], bias_sb[:, ot:ot + 1], None, ALU.add),
                            wait=[(tag + "m", sc.cur(tag + "m")), rr_woT.use(0)],
                            inc=tag + "e")
                    else:
                        sc.op("vector", lambda e, ot=ot: e.tensor_scalar(
                            woT[:, :], pw[:, :], scale_sb[:, ot:ot + 1], None, ALU.mult),
                            wait=[(tag + "m", sc.cur(tag + "m")), rr_woT.use(0)],
                            inc=tag + "e")
                    rr_pw.done(0, tag + "e")
                    for tt in range(4):
                        i2 = sc.cur(tag + "tr")
                        w3 = [(tag + "e", sc.cur(tag + "e")), rr_trf.use(i2)]
                        if ot == 0 and tt == 0:
                            w3.append(atok_guard)
                        sc.op("tensor", lambda e, tt=tt, i2=i2: e.transpose(
                            ptrf[i2 % 2][:, :], woT[:, tt * 128:(tt + 1) * 128],
                            idf[:, :]), wait=w3, inc=tag + "tr")
                        sc.op("scalar", lambda e, tt=tt, ot=ot, i2=i2: e.copy(
                            atok[tt][:, ot * 128:(ot + 1) * 128], ptrf[i2 % 2][:, :]),
                            wait=[(tag + "tr", sc.cur(tag + "tr"))], inc=tag + "te")
                        rr_trf.done(i2, tag + "te")
                    rr_woT.done(0, tag + "te")
                te_all = sc.cur(tag + "te")
                for tt in range(4):
                    sc.dma("gpsimd",
                           out_dram[cc * 512 + tt * 128: cc * 512 + (tt + 1) * 128, :],
                           atok[tt][:, :], "rda",
                           wait=([(tag + "te", te_all)] + (add_wait or [])),
                           accum_op=ALU.add)
                sc.flush(tag + f"_{cc}")

        proj_transpose_add(sc, w_o, 16,
                           lambda kt, cc: attnT[:, kt, cc * 512:(cc + 1) * 512],
                           x1_out[half * NOWN:(half + 1) * NOWN, :], "wo",
                       bias_sb=bias_o, add_wait=[("x1c", 32)])

        es3b.close()
        es25.close()
        es4 = ExitStack()
        T4 = lambda n, s, d: es4.enter_context(nc.sbuf_tensor(n, s, d, allow_name_mangling=True))
        P4s = lambda n, s, d: es4.enter_context(nc.psum_tensor(n, s, d, allow_name_mangling=True))
        woT = T4("woT4", [128, 512], F32)
        ptrf = [P4s(f"ptrf4_{i}", [128, 128], F32) for i in range(2)]
        pw = P4s("pw4", [128, 512], F32)
        rr_pw = RR(sc, 1)
        rr_trf = RR(sc, 2)
        rr_woT = RR(sc, 1)
        gub = [T4(f"gub{i}", [128, 16, 128], FP8) for i in range(4)]
        h_sb = T4("h_sb", [128, 48, NOWN], BF16)
        gtmp = es4.enter_context(nc.sbuf_tensor(
            "gtmp", [128, 512], BF16, side="right", allow_name_mangling=True))
        pg = P4s("pg", [128, 512], F32)
        pu = P4s("pu", [128, 512], F32)
        rr_gu = RR(sc, 2)
        rr_pg = RR(sc, 1)
        rr_pu = RR(sc, 1)
        rr_gt = RR(sc, 1)
        rr_ut = RR(sc, 1)
        for j in range(48):
            bw = j % 2
            sc.dma("gpsimd", gub[bw][:, :, :], kview(gw)[:, :, j * 128:(j + 1) * 128],
                   "guw", wait=[rr_gu.use(j)], max_dma_last_dim=4096)
            sc.dma("gpsimd", gub[2 + bw][:, :, :], kview(uw)[:, :, j * 128:(j + 1) * 128],
                   "guw", max_dma_last_dim=4096)
            for cc in range(2):
                for kt in range(16):
                    w = [("guw", sc.cur("guw")), rr_pg.use(0)] if kt == 0 else []
                    sc.op("tensor", lambda e, j=j, cc=cc, kt=kt, bw=bw: e.matmul(
                        pg[:, :], gub[bw][:, kt, :],
                        xnT_own[:, kt, cc * 512:(cc + 1) * 512],
                        start=(kt == 0), stop=(kt == 15)),
                        wait=w, inc="gm" if kt == 15 else None)
                for kt in range(16):
                    w = [rr_pu.use(0)] if kt == 0 else []
                    sc.op("tensor", lambda e, j=j, cc=cc, kt=kt, bw=bw: e.matmul(
                        pu[:, :], gub[2 + bw][:, kt, :],
                        xnT_own[:, kt, cc * 512:(cc + 1) * 512],
                        start=(kt == 0), stop=(kt == 15)),
                        wait=w, inc="um" if kt == 15 else None)
                sc.op("scalar", lambda e, j=j: e.activation(
                    gtmp[:, :], pg[:, :], AF.Silu, scale=gsc_sb[:, j:j + 1]),
                    wait=[("gm", sc.cur("gm")), rr_gt.use(0)], inc="gsl")
                rr_pg.done(0, "gsl")
                sc.op("vector", lambda e, j=j, cc=cc: e.tensor_scalar(
                    h_sb[:, j, cc * 512:(cc + 1) * 512], pu[:, :],
                    usc_sb[:, j:j + 1], None, ALU.mult),
                    wait=[("um", sc.cur("um"))], inc="uev")
                rr_pu.done(0, "uev")
                sc.op("vector", lambda e, j=j, cc=cc: e.tensor_mul(
                    h_sb[:, j, cc * 512:(cc + 1) * 512], gtmp[:, :],
                    h_sb[:, j, cc * 512:(cc + 1) * 512]),
                    wait=[("gsl", sc.cur("gsl")), ("uev", sc.cur("uev"))], inc="hc")
                rr_gt.done(0, "hc")
            rr_gu.done(j, "um")
        sc.flush("p4_gu_h" + str(half))
        dwb = [es4.enter_context(nc.sbuf_tensor(
            f"dwb{i}", [128, 48, 128], FP8, side="right",
            allow_name_mangling=True)) for i in range(2)]
        dtile = [T4(f"dtile{i}", [128, 128], F32) for i in range(2)]
        rr_dw = RR(sc, 2)
        rr_dt = RR(sc, 2)
        for ot in range(16):
            bw = ot % 2
            sc.dma("gpsimd", dwb[bw][:, :, :], kview(dw)[:, :, ot * 128:(ot + 1) * 128],
                   "dww", wait=[rr_dw.use(ot)], max_dma_last_dim=4096)
            for cc in range(2):
                for kt in range(48):
                    w = [("dww", sc.cur("dww")), rr_pw.use(0)] if kt == 0 else []
                    sc.op("tensor", lambda e, cc=cc, ot=ot, kt=kt, bw=bw: e.matmul(
                        pw[:, :], dwb[bw][:, kt, :],
                        h_sb[:, kt, cc * 512:(cc + 1) * 512],
                        start=(kt == 0), stop=(kt == 47)),
                        wait=w, inc="dm" if kt == 47 else None)
                sc.op("vector", lambda e, ot=ot: e.tensor_scalar(
                    woT[:, :], pw[:, :], dsc_sb[:, ot:ot + 1], None, ALU.mult),
                    wait=[("dm", sc.cur("dm")), rr_woT.use(0)], inc="de")
                rr_pw.done(0, "de")
                for tt in range(4):
                    i2 = sc.cur("ftr")
                    w3 = [("de", sc.cur("de")), rr_trf.use(i2)]
                    sc.op("tensor", lambda e, tt=tt, i2=i2: e.transpose(
                        ptrf[i2 % 2][:, :], woT[:, tt * 128:(tt + 1) * 128],
                        idf[:, :]), wait=w3, inc="ftr")
                    sc.op("scalar", lambda e, i2=i2: e.copy(
                        dtile[i2 % 2][:, :], ptrf[i2 % 2][:, :]),
                        wait=[("ftr", sc.cur("ftr")), rr_dt.use(i2)], inc="fte")
                    rr_trf.done(i2, "fte")
                    sc.dma("gpsimd",
                           x1_out[half * NOWN + cc * 512 + tt * 128:
                                  half * NOWN + cc * 512 + (tt + 1) * 128,
                                  ot * 128:(ot + 1) * 128],
                           dtile[i2 % 2][:, :], "rda",
                           wait=[("fte", sc.cur("fte"))], accum_op=ALU.add)
                    rr_dt.done(i2, "rda")
                rr_woT.done(0, "fte")
            rr_dw.done(ot, "dm")
        sc.flush(f"p4d_h{half}")

        es4.close()
        es25.close()

    esL.close()
    sc.op("gpsimd", lambda e: e.engine_nop(),
          wait=[("rda", sc.cur("rda")), ("x1c", 32),
                ("ksp", sc.cur("ksp")), ("vsp", sc.cur("vsp"))])
    sc.flush("l0_fin")


    esR.close()
    es.close()

    esX = ExitStack()
    TX = lambda n, s, d: esX.enter_context(nc.sbuf_tensor(n, s, d, allow_name_mangling=True))
    sidx = TX("sidx", [128, 16], U32)
    gidx = TX("gidx", [128, 8], U32)
    fidx = TX("fidx", [2, 1], U32)
    nnc = TX("nnc", [2, 1], U32)
    fpoll = TX("fpoll", [1, 2], U32)
    xtile = [TX(f"xtile{i}", [128, D], F32) for i in range(2)]
    sc.dma("sync", sidx[:, :], scat_idx, "xgl")
    sc.dma("sync", gidx[:, :], gath_idx, "xgl")
    sc.dma("sync", fidx[:, :], flag_idx, "xgl")
    sc.dma("sync", nnc[:, :], nonce_in, "xgl")
    for t in range(16):
        sc.dma("sync", xtile[t % 2][:, :],
               x1_out[t * 128:(t + 1) * 128, :], "xtl",
               wait=[("xsc", sc.cur("xsc") - 16)] if t >= 2 else [("xgl", sc.cur("xgl"))])
        sc.op("gpsimd", lambda e, t=t: e.indirect_dma_start(
            x1_sh[:, :], bass.IndirectOffsetOnAxis(sidx[:, t:t + 1], 0),
            xtile[t % 2][:, :], None),
            wait=[("xtl", sc.cur("xtl"))], inc="xsc", inc_amt=16)
    sc.op("gpsimd", lambda e: e.indirect_dma_start(
        shf[:, :], bass.IndirectOffsetOnAxis(fidx[0:2, 0:1], 0),
        nnc[:, :], None), wait=[("xsc", sc.cur("xsc"))], inc="xsc", inc_amt=16)
    sc.flush("x1_scatter")

    with nc.Block("x1_poll") as blk:
        @blk.gpsimd
        def _(g):
            with g.register("rn") as rn, g.register("done") as done, \
                 g.register("plc") as plc, g.register("f0") as f0, \
                 g.register("f1") as f1:
                g.wait_ge(sc.sem("xsc"), sc.cur("xsc"))
                g.reg_load(rn, nnc[0:1, 0:1])
                g.reg_mov(done, 0)
                g.reg_mov(plc, 0)
                with g.Fori(0, 400):
                    with g.If_ne(done, 1):
                        with g.Fori(0, 64):
                            with g.If_ne(done, 1):
                                g.dma_start(fpoll[0:1, 0:2],
                                            shf.rearrange("r c -> c (r)")[0:1, 0:2]
                                            ).then_inc(sc.sem("plf"), 16)
                                g.reg_add(plc, plc, 16)
                                g.wait_ge(sc.sem("plf"), plc)
                                g.reg_load(f0, fpoll[0:1, 0:1])
                                g.reg_load(f1, fpoll[0:1, 1:2])
                                with g.If_eq(f0, rn):
                                    with g.If_eq(f1, rn):
                                        g.reg_mov(done, 1)

    for t in range(8):
        sc.op("gpsimd", lambda e, t=t: e.indirect_dma_start(
            xtile[t % 2][:, :], None, x1_sh[:, :],
            bass.IndirectOffsetOnAxis(gidx[:, t:t + 1], 0)),
            wait=[("x2c", sc.cur("x2c") - 16)] if t >= 2 else [],
            inc="xga", inc_amt=16)
        sc.dma("sync", out_own[t * 128:(t + 1) * 128, :], xtile[t % 2][:, :],
               "x2c", wait=[("xga", sc.cur("xga"))])
    sc.flush("x1_gather")
    esX.close()

    x1_full = x1_sh
    cos1 = DT("cos1", [32, S], F32, EI)
    sin1 = DT("sin1", [32, S], F32, EI)
    cos1q = DT("cos1q", [32, NOWN], F32, EI)
    sin1q = DT("sin1q", [32, NOWN], F32, EI)
    w_qkvT = DT("w_qkvT", [D, 2560], F32, EI)
    w_oT = DT("w_oT", [D, D], F32, EI)
    gate_wT = DT("gate_wT", [D, 8], F32, EI)
    w_gate = DT("w_gate", [8, D, 1408], F32, EI)
    w_up = DT("w_up", [8, D, 1408], F32, EI)
    w_down = DT("w_down", [8, 1408, D], F32, EI)
    sw_gate = DT("sw_gate", [D, 2816], F32, EI)
    sw_up = DT("sw_up", [D, 2816], F32, EI)
    sw_down = DT("sw_down", [2816, D], F32, EI)

    es = ExitStack()
    esR = ExitStack()
    T = lambda n, s, d: es.enter_context(nc.sbuf_tensor(n, s, d, allow_name_mangling=True))
    TR = lambda n, s, d: esR.enter_context(
        nc.sbuf_tensor(n, s, d, side="right", allow_name_mangling=True))

    kmax = T("kmax", [1, NCH * 4], F32)
    Km = T("Km", [1, 4 * 4], F32)
    xn1T_own = T("xn1T_own", [128, 16, NOWN], BF16)
    c_row = T("c_row", [1, 8 * NOWN], BF16)
    msk = TR("msk", [128, 4 * 8 * CHUNK], BF16)
    k1T = TR("k1T", [128, 4, S], BF16)
    v1 = TR("v1", [128, 32, 4 * 65], BF16)

    sc.dma("sync", msk.ap().rearrange("p (a q) -> p a q", q=CHUNK),
           cmask.rearrange("s r p q -> p (s r) q"), "cld")

    sc.op("vector", lambda e: e.memset(k1T[64:65, :, :], 1.0))
    sc.op("vector", lambda e: e.memset(v1[:, :, :], 0.0))
    for g in range(4):
        sc.op("vector", lambda e, g=g: e.memset(
            v1[:, :, g * 65 + 64:g * 65 + 65], 1.0), inc="vset")
    sc.flush("q0")

    NT = 2
    esL = ExitStack()
    TL = lambda n, s, d: esL.enter_context(nc.sbuf_tensor(n, s, d, allow_name_mangling=True))
    PL = lambda n, s, d: esL.enter_context(nc.psum_tensor(n, s, d, allow_name_mangling=True))
    es1 = ExitStack()
    T1 = lambda n, s, d: es1.enter_context(nc.sbuf_tensor(n, s, d, allow_name_mangling=True))
    P1s = lambda n, s, d: es1.enter_context(nc.psum_tensor(n, s, d, allow_name_mangling=True))
    sq_junk = TL("sq_junk", [128, D], BF16)
    tr1 = TL("tr1", [32, CHUNK], F32)
    tr2 = TL("tr2", [32, CHUNK], F32)
    sqbuf = TL("sqbuf", [32, 2 * CHUNK], BF16)
    x1b = TL("x1b", [32, CHUNK], F32)
    x2b = TL("x2b", [32, CHUNK], F32)
    qrow = TL("qrow", [1, 512], F32)
    pproj = [PL(f"pproj{i}", [128, CHUNK], F32) for i in range(2)]
    pnrm = PL("pnrm", [1, 512], F32)
    xc = [T1(f"xc{i}", [128, NT, D], F32) for i in range(2)]
    xntok = [T1(f"xntok{i}", [128, NT, D], BF16) for i in range(2)]
    stt = T1("stt", [128, 8], F32)
    c1c = T1("c1c", [32, CHUNK], F32)
    s1c = T1("s1c", [32, CHUNK], F32)
    wkv_sb = T1("wkv_sb", [128, 16, 512], BF16)
    xnT_c = [T1(f"xnTc{i}", [128, 16, CHUNK], BF16) for i in range(2)]
    ptp = [P1s(f"ptp{i}", [128, 128], BF16) for i in range(2)]
    pv = P1s("pvv", [128, 512], F32)
    rr_tp = RR(sc, 2)
    rr_pp = RR(sc, 2)
    rr_xc = RR(sc, 2)
    rr_pv = RR(sc, 1)
    rr_pn = RR(sc, 1)
    rr_sqb = RR(sc, 1)
    sc.dma("gpsimd", wkv_sb[:, :, :], kview(w_qkvT)[:, :, 2048:2560], "wld")

    wait_start = [("cld", sc.cur("cld")), ("wld", sc.cur("wld")),
                  ("vset", sc.cur("vset"))]
    for c in range(NCH):
        bi = c % 2
        sc.dma("sync", xc[bi][:, :, :],
               x1_full.rearrange("(n t p) d -> n p t d", t=NT, p=128)[c], "xld",
               wait=(wait_start if c == 0 else [rr_xc.use(c)]))
        emit_ln(sc, xc[bi], xntok[bi], sq_junk, stt, ("xld", sc.cur("xld")), "L", epsb)
        rr_xc.done(c, "Lxn")
        emit_tposes(sc, xntok[bi],
                    lambda f, tt, bi=bi: xnT_c[bi][:, f, tt * 128:(tt + 1) * 128],
                    idb, ptp, rr_tp, "L")
        xt_done = sc.cur("Lte")
        sc.dma("sync", c1c[:, :], cos1[:, c * CHUNK:(c + 1) * CHUNK], "cs",
               wait=[("krp", sc.cur("krp"))])
        sc.dma("sync", s1c[:, :], sin1[:, c * CHUNK:(c + 1) * CHUNK], "cs")
        for g in range(4):
            i = sc.cur("km")
            for kt in range(16):
                w = [rr_pp.use(i)] if kt == 0 else []
                if kt == 0 and g == 0:
                    w.append(("Lte", xt_done))
                sc.op("tensor", lambda e, g=g, kt=kt, i=i, bi=bi: e.matmul(
                    pproj[i % 2][0:64, :], wkv_sb[:, kt, g * 64:(g + 1) * 64],
                    xnT_c[bi][:, kt, :], start=(kt == 0), stop=(kt == 15)),
                    wait=w, inc="km" if kt == 15 else None)
            sc.op("vector", lambda e, i=i: e.tensor_copy(x1b[:, :], pproj[i % 2][0:32, :]),
                  wait=[("km", sc.cur("km")), ("krp", sc.cur("krp"))])
            sc.op("vector", lambda e, i=i: e.tensor_copy(x2b[:, :], pproj[i % 2][32:64, :]),
                  inc="kre")
            rr_pp.done(i, "kre")
            sc.op("scalar", lambda e: e.activation(
                sqbuf[:, 0:CHUNK], x1b[:, :], AF.Square),
                wait=[("kre", sc.cur("kre")), rr_sqb.use(0)], inc="ksq")
            sc.op("scalar", lambda e: e.activation(
                sqbuf[:, CHUNK:2 * CHUNK], x2b[:, :], AF.Square), inc="ksq")
            sc.op("tensor", lambda e: e.matmul(
                pnrm[0:1, 0:CHUNK], onesc[0:32, :], sqbuf[:, 0:CHUNK],
                start=True, stop=False),
                wait=[("ksq", sc.cur("ksq")), rr_pn.use(0)])
            sc.op("tensor", lambda e: e.matmul(
                pnrm[0:1, 0:CHUNK], onesc[0:32, :], sqbuf[:, CHUNK:2 * CHUNK],
                start=False, stop=True), inc="nm")
            rr_sqb.done(0, "nm")
            sc.op("vector", lambda e, g=g, c=c: e.tensor_reduce(
                kmax[0:1, c * 4 + g:c * 4 + g + 1], pnrm[0:1, 0:CHUNK],
                axis=AX.X, op=ALU.max), wait=[("nm", sc.cur("nm"))], inc="nrd")
            rr_pn.done(0, "nrd")
            emit_rope(sc, (tr1, tr2), x1b[:, :], x2b[:, :],
                      k1T[0:32, g, c * CHUNK:(c + 1) * CHUNK],
                      k1T[32:64, g, c * CHUNK:(c + 1) * CHUNK],
                      c1c[0:32, :], s1c[0:32, :], "krp",
                      wait=[("kre", sc.cur("kre")), ("cs", sc.cur("cs"))])
        for ktk in range(NT):
            for kt in range(16):
                w = [rr_pv.use(0)] if kt == 0 else []
                sc.op("tensor", lambda e, ktk=ktk, kt=kt, bi=bi: e.matmul(
                    pv[:, 0:256], xnT_c[bi][:, kt, ktk * 128:(ktk + 1) * 128],
                    wkv_sb[:, kt, 256:512], start=(kt == 0), stop=(kt == 15)),
                    wait=w, inc="vm" if kt == 15 else None)
            sc.op("vector", lambda e, ktk=ktk, c=c: e.tensor_copy(
                v1.ap().rearrange("p t (g d) -> p t g d", d=65)[:, 2 * c + ktk, :, 0:64],
                pv.ap()[:, 0:256].rearrange("p (g d) -> p g d", d=64)),
                wait=[("vm", sc.cur("vm"))], inc="ve")
            rr_pv.done(0, "ve")
    sc.flush("q1_end")
    for g in range(4):
        for s_ in range(4):
            npre = 4 * (s_ + 1)
            sc.op("vector", lambda e, g=g, s_=s_, npre=npre: e.tensor_reduce(
                Km[0:1, g * 4 + s_: g * 4 + s_ + 1],
                kmax.ap().rearrange("p (c h) -> p h c", h=4)[0:1, g, 0:npre],
                axis=AX.X, op=ALU.max), inc="kmf")
    sc.op("scalar", lambda e: e.activation(Km[0:1, :], Km[0:1, :], AF.Sqrt),
          wait=[("kmf", sc.cur("kmf"))], inc="kms")
    sc.op("vector", lambda e: e.tensor_scalar(
        Km[0:1, :], Km[0:1, :], -1.0, None, ALU.mult),
          wait=[("kms", sc.cur("kms"))], inc="kmn")
    sc.flush("q1_km")

    es2 = ExitStack()
    T2 = lambda n, s, d: es2.enter_context(nc.sbuf_tensor(n, s, d, allow_name_mangling=True))
    P2s = lambda n, s, d: es2.enter_context(nc.psum_tensor(n, s, d, allow_name_mangling=True))
    gwT_sb = T2("gwT_sb", [128, 16, 8], BF16)
    glog = T2("glog", [128, 8], F32)
    gst = T2("gst", [128, 4], F32)
    comb = T2("comb", [128, 8], F32)
    ptiny = P2s("ptiny", [1, 128], F32)
    pgl = P2s("pgl", [128, 8], F32)
    rr_pgl = RR(sc, 1)
    rr_pti = RR(sc, 1)
    sc.dma("gpsimd", gwT_sb[:, :, :], kview(gate_wT), "wld")
    wait_q1b = [("wld", sc.cur("wld"))]
    for oc in range(4):
        bi = oc % 2
        sc.dma("sync", xc[bi][:, :, :],
               out_own_rd.rearrange("(n t p) d -> n p t d", t=NT, p=128)[oc], "xld",
               wait=[rr_xc.use(oc)] + (wait_q1b if oc == 0 else []))
        emit_ln(sc, xc[bi], xntok[bi], sq_junk, stt, ("xld", sc.cur("xld")), "L", epsb)
        rr_xc.done(oc, "Lxn")
        emit_tposes(sc, xntok[bi],
                    lambda f, tt, oc=oc: xn1T_own[:, f, oc * CHUNK + tt * 128:
                                                  oc * CHUNK + (tt + 1) * 128],
                    idb, ptp, rr_tp, "L")
    sc.flush("q1b_ln")
    xt_own = sc.cur("Lte")
    for tt in range(8):
        for kt in range(16):
            w = [rr_pgl.use(0)] if kt == 0 else []
            if kt == 0 and tt == 0:
                w.append(("Lte", xt_own))
            sc.op("tensor", lambda e, tt=tt, kt=kt: e.matmul(
                pgl[:, :], xn1T_own[:, kt, tt * 128:(tt + 1) * 128],
                gwT_sb[:, kt, :], start=(kt == 0), stop=(kt == 15)),
                wait=w, inc="glm" if kt == 15 else None)
        def gch(fn, extra=(), name=None):
            w = [("gch", sc.cur("gch"))] + list(extra)
            sc.op("vector", fn, wait=w, inc=(name or "gch"))
        gch(lambda e: e.reduce_max(
            gst[:, 1:2], pgl[:, :], axis=AX.X, negate=True),
            extra=[("glm", sc.cur("glm"))], name="gmx")
        sc.op("scalar", lambda e: e.activation(
            glog[:, :], pgl[:, :], AF.Exp, bias=gst[:, 1:2]),
            wait=[("gmx", sc.cur("gmx"))], inc="gex")
        rr_pgl.done(0, "gex")
        gch(lambda e: e.reduce_sum(gst[:, 2:3], glog[:, :], axis=AX.X),
            extra=[("gex", sc.cur("gex")), ("gmx", sc.cur("gmx"))])
        gch(lambda e: e.reciprocal(gst[:, 3:4], gst[:, 2:3]))
        gch(lambda e: e.tensor_scalar(
            glog[:, :], glog[:, :], gst[:, 3:4], None, ALU.mult))
        gch(lambda e: e.reduce_max(gst[:, 0:1], glog[:, :], axis=AX.X))
        gch(lambda e: e.tensor_scalar(
            comb[:, :], glog[:, :], gst[:, 0:1], None, ALU.is_ge))
        gch(lambda e: e.tensor_scalar(
            comb[:, :], comb[:, :], -2.0, None, ALU.mult))
        gch(lambda e: e.tensor_add(comb[:, :], comb[:, :], glog[:, :]))
        gch(lambda e: e.reduce_max(gst[:, 1:2], comb[:, :], axis=AX.X))
        gch(lambda e: e.tensor_scalar(
            comb[:, :], glog[:, :], gst[:, 1:2], None, ALU.is_ge))
        gch(lambda e: e.tensor_mul(comb[:, :], comb[:, :], glog[:, :]),
            name="gcb")
        sc.op("vector", lambda e: e.engine_nop(),
              wait=[("gcb", sc.cur("gcb"))], inc="gch")
        for e_ in range(8):
            sc.op("tensor", lambda e, e_=e_: e.transpose(
                ptiny[0:1, :], comb[:, e_:e_ + 1], idf[:, :]),
                wait=[("gcb", sc.cur("gcb")), rr_pti.use(0)], inc="gtr")
            sc.op("vector", lambda e, e_=e_, tt=tt: e.tensor_copy(
                c_row[0:1, e_ * NOWN + tt * 128: e_ * NOWN + (tt + 1) * 128],
                ptiny[0:1, :]), wait=[("gtr", sc.cur("gtr"))], inc="gte")
            rr_pti.done(0, "gte")
    sc.flush("q1c")
    es2.close()
    es1.close()

    esR2 = ExitStack()
    attnT = esR2.enter_context(
        nc.sbuf_tensor("attnT_l1", [128, 16, NOWN], BF16, side="right"))
    esR3 = ExitStack()
    q1x = esR3.enter_context(
        nc.sbuf_tensor("q1x", [128, 16 * NOWN], BF16, side="right", allow_name_mangling=True))
    es2b = ExitStack()
    T2b = lambda n, s, d: es2b.enter_context(nc.sbuf_tensor(n, s, d, allow_name_mangling=True))
    P2b = lambda n, s, d: es2b.enter_context(nc.psum_tensor(n, s, d, allow_name_mangling=True))
    c1q = T2b("c1q", [32, NOWN], F32)
    s1q = T2b("s1q", [32, NOWN], F32)
    wq_sb = [T2b("wq_sb0", [128, 16, 128], BF16)]
    ptsb = [T2b(f"ptsb{i}", [128, CHUNK], BF16) for i in range(2)]
    lsb = T2b("lsb", [1, CHUNK], F32)
    rbsb = T2b("rbsb", [128, CHUNK], F32)
    psc = [P2b(f"psc{i}", [128, CHUNK], F32) for i in range(2)]
    pout = [P2b(f"pout{i}", [65, CHUNK], F32) for i in range(2)]
    pbc = P2b("pbc", [128, CHUNK], F32)
    rr_wq = RR(sc, 1)
    sc.dma("sync", c1q[:, :], cos1q, "cld")
    sc.dma("sync", s1q[:, :], sin1q, "cld")

    for grp in range(2):
        for hl in range(16):
            h = grp * 16 + hl
            g = h // 8
            bw = 0
            if hl % 2 == 0:
                sc.dma("gpsimd", wq_sb[bw][:, :, :],
                       kview(w_qkvT)[:, :, h * 64:(h + 2) * 64], "qw",
                       wait=[rr_wq.use(0)])
            for oc in range(4):
                i = sc.cur("qm")
                for kt in range(16):
                    w = [("qw", sc.cur("qw")), rr_pp.use(i)] if kt == 0 else []
                    sc.op("tensor", lambda e, hl=hl, oc=oc, kt=kt, bw=bw, i=i: e.matmul(
                        pproj[i % 2][0:64, :],
                        wq_sb[bw][:, kt, (hl % 2) * 64:(hl % 2) * 64 + 64],
                        xn1T_own[:, kt, oc * CHUNK:(oc + 1) * CHUNK],
                        start=(kt == 0), stop=(kt == 15)),
                        wait=w, inc="qm" if kt == 15 else None)
                sc.op("vector", lambda e, i=i: e.tensor_copy(x1b[:, :], pproj[i % 2][0:32, :]),
                      wait=[("qm", sc.cur("qm")), ("qrp", sc.cur("qrp"))])
                sc.op("vector", lambda e, i=i: e.tensor_copy(x2b[:, :], pproj[i % 2][32:64, :]),
                      inc="qre")
                rr_pp.done(i, "qre")
                qc = hl * NOWN + oc * CHUNK
                sc.op("scalar", lambda e: e.activation(
                    sqbuf[:, 0:CHUNK], x1b[:, :], AF.Square),
                    wait=[("qre", sc.cur("qre")), rr_sqb.use(0)], inc="qsq")
                sc.op("scalar", lambda e: e.activation(
                    sqbuf[:, CHUNK:2 * CHUNK], x2b[:, :], AF.Square), inc="qsq")
                sc.op("tensor", lambda e: e.matmul(
                    pnrm[0:1, 0:CHUNK], onesc[0:32, :], sqbuf[:, 0:CHUNK],
                    start=True, stop=False),
                    wait=[("qsq", sc.cur("qsq")), rr_pn.use(0)])
                sc.op("tensor", lambda e: e.matmul(
                    pnrm[0:1, 0:CHUNK], onesc[0:32, :], sqbuf[:, CHUNK:2 * CHUNK],
                    start=False, stop=True), inc="qnm")
                rr_sqb.done(0, "qnm")
                sc.op("scalar", lambda e: e.activation(
                    qrow[0:1, 0:CHUNK], pnrm[0:1, 0:CHUNK], AF.Sqrt),
                    wait=[("qnm", sc.cur("qnm"))], inc="qrt")
                rr_pn.done(0, "qrt")
                s_ = oc
                sc.op("vector", lambda e, g=g, s_=s_, qc=qc: e.tensor_scalar(
                    q1x[64:65, qc:qc + CHUNK], qrow[0:1, 0:CHUNK],
                    Km[0:1, g * 4 + s_:g * 4 + s_ + 1], COFF / SC1,
                    ALU.mult, ALU.add),
                    wait=[("qrt", sc.cur("qrt")), ("kmn", sc.cur("kmn"))],
                    inc="brow")
                emit_rope(sc, (tr1, tr2), x1b[:, :], x2b[:, :],
                          q1x[0:32, qc:qc + CHUNK], q1x[32:64, qc:qc + CHUNK],
                          c1q[0:32, oc * CHUNK:(oc + 1) * CHUNK],
                          s1q[0:32, oc * CHUNK:(oc + 1) * CHUNK], "qrp",
                          wait=[("qre", sc.cur("qre"))])
            if hl % 2 == 1:
                rr_wq.done(0, "qm")
        sc.flush(f"q2p_{grp}")

        def kf1(e, bank, hl, s_, t, grp=grp):
            h = grp * 16 + hl
            g = h // 8
            return e.matmul(bank[:, :], k1T[0:65, g, t * 128:(t + 1) * 128],
                            q1x[0:65, hl * NOWN + s_ * CHUNK: hl * NOWN + (s_ + 1) * CHUNK],
                            start=True, stop=True)

        def qxf1(e, dst, srcbank):
            return e.activation(dst[:, :], srcbank[:, :], AF.Exp, scale=SC1)

        def vf1(e, bank, hl, t, pb, st_, sp_, grp=grp):
            h = grp * 16 + hl
            g = h // 8
            return e.matmul(bank[:, :], v1[:, t, g * 65:(g + 1) * 65],
                            ptsb[pb][:, :], start=st_, stop=sp_)

        def outf1(e, hl, s_, ob, grp=grp):
            h = grp * 16 + hl
            hh = h % 2
            return e.tensor_mul(
                attnT[64 * hh:64 * hh + 64, h // 2, s_ * CHUNK:(s_ + 1) * CHUNK],
                pout[ob][0:64, :], rbsb[0:64, :])

        emit_attention(sc, 16, PADT, kf1, qxf1, vf1,
                       lambda s_, rt: msk[:, (s_ * 8 + rt) * CHUNK:(s_ * 8 + rt + 1) * CHUNK],
                       ptsb, psc, pout, None,
                       pbc, lsb, rbsb, onesc, ones1f, outf1, l_in_out=True)
        sc.flush(f"q2_{grp}")
    es2b.close()
    esR3.close()
    esL.close()

    es3 = ExitStack()
    T3 = lambda n, s, d: es3.enter_context(nc.sbuf_tensor(n, s, d, allow_name_mangling=True))
    P3s = lambda n, s, d: es3.enter_context(nc.psum_tensor(n, s, d, allow_name_mangling=True))
    wo_sb = [T3(f"wo_sb{i}", [128, 16, 128], BF16) for i in range(2)]
    woT = T3("woT", [128, 512], F32)
    atok = [T3(f"atok{i}", [128, D], F32) for i in range(4)]
    ptrf = [P3s(f"ptrf{i}", [128, 128], F32) for i in range(2)]
    pw3 = P3s("pw3", [128, 512], F32)
    rr_wo = RR(sc, 2)
    rr_pw = RR(sc, 1)
    rr_trf = RR(sc, 2)
    rr_woT = RR(sc, 1)
    for cc in range(2):
        atok_guard = ("rda", sc.cur("rda"))
        for ot in range(16):
            bw = ot % 2
            i = cc * 16 + ot
            sc.dma("gpsimd", wo_sb[bw][:, :, :],
                   kview(w_oT)[:, :, ot * 128:(ot + 1) * 128], "wow",
                   wait=[rr_wo.use(i)])
            for kt in range(16):
                w = [("wow", sc.cur("wow")), rr_pw.use(0)] if kt == 0 else []
                sc.op("tensor", lambda e, cc=cc, ot=ot, kt=kt, bw=bw: e.matmul(
                    pw3[:, :], wo_sb[bw][:, kt, :],
                    attnT[:, kt, cc * 512:(cc + 1) * 512],
                    start=(kt == 0), stop=(kt == 15)),
                    wait=w, inc="wm" if kt == 15 else None)
            rr_wo.done(i, "wm")
            sc.op("vector", lambda e: e.tensor_copy(woT[:, :], pw3[:, :]),
                  wait=[("wm", sc.cur("wm")), rr_woT.use(0)], inc="we")
            rr_pw.done(0, "we")
            for tt in range(4):
                i2 = sc.cur("wtr")
                w3 = [("we", sc.cur("we")), rr_trf.use(i2)]
                if ot == 0 and tt == 0:
                    w3.append(atok_guard)
                sc.op("tensor", lambda e, tt=tt, i2=i2: e.transpose(
                    ptrf[i2 % 2][:, :], woT[:, tt * 128:(tt + 1) * 128],
                    idf[:, :]), wait=w3, inc="wtr")
                sc.op("scalar", lambda e, tt=tt, ot=ot, i2=i2: e.copy(
                    atok[tt][:, ot * 128:(ot + 1) * 128], ptrf[i2 % 2][:, :]),
                    wait=[("wtr", sc.cur("wtr"))], inc="wte")
                rr_trf.done(i2, "wte")
            rr_woT.done(0, "wte")
        te_all = sc.cur("wte")
        for tt in range(4):
            sc.dma("gpsimd",
                   out_own[cc * 512 + tt * 128: cc * 512 + (tt + 1) * 128, :],
                   atok[tt][:, :], "rda",
                   wait=[("wte", te_all), ("x2c", sc.cur("x2c"))], accum_op=ALU.add)
        sc.flush(f"q3_{cc}")
    es3.close()
    esR2.close()
    esR.close()

    es4 = ExitStack()
    T4 = lambda n, s, d: es4.enter_context(nc.sbuf_tensor(n, s, d, allow_name_mangling=True))
    P4s = lambda n, s, d: es4.enter_context(nc.psum_tensor(n, s, d, allow_name_mangling=True))
    ySB = T4("ySB", [128, 16, NOWN], F32)
    es5 = ExitStack()
    T5 = lambda n, s, d: es5.enter_context(nc.sbuf_tensor(n, s, d, allow_name_mangling=True))
    P5s = lambda n, s, d: es5.enter_context(nc.psum_tensor(n, s, d, allow_name_mangling=True))
    hbig = T5("hbig", [128, 22, NOWN], BF16)
    web = [T5(f"web{i}", [128, 16, 128], BF16) for i in range(4)]
    cb = T5("cb", [128, NOWN], F32)
    gtmp = T5("gtmp", [128, 512], BF16)
    pg = P5s("pg", [128, 512], F32)
    pu = P5s("pu", [128, 512], F32)
    pcb = P5s("pcb", [128, 512], F32)
    pw4 = P5s("pw4", [128, 512], F32)
    h_e = [hbig[:, 0:11, :], hbig[:, 11:22, :]]
    rr_we = RR(sc, 2)
    rr_pg = RR(sc, 1)
    rr_pu = RR(sc, 1)
    rr_gt = RR(sc, 1)
    rr_pcb = RR(sc, 1)
    rr_pw = RR(sc, 1)
    rr_dwe = RR(sc, 2)
    gucnt = [0]
    for pair in range(4):
        for ei in range(2):
            e_ = pair * 2 + ei
            for hf in range(2):
                sc.op("tensor", lambda e, e_=e_, hf=hf: e.matmul(
                    pcb[:, :], ones1b[:, :],
                    c_row[0:1, e_ * NOWN + hf * 512: e_ * NOWN + (hf + 1) * 512],
                    start=True, stop=True),
                    wait=[rr_pcb.use(0), ("uevm", sc.cur("uevm"))], inc="cbm")
                sc.op("vector", lambda e, hf=hf: e.tensor_copy(
                    cb[:, hf * 512:(hf + 1) * 512], pcb[:, :]),
                    wait=[("cbm", sc.cur("cbm"))], inc="cbe")
                rr_pcb.done(0, "cbe")
            for j in range(11):
                bw = gucnt[0] % 2
                sc.dma("gpsimd", web[bw][:, :, :],
                       kview(w_gate[e_])[:, :, j * 128:(j + 1) * 128], "guw",
                       wait=[rr_we.use(gucnt[0])])
                sc.dma("gpsimd", web[2 + bw][:, :, :],
                       kview(w_up[e_])[:, :, j * 128:(j + 1) * 128], "guw")
                for cc in range(2):
                    for kt in range(16):
                        w = [("guw", sc.cur("guw")), rr_pg.use(0)] if kt == 0 else []
                        sc.op("tensor", lambda e, kt=kt, cc=cc, bw=bw: e.matmul(
                            pg[:, :], web[bw][:, kt, :],
                            xn1T_own[:, kt, cc * 512:(cc + 1) * 512],
                            start=(kt == 0), stop=(kt == 15)),
                            wait=w, inc="gm" if kt == 15 else None)
                    for kt in range(16):
                        w = [rr_pu.use(0)] if kt == 0 else []
                        sc.op("tensor", lambda e, kt=kt, cc=cc, bw=bw: e.matmul(
                            pu[:, :], web[2 + bw][:, kt, :],
                            xn1T_own[:, kt, cc * 512:(cc + 1) * 512],
                            start=(kt == 0), stop=(kt == 15)),
                            wait=w, inc="um" if kt == 15 else None)
                    sc.op("scalar", lambda e: e.activation(
                        gtmp[:, :], pg[:, :], AF.Silu),
                        wait=[("gm", sc.cur("gm")), rr_gt.use(0),
                              ("cbe", sc.cur("cbe"))], inc="gsl")
                    rr_pg.done(0, "gsl")
                    sc.op("vector", lambda e, cc=cc, j=j, ei=ei: e.tensor_tensor(
                        h_e[ei][:, j, cc * 512:(cc + 1) * 512], pu[:, :],
                        cb[:, cc * 512:(cc + 1) * 512], ALU.mult),
                        wait=[("um", sc.cur("um"))], inc="uevm")
                    rr_pu.done(0, "uevm")
                    sc.op("vector", lambda e, cc=cc, j=j, ei=ei: e.tensor_mul(
                        h_e[ei][:, j, cc * 512:(cc + 1) * 512], gtmp[:, :],
                        h_e[ei][:, j, cc * 512:(cc + 1) * 512]),
                        wait=[("gsl", sc.cur("gsl"))], inc="hc")
                    rr_gt.done(0, "hc")
                rr_we.done(gucnt[0], "um")
                gucnt[0] += 1
        sc.flush(f"q4g_{pair}")
        for ot in range(16):
            for ei in range(2):
                e_ = pair * 2 + ei
                sc.dma("gpsimd", web[ei][:, 0:11, :],
                       kview(w_down[e_])[:, :, ot * 128:(ot + 1) * 128], "dww",
                       wait=[rr_dwe.use(ot * 2 + ei)])
            for cc in range(2):
                for ei in range(2):
                    for kt in range(11):
                        w = []
                        if ei == 0 and kt == 0:
                            w = [("dww", sc.cur("dww")), rr_pw.use(0)]
                        sc.op("tensor", lambda e, kt=kt, cc=cc, ei=ei: e.matmul(
                            pw4[:, :], web[ei][:, kt, :],
                            h_e[ei][:, kt, cc * 512:(cc + 1) * 512],
                            start=(kt == 0 and ei == 0),
                            stop=(kt == 10 and ei == 1)),
                            wait=w, inc="dm" if (kt == 10 and ei == 1) else None)
                if pair == 0:
                    sc.op("vector", lambda e, ot=ot, cc=cc: e.tensor_copy(
                        ySB[:, ot, cc * 512:(cc + 1) * 512], pw4[:, :]),
                        wait=[("dm", sc.cur("dm"))], inc="de")
                else:
                    sc.op("vector", lambda e, ot=ot, cc=cc: e.tensor_add(
                        ySB[:, ot, cc * 512:(cc + 1) * 512],
                        ySB[:, ot, cc * 512:(cc + 1) * 512], pw4[:, :]),
                        wait=[("dm", sc.cur("dm"))], inc="de")
                rr_pw.done(0, "de")
            rr_dwe.done(ot * 2, "dm")
            rr_dwe.done(ot * 2 + 1, "dm")
        sc.flush(f"q4d_{pair}")

    for j in range(22):
        bw = j % 2
        sc.dma("gpsimd", web[bw][:, :, :],
               kview(sw_gate)[:, :, j * 128:(j + 1) * 128], "guw",
               wait=[rr_we.use(j)])
        sc.dma("gpsimd", web[2 + bw][:, :, :],
               kview(sw_up)[:, :, j * 128:(j + 1) * 128], "guw")
        for cc in range(2):
            for kt in range(16):
                w = [("guw", sc.cur("guw")), rr_pg.use(0)] if kt == 0 else []
                sc.op("tensor", lambda e, kt=kt, cc=cc, bw=bw: e.matmul(
                    pg[:, :], web[bw][:, kt, :],
                    xn1T_own[:, kt, cc * 512:(cc + 1) * 512],
                    start=(kt == 0), stop=(kt == 15)),
                    wait=w, inc="gm" if kt == 15 else None)
            for kt in range(16):
                w = [rr_pu.use(0)] if kt == 0 else []
                sc.op("tensor", lambda e, kt=kt, cc=cc, bw=bw: e.matmul(
                    pu[:, :], web[2 + bw][:, kt, :],
                    xn1T_own[:, kt, cc * 512:(cc + 1) * 512],
                    start=(kt == 0), stop=(kt == 15)),
                    wait=w, inc="um" if kt == 15 else None)
            sc.op("scalar", lambda e: e.activation(gtmp[:, :], pg[:, :], AF.Silu),
                  wait=[("gm", sc.cur("gm")), rr_gt.use(0)], inc="gsl")
            rr_pg.done(0, "gsl")
            sc.op("vector", lambda e, cc=cc, j=j: e.tensor_copy(
                hbig[:, j, cc * 512:(cc + 1) * 512], pu[:, :]),
                wait=[("um", sc.cur("um"))], inc="uevm")
            rr_pu.done(0, "uevm")
            sc.op("vector", lambda e, cc=cc, j=j: e.tensor_mul(
                hbig[:, j, cc * 512:(cc + 1) * 512], gtmp[:, :],
                hbig[:, j, cc * 512:(cc + 1) * 512]),
                wait=[("gsl", sc.cur("gsl"))], inc="hc")
            rr_gt.done(0, "hc")
        rr_we.done(j, "um")
        if j % 8 == 7:
            sc.flush(f"q4s_{j}")
    sc.flush("q4_sh")
    rr_sd1 = RR(sc, 2)
    rr_sd2 = RR(sc, 2)
    for cc in range(2):
        for ot in range(16):
            bw = (cc * 16 + ot) % 2
            i = cc * 16 + ot
            sc.dma("gpsimd", web[bw][:, :, :],
                   kview(sw_down)[:, 0:16, ot * 128:(ot + 1) * 128], "sdw",
                   wait=[rr_sd1.use(i)])
            sc.dma("gpsimd", web[2 + bw][:, 0:6, :],
                   kview(sw_down)[:, 16:22, ot * 128:(ot + 1) * 128], "sdw",
                   wait=[rr_sd2.use(i)])
            for kt in range(22):
                w = [("sdw", sc.cur("sdw")), rr_pw.use(0)] if kt == 0 else []
                wsrc = web[bw] if kt < 16 else web[2 + bw]
                ktl = kt if kt < 16 else kt - 16
                sc.op("tensor", lambda e, kt=kt, cc=cc, wsrc=wsrc, ktl=ktl: e.matmul(
                    pw4[:, :], wsrc[:, ktl, :], hbig[:, kt, cc * 512:(cc + 1) * 512],
                    start=(kt == 0), stop=(kt == 21)),
                    wait=w, inc="sm" if kt == 21 else None)
            rr_sd1.done(i, "sm")
            rr_sd2.done(i, "sm")
            sc.op("vector", lambda e, ot=ot, cc=cc: e.tensor_add(
                ySB[:, ot, cc * 512:(cc + 1) * 512],
                ySB[:, ot, cc * 512:(cc + 1) * 512], pw4[:, :]),
                wait=[("sm", sc.cur("sm"))], inc="de")
            rr_pw.done(0, "de")
    sc.flush("q4_sd")
    es5.close()
    es6 = ExitStack()
    T6 = lambda n, s, d: es6.enter_context(
        nc.sbuf_tensor(n, s, d, side="right"))
    P6s = lambda n, s, d: es6.enter_context(nc.psum_tensor(n, s, d, allow_name_mangling=True))
    atok = [T6(f"atok2_{i}", [128, D], F32) for i in range(4)]
    ptrf = [P6s(f"ptrf2_{i}", [128, 128], F32) for i in range(2)]
    rr_trf = RR(sc, 2)
    for cc in range(2):
        atok_guard = ("rda", sc.cur("rda"))
        for ot in range(16):
            for tt in range(4):
                i2 = sc.cur("ytr")
                w3 = [rr_trf.use(i2)]
                if ot == 0 and tt == 0:
                    w3.append(atok_guard)
                sc.op("tensor", lambda e, ot=ot, tt=tt, cc=cc, i2=i2: e.transpose(
                    ptrf[i2 % 2][:, :],
                    ySB[:, ot, cc * 512 + tt * 128: cc * 512 + (tt + 1) * 128],
                    idf[:, :]), wait=w3, inc="ytr")
                sc.op("scalar", lambda e, tt=tt, ot=ot, i2=i2: e.copy(
                    atok[tt][:, ot * 128:(ot + 1) * 128], ptrf[i2 % 2][:, :]),
                    wait=[("ytr", sc.cur("ytr"))], inc="yte")
                rr_trf.done(i2, "yte")
        te_all = sc.cur("yte")
        for tt in range(4):
            sc.dma("gpsimd",
                   out_own[cc * 512 + tt * 128: cc * 512 + (tt + 1) * 128, :],
                   atok[tt][:, :], "rda", wait=[("yte", te_all)],
                   accum_op=ALU.add)
        sc.flush(f"q5_{cc}")
    sc.op("gpsimd", lambda e: e.engine_nop(),
          wait=[("rda", sc.cur("rda")), ("x2c", sc.cur("x2c"))])
    sc.flush("fused_fin")
    es6.close()
    es4.close()
    es.close()
    esG.close()
    return nc




def zig_chunks(par):
    return [par, 15 - par, 2 + par, 13 - par, 4 + par, 11 - par, 6 + par, 9 - par]


def build_masks_l0(par):
    m = np.zeros((8, 4, 128, CHUNK), np.float32)
    zz = zig_chunks(par)
    kk = np.arange(128)[:, None]
    qq = np.arange(CHUNK)[None, :]
    for s in range(8):
        for rt in range(4):
            t = PADL0_ALL[s] - 4 + rt
            m[s, rt] = ((128 * t + kk) <= (CHUNK * zz[s] + qq)).astype(np.float32)
    return m.astype(mld.bfloat16)


def kernel(**inp):
    cos0, sin0 = rope_tables_L0()
    cos1, sin1 = rope_tables_L1()
    ident = np.eye(128, dtype=mld.bfloat16)
    wrq = nb(inp["L0_attn_w_rq"])
    brq = nb(inp["L0_attn_b_rq"])
    wrq_p = np.concatenate([wrq[:, h * 64 + PERM64] for h in range(16)], axis=1)
    brq_p = np.concatenate([brq[h * 64 + PERM64] for h in range(16)])
    wrk_p = nb(inp["L0_attn_w_rk"])[:, PERM64]
    brk_p = nb(inp["L0_attn_b_rk"])[PERM64]

    common = {
        "ident": ident, "cos0": cos0, "sin0": sin0,
        "cos1": cos1, "sin1": sin1,
        "w_dkv": nb(inp["L0_attn_w_dkv"]), "b_dkv": to128(inp["L0_attn_b_dkv"]),
        "w_rk": np.ascontiguousarray(wrk_p),
        "b_rk": np.ascontiguousarray(brk_p.reshape(2, 32).T),
        "w_dq": nb(inp["L0_attn_w_dq"]), "b_dq": to128(inp["L0_attn_b_dq"]),
        "w_rq": np.ascontiguousarray(wrq_p),
        "b_rq": np.ascontiguousarray(brq_p.reshape(32, 32).T),
        "w_uk": nb(inp["L0_attn_w_uk"]), "b_uk": to128(inp["L0_attn_b_uk"]),
        "w_uv": nb(inp["L0_attn_w_uv"]),
        "b_uv": nb(inp["L0_attn_b_uv"]).reshape(1, D),
        "w_uq": nb(inp["L0_attn_w_uq"]), "b_uq": to128(inp["L0_attn_b_uq"]),
        "w_o": nb(inp["L0_attn_w_o"]), "b_o": to128(inp["L0_attn_b_o"]),
        "gw": nb(inp["L0_ffn_gate_w"]), "uw": nb(inp["L0_ffn_up_w"]),
        "dw": nb(inp["L0_ffn_down_w"]),
        "gsc": to128(inp["L0_ffn_gate_scale"]),
        "usc": to128(inp["L0_ffn_up_scale"]),
        "dsc": to128(inp["L0_ffn_down_scale"]),
        "w_qkvT": np.ascontiguousarray(nb(inp["L1_attn_w_qkv"]).T),
        "w_oT": np.ascontiguousarray(nb(inp["L1_attn_w_o"]).T),
        "gate_wT": np.ascontiguousarray(nb(inp["L1_ffn_gate_w"]).T),
        "w_gate": nb(inp["L1_ffn_w_gate"]), "w_up": nb(inp["L1_ffn_w_up"]),
        "w_down": nb(inp["L1_ffn_w_down"]),
        "sw_gate": nb(inp["L1_ffn_sw_gate"]), "sw_up": nb(inp["L1_ffn_sw_up"]),
        "sw_down": nb(inp["L1_ffn_sw_down"]),
    }
    x = nb(inp["L0_attn_x"])
    tokidx = [own_token_index(j) for j in range(4)]
    masks = [build_masks(j) for j in range(4)]
    l0_tok = [np.concatenate([np.arange(z * CHUNK, (z + 1) * CHUNK)
                              for z in zig_chunks(par)]) for par in range(2)]
    cmask0s = [build_masks_l0(par) for par in range(2)]
    nonce = np.uint32(np.random.default_rng().integers(1, 2**32 - 1))
    in_maps = []
    for c in range(8):
        b, j, par = c // 4, c % 4, c % 2
        m = dict(common)
        m["x_full"] = np.ascontiguousarray(x[b])
        m["x_own"] = np.ascontiguousarray(x[b][l0_tok[par]])
        m["cmask0"] = cmask0s[par]
        m["cmask"] = masks[j]
        m["cos0q"] = np.ascontiguousarray(cos0[:, l0_tok[par]])
        m["sin0q"] = np.ascontiguousarray(sin0[:, l0_tok[par]])
        m["cos1q"] = np.ascontiguousarray(cos1[:, tokidx[j]])
        m["sin1q"] = np.ascontiguousarray(sin1[:, tokidx[j]])
        m["scat_idx"] = np.ascontiguousarray(
            l0_tok[par].astype(np.uint32).reshape(16, 128).T)
        m["gath_idx"] = np.ascontiguousarray(
            tokidx[j].astype(np.uint32).reshape(8, 128).T)
        m["flag_idx"] = np.array([[par], [par + 2]], np.uint32)
        m["nonce_in"] = np.array([[nonce], [nonce]], np.uint32)
        in_maps.append(m)
    nc = build_fused()
    res = run_bass_kernel_spmd(nc, in_maps, core_ids=list(range(8)))
    out = np.zeros((B, S, D), np.float32)
    for c in range(8):
        b, j = c // 4, c % 4
        out[b][tokidx[j]] = res.results[c]["out_own"]
    return out


def kernel_unfused(**inp):
    cos0, sin0 = rope_tables_L0()
    cos1, sin1 = rope_tables_L1()
    ident = np.eye(128, dtype=mld.bfloat16)
    wrq = nb(inp["L0_attn_w_rq"])
    brq = nb(inp["L0_attn_b_rq"])
    wrq_p = np.concatenate([wrq[:, h * 64 + PERM64] for h in range(16)], axis=1)
    brq_p = np.concatenate([brq[h * 64 + PERM64] for h in range(16)])
    wrk_p = nb(inp["L0_attn_w_rk"])[:, PERM64]
    brk_p = nb(inp["L0_attn_b_rk"])[PERM64]

    common0 = {
        "ident": ident, "cos0": cos0, "sin0": sin0,
        "w_dkv": nb(inp["L0_attn_w_dkv"]), "b_dkv": to128(inp["L0_attn_b_dkv"]),
        "w_rk": np.ascontiguousarray(wrk_p),
        "b_rk": np.ascontiguousarray(brk_p.reshape(2, 32).T),
        "w_dq": nb(inp["L0_attn_w_dq"]), "b_dq": to128(inp["L0_attn_b_dq"]),
        "w_rq": np.ascontiguousarray(wrq_p), "b_rq": np.ascontiguousarray(brq_p.reshape(32, 32).T),
        "w_uk": nb(inp["L0_attn_w_uk"]), "b_uk": to128(inp["L0_attn_b_uk"]),
        "w_uv": nb(inp["L0_attn_w_uv"]),
        "b_uv": nb(inp["L0_attn_b_uv"]).reshape(1, D),
        "w_uq": nb(inp["L0_attn_w_uq"]), "b_uq": to128(inp["L0_attn_b_uq"]),
        "w_o": nb(inp["L0_attn_w_o"]), "b_o": to128(inp["L0_attn_b_o"]),
        "gw": nb(inp["L0_ffn_gate_w"]), "uw": nb(inp["L0_ffn_up_w"]),
        "dw": nb(inp["L0_ffn_down_w"]),
        "gsc": to128(inp["L0_ffn_gate_scale"]),
        "usc": to128(inp["L0_ffn_up_scale"]),
        "dsc": to128(inp["L0_ffn_down_scale"]),
    }
    x = nb(inp["L0_attn_x"])
    tokidx = [own_token_index(j) for j in range(4)]
    masks = [build_masks(j) for j in range(4)]
    in_maps0 = []
    for c in range(8):
        b, j = c // 4, c % 4
        m = dict(common0)
        m["x_full"] = np.ascontiguousarray(x[b])
        m["x_own"] = np.ascontiguousarray(x[b][tokidx[j]])
        m["cmask"] = masks[j]
        m["cos0q"] = np.ascontiguousarray(cos0[:, tokidx[j]])
        m["sin0q"] = np.ascontiguousarray(sin0[:, tokidx[j]])
        in_maps0.append(m)
    nc0 = build_l0()
    res0 = run_bass_kernel_spmd(nc0, in_maps0, core_ids=list(range(8)))
    x1 = np.zeros((B, S, D), np.float32)
    for c in range(8):
        b, j = c // 4, c % 4
        x1[b][tokidx[j]] = res0.results[c]["x1_out"]

    common1 = {
        "ident": ident, "cos1": cos1, "sin1": sin1,
        "w_qkvT": np.ascontiguousarray(nb(inp["L1_attn_w_qkv"]).T),
        "w_oT": np.ascontiguousarray(nb(inp["L1_attn_w_o"]).T),
        "gate_wT": np.ascontiguousarray(nb(inp["L1_ffn_gate_w"]).T),
        "w_gate": nb(inp["L1_ffn_w_gate"]), "w_up": nb(inp["L1_ffn_w_up"]),
        "w_down": nb(inp["L1_ffn_w_down"]),
        "sw_gate": nb(inp["L1_ffn_sw_gate"]), "sw_up": nb(inp["L1_ffn_sw_up"]),
        "sw_down": nb(inp["L1_ffn_sw_down"]),
    }
    in_maps1 = []
    for c in range(8):
        b, j = c // 4, c % 4
        m = dict(common1)
        m["x1_full"] = np.ascontiguousarray(x1[b])
        m["x1_own"] = np.ascontiguousarray(x1[b][tokidx[j]])
        m["cmask"] = masks[j]
        m["cos1q"] = np.ascontiguousarray(cos1[:, tokidx[j]])
        m["sin1q"] = np.ascontiguousarray(sin1[:, tokidx[j]])
        in_maps1.append(m)
    nc1 = build_l1()
    res1 = run_bass_kernel_spmd(nc1, in_maps1, core_ids=list(range(8)))
    out = np.zeros((B, S, D), np.float32)
    for c in range(8):
        b, j = c // 4, c % 4
        out[b][tokidx[j]] = res1.results[c]["out_own"]
    return out


# version 07
# speedup vs baseline: 1.0490x; 1.0490x over previous
import math
import numpy as np
import ml_dtypes as mld
import concourse.bass as bass
import concourse.mybir as mybir
from concourse.bass_utils import run_bass_kernel_spmd
from contextlib import ExitStack

F32 = mybir.dt.float32
BF16 = mybir.dt.bfloat16
FP8 = mybir.dt.float8e4
AF = mybir.ActivationFunctionType
ALU = mybir.AluOpType
AX = mybir.AxisListType
U32 = mybir.dt.uint32

B, S, D = 2, 4096, 2048
NCH, CHUNK = 16, 256
NOWN = 1024
PADT = [8, 16, 24, 32]
PADL0_ALL = [4, 32, 8, 28, 12, 24, 16, 20]
PADL0_HALF = [PADL0_ALL[0:4], PADL0_ALL[4:8]]
SC0 = 1.0 / math.sqrt(192.0)
SC1 = 1.0 / math.sqrt(64.0)
COFF = 40.0
EPS = 1e-5
ENGS = ("sync", "gpsimd", "vector", "scalar", "tensor")


def own_chunks(j):
    return [j, 7 - j, 8 + j, 15 - j]


class Sched:

    def __init__(self, nc):
        self.nc = nc
        self.items = []
        self.val = {}
        self.sems = {}

    ALIAS = {"km": "pm", "kre": "pe", "krp": "rkr", "nm": "nmm", "qw": "wq",
             "qm": "pm2", "qre": "rqe", "qrp": "rqr", "wm": "wom", "we": "woe",
             "wtr": "wotr", "wte": "wote", "uevm": "uev", "cbm": "vm",
             "cbe": "ve", "sdw": "dww", "sm": "dm", "ytr": "ftr", "yte": "fte",
             "glm": "rkm", "gmx": "rke", "gex": "rsq", "gch": "Ls2",
             "gcb": "Lbv", "gtr": "Lsd", "gte": "Lxn"}

    def _n(self, name):
        return self.ALIAS.get(name, name)

    def sem(self, name):
        name = self._n(name)
        if name not in self.sems:
            self.sems[name] = self.nc.alloc_semaphore(name)
            self.val[name] = 0
        return self.sems[name]

    def op(self, eng, fn, wait=(), inc=None, inc_amt=1):
        wait = [(self._n(w[0]), w[1]) for w in wait if w is not None]
        for s, _ in wait:
            self.sem(s)
        v = None
        if inc is not None:
            inc = self._n(inc)
            self.sem(inc)
            self.val[inc] += inc_amt
            v = self.val[inc]
        self.items.append((eng, fn, tuple(wait), (inc, inc_amt) if inc else None))
        return v

    def dma(self, eng, out_ap, in_ap, inc, wait=(), **kw):
        def fn(e, o=out_ap, i=in_ap, kw=kw):
            return e.dma_start(o, i, **kw)
        return self.op(eng, fn, wait=wait, inc=inc, inc_amt=16)

    def cur(self, name):
        return self.val.get(self._n(name), 0)

    def flush(self, name=None):
        items, self.items = self.items, []
        if not items:
            return
        self._blk = getattr(self, "_blk", 0) + 1
        with self.nc.Block(f"{name or 'blk'}_b{self._blk}") as blk:
            for eng_name in ENGS:
                mine = [(fn, w, i) for (en, fn, w, i) in items if en == eng_name]
                if not mine:
                    continue

                def body(eng, mine=mine):
                    for fn, waits, inc in mine:
                        for s, v in waits:
                            if v > 0:
                                eng.wait_ge(self.sems[s], v)
                        inst = fn(eng)
                        if inc is not None:
                            inst.then_inc(self.sems[inc[0]], inc[1])
                getattr(blk, eng_name)(body)


class Box:
    def __init__(self, d, k):
        self._d = d
        self._k = k

    def __getitem__(self, sl):
        return self._d[self._k][sl]

    def __getattr__(self, a):
        return getattr(self._d[self._k], a)


class RR:

    def __init__(self, sc, n):
        self.sc = sc
        self.n = n
        self.rel = [None] * n

    def use(self, i):
        return self.rel[i % self.n]

    def done(self, i, semname):
        self.rel[i % self.n] = (semname, self.sc.cur(semname))


def emit_ln(sc, xc, xntok, sq_junk, st, wait_x, tag, epsb=None):
    NT = xc.shape[1]

    def vch(fn, extra=(), name=None):
        w = [(tag + "ch", sc.cur(tag + "ch"))] + list(extra)
        sc.op("vector", fn, wait=w, inc=(name or (tag + "ch")))

    for tt in range(NT):
        w0 = [wait_x] if tt == 0 else ()
        sc.op("vector", lambda e, tt=tt: e.reduce_sum(
            st[:, 0:1], xc[:, tt, :], axis=AX.X, negate=True),
            wait=list(w0) + [(tag + "ch", sc.cur(tag + "ch"))],
            inc=tag + "ch")
        sc.op("scalar", lambda e, tt=tt: e.activation(
            sq_junk[:, :], xc[:, tt, :], AF.Square, accum_out=st[:, 1:2]),
            wait=w0, inc=tag + "s2")
        vch(lambda e: e.tensor_scalar(
            st[:, 2:3], st[:, 0:1], 1.0 / D, None, ALU.mult),
            extra=[(tag + "s2", sc.cur(tag + "s2"))])
        vch(lambda e: e.tensor_mul(st[:, 3:4], st[:, 0:1], st[:, 2:3]))
        vch(lambda e: e.tensor_sub(st[:, 4:5], st[:, 1:2], st[:, 3:4]),
            name=tag + "bv")
        sc.op("scalar", lambda e: e.activation(
            st[:, 5:6], st[:, 4:5], AF.Sqrt, bias=epsb[:, 0:1], scale=1.0 / D),
            wait=[(tag + "bv", sc.cur(tag + "bv"))], inc=tag + "sd")
        sc.op("vector", lambda e: e.reciprocal(st[:, 6:7], st[:, 5:6]),
              wait=[(tag + "sd", sc.cur(tag + "sd")),
                    (tag + "ch", sc.cur(tag + "ch"))], inc=tag + "ch")
        vch(lambda e: e.tensor_scalar(
            st[:, 7:8], st[:, 2:3], 1.0, st[:, 6:7], ALU.mult, ALU.mult))
        vch(lambda e, tt=tt: e.tensor_scalar(
            xntok[:, tt, :], xc[:, tt, :], st[:, 6:7], st[:, 7:8],
            ALU.mult, ALU.add), name=tag + "xn")
    sc.op("vector", lambda e: e.engine_nop(),
          wait=[(tag + "xn", sc.cur(tag + "xn"))], inc=tag + "ch")


def emit_tposes(sc, xntok, dst_fn, idb, ptp, rr_tp, tag, n_f=16):
    NT = xntok.shape[1]
    xn_done = sc.cur(tag + "xn")
    n_base = sc.cur(tag + "tr")
    for tt in range(NT):
        for f in range(n_f):
            i = sc.cur(tag + "tr")
            w = [(tag + "xn", xn_done), rr_tp.use(i)]
            sc.op("tensor", lambda e, tt=tt, f=f, i=i: e.transpose(
                ptp[i % 2][:, :], xntok[:, tt, f * 128:(f + 1) * 128], idb[:, :]),
                wait=w, inc=tag + "tr")
            sc.op("scalar", lambda e, tt=tt, f=f, i=i: e.copy(
                dst_fn(f, tt), ptp[i % 2][:, :]),
                wait=[(tag + "tr", sc.cur(tag + "tr"))], inc=tag + "te")
            rr_tp.done(i, tag + "te")


def emit_rope(sc, tmp, x1b, x2b, out1, out2, cos_ap, sin_ap, tag, wait):
    t1, t2 = tmp
    sc.op("vector", lambda e: e.tensor_mul(t1[:, :], x1b, cos_ap), wait=wait)
    sc.op("vector", lambda e: e.tensor_mul(t2[:, :], x2b, sin_ap))
    sc.op("vector", lambda e: e.tensor_sub(out1, t1[:, :], t2[:, :]))
    sc.op("vector", lambda e: e.tensor_mul(t1[:, :], x1b, sin_ap))
    sc.op("vector", lambda e: e.tensor_mul(t2[:, :], x2b, cos_ap))
    sc.op("vector", lambda e: e.tensor_add(out2, t1[:, :], t2[:, :]), inc=tag)


def emit_attention(sc, nh, padt, kf_fn, qx_fn, v_fn, mask_fn, ptsb, psc, pout, pl,
                   pbc, lsb, rbsb, ones_l, ones1_f, out_fn, l_in_out,
                   head_hook=None, nmask=8, rr_sc=None, rr_pt=None, rr_slot=None):
    as_list = lambda x: x if isinstance(x, list) else [x]
    pout, pl = as_list(pout), as_list(pl)
    nbk = len(pout)
    if rr_sc is None:
        rr_sc = RR(sc, 2)
    if rr_pt is None:
        rr_pt = RR(sc, 2)
    if rr_slot is None:
        rr_slot = RR(sc, nbk)
    for h in range(nh):
        if head_hook:
            head_hook(h)
        for s_ in range(4):
            ntile = padt[s_]
            so = sc.cur("aev")
            ob = so % nbk
            rdy = {}

            def scores(t, h=h, s_=s_):
                i = sc.cur("scm")
                w = [rr_sc.use(i)]
                if head_hook:
                    w.append(head_hook(h, t, s_))
                sc.op("tensor", lambda e, t=t, h=h, s_=s_, i=i: kf_fn(
                    e, psc[i % 2], h, s_, t), wait=w, inc="scm")

            def expmask(t, s_=s_, h=h):
                i = sc.cur("scm") - 1
                j = sc.cur("ex")
                w = [("scm", sc.cur("scm")), rr_pt.use(j)]
                sc.op("scalar", lambda e, i=i, j=j: qx_fn(
                    e, ptsb[j % 2], psc[i % 2]), wait=w, inc="ex")
                rr_sc.done(i, "ex")
                if t >= padt[s_] - nmask:
                    rt = t - (padt[s_] - nmask)
                    sc.op("vector", lambda e, j=j, rt=rt, s_=s_: e.tensor_mul(
                        ptsb[j % 2][:, :], ptsb[j % 2][:, :], mask_fn(s_, rt)),
                        wait=[("ex", sc.cur("ex"))], inc="mk")
                    return j, ("mk", sc.cur("mk"))
                return j, ("ex", sc.cur("ex"))

            def pvl(t, h=h, s_=s_, ntile=ntile, so=so, ob=ob):
                j, rd = rdy[t]
                w = [rd]
                if t == 0:
                    w.append(rr_slot.use(so))
                sc.op("tensor", lambda e, t=t, h=h, j=j, ntile=ntile, ob=ob: v_fn(
                    e, pout[ob], h, t, j % 2, t == 0, t == ntile - 1),
                    wait=w, inc="pvd" if l_in_out else None)
                if not l_in_out:
                    sc.op("tensor", lambda e, t=t, j=j, ntile=ntile, ob=ob: e.matmul(
                        pl[ob][:, :], ones_l[:, :], ptsb[j % 2][:, :],
                        start=(t == 0), stop=(t == ntile - 1)), inc="pvd")
                rr_pt.done(j, "pvd")

            scores(0)
            rdy[0] = expmask(0)
            if ntile > 1:
                scores(1)
                rdy[1] = expmask(1)
            for t in range(ntile):
                pvl(t)
                if t + 2 < ntile:
                    scores(t + 2)
                    rdy[t + 2] = expmask(t + 2)
            lsrc = pout[ob][64:65, :] if l_in_out else pl[ob][0:1, :]
            sc.op("vector", lambda e, lsrc=lsrc: e.tensor_copy(lsb[0:1, :], lsrc),
                  wait=[("pvd", sc.cur("pvd"))], inc="lcp")
            sc.op("vector", lambda e: e.reciprocal(lsb[0:1, :], lsb[0:1, :]),
                  wait=[("lcp", sc.cur("lcp"))], inc="rc")
            sc.op("tensor", lambda e: e.matmul(
                pbc[:, :], ones1_f[:, :], lsb[0:1, :], start=True, stop=True),
                wait=[("rc", sc.cur("rc")), ("rbu", sc.cur("rbu"))], inc="bc")
            sc.op("vector", lambda e: e.tensor_copy(rbsb[:, :], pbc[:, :]),
                  wait=[("bc", sc.cur("bc"))], inc="rbu")
            sc.op("vector", lambda e, h=h, s_=s_, ob=ob: out_fn(e, h, s_, ob),
                  wait=[("rbu", sc.cur("rbu"))], inc="aev")
            rr_slot.done(so, "aev")


def nb(x):
    return np.ascontiguousarray(np.asarray(x, np.float32))


def to128(v):
    v = np.asarray(v, np.float32)
    return np.ascontiguousarray(v.reshape(-1, 128).T)


PERM64 = np.concatenate([np.arange(0, 64, 2), np.arange(1, 64, 2)])


def rope_tables_L0():
    pos = np.arange(S, dtype=np.float64)[:, None]
    inv = np.exp(np.arange(0, 128, 2, dtype=np.float64) * (-math.log(1e4) / 128))
    ang = pos * inv[None, :]
    return (np.ascontiguousarray(np.cos(ang)[:, :32].astype(np.float32).T),
            np.ascontiguousarray(np.sin(ang)[:, :32].astype(np.float32).T))


def rope_tables_L1():
    pos = np.arange(S, dtype=np.float64)[:, None]
    inv = 1.0 / (1e4 ** (np.arange(0, 64, 2, dtype=np.float64) / 64))
    ang = pos * inv[None, :]
    return (np.ascontiguousarray(np.cos(ang).astype(np.float32).T),
            np.ascontiguousarray(np.sin(ang).astype(np.float32).T))


def own_token_index(j):
    idx = []
    for c in own_chunks(j):
        idx.extend(range(c * CHUNK, (c + 1) * CHUNK))
    return np.asarray(idx)


def build_masks(j):
    m = np.zeros((4, 8, 128, CHUNK), np.float32)
    ch = own_chunks(j)
    kk = np.arange(128)[:, None]
    qq = np.arange(CHUNK)[None, :]
    for s in range(4):
        for rt in range(8):
            t = 8 * s + rt
            m[s, rt] = ((128 * t + kk) <= (CHUNK * ch[s] + qq)).astype(np.float32)
    return m.astype(mld.bfloat16)


def kview(w_ap, p=128):
    return w_ap.rearrange("(k p) o -> p k o", p=p)


def build_fused():
    nc = bass.Bass("TRN2", target_bir_lowering=False)
    EI, EO = "ExternalInput", "ExternalOutput"
    DT = lambda n, s, d, k="Internal": nc.dram_tensor(n, s, d, kind=k).ap()
    x1_sh = nc.dram_tensor("x1_sh", [S, D], F32, addr_space="Shared").ap()
    shf = nc.dram_tensor("shf", [4, 1], U32, addr_space="Shared").ap()
    scat_idx = DT("scat_idx", [128, 16], U32, EI)
    gath_idx = DT("gath_idx", [128, 8], U32, EI)
    flag_idx = DT("flag_idx", [2, 1], U32, EI)
    nonce_in = DT("nonce_in", [2, 1], U32, EI)
    out_own = DT("out_own", [NOWN, D], F32, EO)
    out_own_rd = out_own
    x_full = DT("x_full", [S, D], F32, EI)
    x_own = DT("x_own", [2 * NOWN, D], F32, EI)
    ident = DT("ident", [128, 128], BF16, EI)
    cmask0 = DT("cmask0", [8, 4, 128, CHUNK], BF16, EI)
    cmask = DT("cmask", [4, 8, 128, CHUNK], BF16, EI)
    cos0 = DT("cos0", [32, S], F32, EI)
    sin0 = DT("sin0", [32, S], F32, EI)
    cos0q = DT("cos0q", [32, 2 * NOWN], F32, EI)
    sin0q = DT("sin0q", [32, 2 * NOWN], F32, EI)
    w_dkv = DT("w_dkv", [D, 512], F32, EI)
    b_dkv = DT("b_dkv", [128, 4], F32, EI)
    w_rk = DT("w_rk", [D, 64], F32, EI)
    b_rk = DT("b_rk", [32, 2], F32, EI)
    w_dq = DT("w_dq", [D, 1536], F32, EI)
    b_dq = DT("b_dq", [128, 12], F32, EI)
    w_rq = DT("w_rq", [1536, 1024], F32, EI)
    b_rq = DT("b_rq", [32, 32], F32, EI)
    w_uk = DT("w_uk", [512, D], F32, EI)
    b_uk = DT("b_uk", [128, 16], F32, EI)
    w_uv = DT("w_uv", [512, D], F32, EI)
    b_uv = DT("b_uv", [1, D], F32, EI)
    w_uq = DT("w_uq", [1536, D], F32, EI)
    b_uq = DT("b_uq", [128, 16], F32, EI)
    w_o = DT("w_o", [D, D], F32, EI)
    b_o = DT("b_o", [128, 16], F32, EI)
    gw = DT("gw", [D, 6144], F32, EI)
    uw = DT("uw", [D, 6144], F32, EI)
    dw = DT("dw", [6144, D], F32, EI)
    gsc = DT("gsc", [128, 48], F32, EI)
    usc = DT("usc", [128, 48], F32, EI)
    dsc = DT("dsc", [128, 16], F32, EI)
    x1_out = DT("x1_local", [2 * NOWN, D], F32)
    k_sp = DT("k_sp", [16, 128, S], BF16)
    v_sp = DT("v_sp", [128, 32, 16, 128], BF16)

    esG = ExitStack()
    TG = lambda n, s, d: esG.enter_context(nc.sbuf_tensor(n, s, d))
    es = ExitStack()
    esR = ExitStack()
    T = lambda n, s, d: es.enter_context(nc.sbuf_tensor(n, s, d))
    TR = lambda n, s, d: esR.enter_context(nc.sbuf_tensor(n, s, d, side="right"))
    sc = Sched(nc)

    idb = TG("idb", [128, 128], BF16)
    idf = TG("idf", [128, 128], F32)
    onesc = TG("onesc", [128, 1], BF16)
    epsb = TG("epsb", [128, 1], F32)
    ones1b = TG("ones1b", [1, 128], BF16)
    ones1f = TG("ones1f", [1, 128], F32)
    kmax = T("kmax", [1, NCH * 16], F32)
    Km = T("Km", [1, 16 * 8], F32)
    bias_dkv = T("bias_dkv", [128, 4], F32)
    bias_rk = T("bias_rk", [32, 2], F32)
    bias_dq = T("bias_dq", [128, 12], F32)
    bias_rq = T("bias_rq", [32, 32], F32)
    bias_uk = T("bias_uk", [128, 16], F32)

    bias_uq = T("bias_uq", [128, 16], F32)
    bias_o = T("bias_o", [128, 16], F32)
    gsc_sb = T("gsc_sb", [128, 48], F32)
    usc_sb = T("usc_sb", [128, 48], F32)
    dsc_sb = T("dsc_sb", [128, 16], F32)
    xnT_own = T("xnT_own", [128, 16, NOWN], BF16)
    ropek = TR("ropek", [128, S], BF16)
    c0q = TR("c0q", [32, NOWN], F32)
    s0q = TR("s0q", [32, NOWN], F32)

    for dst, src in [(idb, ident), (bias_dkv, b_dkv), (bias_rk, b_rk),
                     (bias_dq, b_dq), (bias_rq, b_rq), (bias_uk, b_uk),
                     (bias_uq, b_uq), (bias_o, b_o), (gsc_sb, gsc),
                     (usc_sb, usc), (dsc_sb, dsc)]:
        sc.dma("sync", dst[:, :], src, "cld")
    sc.dma("gpsimd", idf[:, :], ident, "cld")
    sc.dma("sync", x1_out[0:NOWN, :], x_own[0:NOWN, :], "x1c")
    sc.dma("sync", x1_out[NOWN:2 * NOWN, :], x_own[NOWN:2 * NOWN, :], "x1c")
    sc.op("vector", lambda e: e.memset(onesc[:, :], 1.0))
    sc.op("vector", lambda e: e.memset(epsb[:, :], EPS))
    sc.op("vector", lambda e: e.memset(ones1b[:, :], 1.0))
    sc.op("vector", lambda e: e.memset(ones1f[:, :], 1.0))
    sc.op("vector", lambda e: e.memset(ropek[64:65, :], 1.0))
    sc.flush("p0")

    NT = 2
    esL = ExitStack()
    TL = lambda n, s, d: esL.enter_context(nc.sbuf_tensor(n, s, d))
    TLR = lambda n, s, d: esL.enter_context(nc.sbuf_tensor(n, s, d, side="right"))
    PL = lambda n, s, d: esL.enter_context(nc.psum_tensor(n, s, d))
    es1 = ExitStack()
    T1 = lambda n, s, d: es1.enter_context(nc.sbuf_tensor(n, s, d))
    _scr = {}
    c0c = T1("c0c", [32, CHUNK], F32)
    s0c = T1("s0c", [32, CHUNK], F32)
    xnT_c = [T1(f"xnTc{i}", [128, 16, CHUNK], BF16) for i in range(2)]
    lkvT_c = [T1(f"lkvTc{i}", [128, 4, CHUNK], BF16) for i in range(2)]
    kc_sb = T1("kc_sb", [128, 16, CHUNK], BF16)
    vc_sb = T1("vc_sb", [128, NT, D], BF16)
    P1p = lambda n, s, d: es1.enter_context(nc.psum_tensor(n, s, d))
    ptp = [P1p(f"ptp{i}", [128, 128], BF16) for i in range(2)]
    pproj = [P1p(f"pproj{i}", [128, CHUNK], F32) for i in range(2)]
    prk = P1p("prk", [64, CHUNK], F32)
    pv = P1p("pvv", [128, 512], F32)
    pnrm = P1p("pnrm", [1, 512], F32)
    rr_tp = RR(sc, 2)
    rr_pp = RR(sc, 2)
    rr_xc = RR(sc, 2)
    rr_lkv = RR(sc, 2)
    rr_prk = RR(sc, 1)
    rr_pv = RR(sc, 1)
    rr_pn = RR(sc, 1)
    rr_kc = RR(sc, 1)
    rr_vc = RR(sc, 1)

    for _n, _s, _d in [("sq_junk", [128, D], BF16), ("tr1", [32, CHUNK], F32),
                       ("tr2", [32, CHUNK], F32), ("x1b", [32, CHUNK], F32),
                       ("x2b", [32, CHUNK], F32), ("qrow", [1, 512], F32)]:
        _scr[_n] = es1.enter_context(nc.sbuf_tensor(_n, _s, _d, allow_name_mangling=True))
    sq_junk = Box(_scr, "sq_junk"); tr1 = Box(_scr, "tr1"); tr2 = Box(_scr, "tr2")
    x1b = Box(_scr, "x1b"); x2b = Box(_scr, "x2b"); qrow = Box(_scr, "qrow")
    sqbuf = T1("sqbuf", [128, CHUNK], BF16)
    sqrope = T1("sqrope", [64, CHUNK], BF16)
    xc = [T1(f"xc{i}", [128, NT, D], F32) for i in range(2)]
    xntok = [T1(f"xntok{i}", [128, NT, D], BF16) for i in range(2)]
    sttb = [T1("stt", [128, 8], F32)]
    bias_uv = T1("bias_uv", [1, D], BF16)
    sc.dma("gpsimd", bias_uv[:, :], b_uv, "cld")
    wdkv_sb = T1("wdkv_sb", [128, 16, 512], BF16)
    wrk_sb = T1("wrk_sb", [128, 16, 64], BF16)
    wuk_sb = T1("wuk_sb", [128, 4, D], BF16)
    wuv_sb = T1("wuv_sb", [128, 4, D], BF16)
    sc.dma("gpsimd", wdkv_sb[:, :, :], kview(w_dkv), "wld")
    sc.dma("gpsimd", wrk_sb[:, :, :], kview(w_rk), "wld")
    sc.dma("gpsimd", wuk_sb[:, :, :], kview(w_uk), "wld")
    sc.dma("gpsimd", wuv_sb[:, :, :], kview(w_uv), "wld")
    wait_start = [("cld", sc.cur("cld")), ("wld", sc.cur("wld"))]

    def l0_chunk(c, x_src, is_own, oc=None):
        bi = c % 2
        sc.dma("sync", xc[bi][:, :, :],
               x_src.rearrange("(n t p) d -> n p t d", t=NT, p=128)[c if not is_own else oc],
               "xld", wait=(wait_start if (c == 0 and not is_own) else [rr_xc.use(c)]))
        emit_ln(sc, xc[bi], xntok[bi], sq_junk, sttb[0], ("xld", sc.cur("xld")), "L", epsb)
        rr_xc.done(c, "Lxn")

    for c in range(NCH):
        bi = c % 2
        l0_chunk(c, x_full, False)
        emit_tposes(sc, xntok[bi],
                    lambda f, tt, bi=bi: xnT_c[bi][:, f, tt * 128:(tt + 1) * 128],
                    idb, ptp, rr_tp, "L")
        xt_done = sc.cur("Lte")
        for ot in range(4):
            i = sc.cur("pm")
            for kt in range(16):
                w = [rr_pp.use(i)] if kt == 0 else []
                if kt == 0 and ot == 0:
                    w.append(("Lte", xt_done))
                sc.op("tensor", lambda e, ot=ot, kt=kt, i=i, bi=bi: e.matmul(
                    pproj[i % 2][:, :], wdkv_sb[:, kt, ot * 128:(ot + 1) * 128],
                    xnT_c[bi][:, kt, :], start=(kt == 0), stop=(kt == 15)),
                    wait=w, inc="pm" if kt == 15 else None)
            lw = [("pm", sc.cur("pm")), rr_lkv.use(c)] if ot == 0 else \
                 [("pm", sc.cur("pm"))]
            sc.op("vector", lambda e, ot=ot, i=i, bi=bi: e.tensor_scalar(
                lkvT_c[bi][:, ot, :], pproj[i % 2][:, :],
                bias_dkv[:, ot:ot + 1], None, ALU.add), wait=lw, inc="pe")
            rr_pp.done(i, "pe")
        for kt in range(16):
            w = [rr_prk.use(0)] if kt == 0 else ()
            sc.op("tensor", lambda e, kt=kt, bi=bi: e.matmul(
                prk[:, :], wrk_sb[:, kt, :], xnT_c[bi][:, kt, :],
                start=(kt == 0), stop=(kt == 15)),
                wait=w, inc="rkm" if kt == 15 else None)
        sc.op("vector", lambda e: e.tensor_scalar(
            x1b[:, :], prk[0:32, :], bias_rk[:, 0:1], None, ALU.add),
            wait=[("rkm", sc.cur("rkm")), ("rkr", sc.cur("rkr"))])
        sc.op("vector", lambda e: e.tensor_scalar(
            x2b[:, :], prk[32:64, :], bias_rk[:, 1:2], None, ALU.add),
            inc="rke")
        rr_prk.done(0, "rke")
        sc.dma("sync", c0c[:, :], cos0[:, c * CHUNK:(c + 1) * CHUNK], "cs",
               wait=[("rkr", sc.cur("rkr"))])
        sc.dma("sync", s0c[:, :], sin0[:, c * CHUNK:(c + 1) * CHUNK], "cs")
        emit_rope(sc, (tr1, tr2), x1b[:, :], x2b[:, :],
                  ropek[0:32, c * CHUNK:(c + 1) * CHUNK],
                  ropek[32:64, c * CHUNK:(c + 1) * CHUNK],
                  c0c[0:32, :], s0c[0:32, :], "rkr",
                  wait=[("rke", sc.cur("rke")), ("cs", sc.cur("cs"))])
        lkv_done = sc.cur("pe")
        for h in range(16):
            i = sc.cur("pm")
            for kt in range(4):
                w = []
                if kt == 0:
                    w = [rr_pp.use(i), ("pe", lkv_done)]
                    if h == 0:
                        w.append(rr_kc.use(0))
                sc.op("tensor", lambda e, h=h, kt=kt, i=i, bi=bi: e.matmul(
                    pproj[i % 2][:, :], wuk_sb[:, kt, h * 128:(h + 1) * 128],
                    lkvT_c[bi][:, kt, :], start=(kt == 0), stop=(kt == 3)),
                    wait=w, inc="pm" if kt == 3 else None)
            sc.op("vector", lambda e, h=h, i=i: e.tensor_scalar(
                kc_sb[:, h, :], pproj[i % 2][:, :], bias_uk[:, h:h + 1],
                None, ALU.add), wait=[("pm", sc.cur("pm"))], inc="pe")
            rr_pp.done(i, "pe")
        sc.op("scalar", lambda e, c=c: e.activation(
            sqrope[:, :], ropek[0:64, c * CHUNK:(c + 1) * CHUNK], AF.Square),
            wait=[("rkr", sc.cur("rkr")), ("nmm", sc.cur("nmm"))], inc="rsq")
        for h in range(16):
            sc.op("scalar", lambda e, h=h: e.activation(
                sqbuf[:, :], kc_sb[:, h, :], AF.Square),
                wait=[("pe", sc.cur("pe")), ("nmm", sc.cur("nmm"))], inc="nsq")
            sc.op("tensor", lambda e: e.matmul(
                pnrm[0:1, 0:CHUNK], onesc[:, :], sqbuf[:, :], start=True, stop=False),
                wait=[("nsq", sc.cur("nsq")), rr_pn.use(0),
                      ("rsq", sc.cur("rsq"))])
            sc.op("tensor", lambda e: e.matmul(
                pnrm[0:1, 0:CHUNK], onesc[0:64, :], sqrope[:, :], start=False, stop=True),
                inc="nmm")
            sc.op("vector", lambda e, h=h, c=c: e.tensor_reduce(
                kmax[0:1, c * 16 + h:c * 16 + h + 1], pnrm[0:1, 0:CHUNK],
                axis=AX.X, op=ALU.max),
                wait=[("nmm", sc.cur("nmm"))], inc="nrd")
            rr_pn.done(0, "nrd")
        sc.dma("sync", k_sp.rearrange("h p (n t) -> p h n t", t=CHUNK)[:, :, c, :],
               kc_sb[:, :, :], "ksp", wait=[("nsq", sc.cur("nsq"))])
        rr_kc.done(0, "ksp")
        for g in range(4):
            for ktk in range(NT):
                for lt in range(4):
                    w = []
                    if lt == 0:
                        w = [rr_pv.use(0)]
                        if g == 0 and ktk == 0:
                            w.append(rr_vc.use(0))
                    sc.op("tensor", lambda e, g=g, ktk=ktk, lt=lt, bi=bi: e.matmul(
                        pv[:, :], lkvT_c[bi][:, lt, ktk * 128:(ktk + 1) * 128],
                        wuv_sb[:, lt, g * 512:(g + 1) * 512],
                        start=(lt == 0), stop=False), wait=w)
                sc.op("tensor", lambda e, g=g: e.matmul(
                    pv[:, :], ones1b[:, :], bias_uv[0:1, g * 512:(g + 1) * 512],
                    start=False, stop=True), inc="vm")
                sc.op("vector", lambda e, g=g, ktk=ktk: e.tensor_copy(
                    vc_sb[:, ktk, g * 512:(g + 1) * 512], pv[:, :]),
                    wait=[("vm", sc.cur("vm"))], inc="ve")
                rr_pv.done(0, "ve")
        sc.dma("sync",
               v_sp.rearrange("p t h d -> p t (h d)")[:, 2 * c:2 * c + 2, :],
               vc_sb[:, :, :], "vsp", wait=[("ve", sc.cur("ve"))])
        rr_vc.done(0, "vsp")
        rr_lkv.done(c, "ve")
    sc.flush("p1_end")
    for h in range(16):
        for s_ in range(8):
            npre = PADL0_ALL[s_] // 2
            sc.op("vector", lambda e, h=h, s_=s_, npre=npre: e.tensor_reduce(
                Km[0:1, h * 8 + s_: h * 8 + s_ + 1],
                kmax.ap().rearrange("p (c h) -> p h c", h=16)[0:1, h, 0:npre],
                axis=AX.X, op=ALU.max), inc="kmf")
    sc.op("scalar", lambda e: e.activation(Km[0:1, :], Km[0:1, :], AF.Sqrt),
          wait=[("kmf", sc.cur("kmf"))], inc="kms")
    sc.op("vector", lambda e: e.tensor_scalar(
        Km[0:1, :], Km[0:1, :], -1.0, None, ALU.mult),
          wait=[("kms", sc.cur("kms"))], inc="kmn")
    sc.flush("p1_km")
    es1.close()

    for half in range(2):
        sc.dma("sync", c0q[:, :], cos0q[:, half * NOWN:(half + 1) * NOWN], "csq")
        sc.dma("sync", s0q[:, :], sin0q[:, half * NOWN:(half + 1) * NOWN], "csq")
        esRH = ExitStack()
        TRH = lambda n, s, d: esRH.enter_context(
            nc.sbuf_tensor(n, s, d, side="right", allow_name_mangling=True))
        qx = TRH("qx", [128, 16, NOWN], BF16)
        rqx = TRH("rqx", [128, 16 * NOWN], BF16)
        es2 = ExitStack()
        T2 = lambda n, s, d: es2.enter_context(nc.sbuf_tensor(n, s, d, allow_name_mangling=True))
        P2p = lambda n, s, d: es2.enter_context(
            nc.psum_tensor(n, s, d, allow_name_mangling=True))
        ptp = [P2p(f"ptpH{i}", [128, 128], BF16) for i in range(2)]
        pproj = [P2p(f"pprojH{i}", [128, CHUNK], F32) for i in range(2)]
        pv = P2p("pvvH", [128, 512], F32)
        pnrm = P2p("pnrmH", [1, 512], F32)
        xc[0] = T2("xcH0", [128, NT, D], F32)
        xc[1] = T2("xcH1", [128, NT, D], F32)
        xntok[0] = T2("xntokH0", [128, NT, D], BF16)
        xntok[1] = T2("xntokH1", [128, NT, D], BF16)
        sttb[0] = T2("sttH", [128, 8], F32)
        for _n, _s, _d in [("sq_junk", [128, D], BF16), ("tr1", [32, CHUNK], F32),
                           ("tr2", [32, CHUNK], F32), ("x1b", [32, CHUNK], F32),
                           ("x2b", [32, CHUNK], F32), ("qrow", [1, 512], F32)]:
            _scr[_n] = es2.enter_context(nc.sbuf_tensor(_n + "H", _s, _d, allow_name_mangling=True))
        for oc in range(4):
            bi = oc % 2
            l0_chunk(oc, x_own, True, half * 4 + oc)
            emit_tposes(sc, xntok[bi],
                        lambda f, tt, oc=oc: xnT_own[:, f, oc * CHUNK + tt * 128:
                                                     oc * CHUNK + (tt + 1) * 128],
                        idb, ptp, rr_tp, "L")
        sc.flush("p1b_ln_h" + str(half))

        wq_sb = [T2(f"wq_sb{i}", [128, 16, 128], BF16) for i in range(2)]
        lqT = T2("lqT", [128, 12, NOWN], BF16)
        rr_wq = RR(sc, 2)
        xt_own = sc.cur("Lte")
        for ot in range(12):
            bw = ot % 2
            sc.dma("gpsimd", wq_sb[bw][:, :, :],
                   kview(w_dq)[:, :, ot * 128:(ot + 1) * 128], "wq", wait=[rr_wq.use(ot)])
            for cc in range(2):
                i = sc.cur("pm2")
                for kt in range(16):
                    w = []
                    if kt == 0:
                        w = [("wq", sc.cur("wq")), rr_pv.use(0)]
                        if ot == 0 and cc == 0:
                            w.append(("Lte", xt_own))
                    sc.op("tensor", lambda e, ot=ot, cc=cc, kt=kt, bw=bw: e.matmul(
                        pv[:, 0:512], wq_sb[bw][:, kt, :],
                        xnT_own[:, kt, cc * 512:(cc + 1) * 512],
                        start=(kt == 0), stop=(kt == 15)),
                        wait=w, inc="pm2" if kt == 15 else None)
                sc.op("vector", lambda e, ot=ot, cc=cc: e.tensor_scalar(
                    lqT[:, ot, cc * 512:(cc + 1) * 512], pv[:, 0:512],
                    bias_dq[:, ot:ot + 1], None, ALU.add),
                    wait=[("pm2", sc.cur("pm2"))], inc="pe2")
                rr_pv.done(0, "pe2")
            rr_wq.done(ot, "pm2")
        sc.flush("p1b_lq_h" + str(half))
        for h in range(16):
            bw = (h // 2) % 2
            if h % 2 == 0:
                sc.dma("gpsimd", wq_sb[bw][:, 0:12, :],
                       kview(w_rq)[:, :, h * 64:(h + 2) * 64], "wq",
                       wait=[rr_wq.use(h // 2)])
            for oc in range(4):
                i = sc.cur("pm2")
                for kt in range(12):
                    w = [("wq", sc.cur("wq")), rr_pp.use(i)] if kt == 0 else []
                    sc.op("tensor", lambda e, h=h, oc=oc, kt=kt, bw=bw, i=i: e.matmul(
                        pproj[i % 2][0:64, :],
                        wq_sb[bw][:, kt, (h % 2) * 64:(h % 2) * 64 + 64],
                        lqT[:, kt, oc * CHUNK:(oc + 1) * CHUNK],
                        start=(kt == 0), stop=(kt == 11)),
                        wait=w, inc="pm2" if kt == 11 else None)
                sc.op("vector", lambda e, h=h, i=i: e.tensor_scalar(
                    x1b[:, :], pproj[i % 2][0:32, :], bias_rq[:, 2 * h:2 * h + 1],
                    None, ALU.add),
                    wait=[("pm2", sc.cur("pm2")), ("rqr", sc.cur("rqr")),
                          ("csq", sc.cur("csq"))])
                sc.op("vector", lambda e, h=h, i=i: e.tensor_scalar(
                    x2b[:, :], pproj[i % 2][32:64, :], bias_rq[:, 2 * h + 1:2 * h + 2],
                    None, ALU.add), inc="rqe")
                rr_pp.done(i, "rqe")
                qc = h * NOWN + oc * CHUNK
                emit_rope(sc, (tr1, tr2), x1b[:, :], x2b[:, :],
                          rqx[0:32, qc:qc + CHUNK], rqx[32:64, qc:qc + CHUNK],
                          c0q[0:32, oc * CHUNK:(oc + 1) * CHUNK],
                          s0q[0:32, oc * CHUNK:(oc + 1) * CHUNK], "rqr",
                          wait=[("rqe", sc.cur("rqe"))])
            if h % 2 == 1:
                rr_wq.done(h // 2, "pm2")
        sc.flush("p1b_rq_h" + str(half))
        for ot in range(16):
            bw = ot % 2
            h = ot
            sc.dma("gpsimd", wq_sb[bw][:, 0:12, :],
                   kview(w_uq)[:, :, ot * 128:(ot + 1) * 128], "wq", wait=[rr_wq.use(ot)])
            for cc in range(2):
                for kt in range(12):
                    w = [("wq", sc.cur("wq")), rr_pv.use(0)] if kt == 0 else []
                    sc.op("tensor", lambda e, ot=ot, cc=cc, kt=kt, bw=bw: e.matmul(
                        pv[:, 0:512], wq_sb[bw][:, kt, :],
                        lqT[:, kt, cc * 512:(cc + 1) * 512],
                        start=(kt == 0), stop=(kt == 11)),
                        wait=w, inc="pm2" if kt == 11 else None)
                sc.op("vector", lambda e, ot=ot, cc=cc: e.tensor_scalar(
                    qx[:, ot, cc * 512:(cc + 1) * 512], pv[:, 0:512],
                    bias_uq[:, ot:ot + 1], None, ALU.add),
                    wait=[("pm2", sc.cur("pm2"))], inc="pe2")
                rr_pv.done(0, "pe2")
                qc = h * NOWN + cc * 512
                sc.op("scalar", lambda e, h=h, cc=cc: e.activation(
                    sq_junk[:, 0:512], qx[:, h, cc * 512:(cc + 1) * 512], AF.Square),
                    wait=[("pe2", sc.cur("pe2")), ("qnm", sc.cur("qnm"))], inc="qsq")
                sc.op("scalar", lambda e, qc=qc: e.activation(
                    sq_junk[0:64, 512:1024], rqx[0:64, qc:qc + 512], AF.Square),
                    inc="qsq")
                sc.op("tensor", lambda e: e.matmul(
                    pnrm[0:1, 0:512], onesc[:, :], sq_junk[:, 0:512],
                    start=True, stop=False),
                    wait=[("qsq", sc.cur("qsq")), ("qrd", sc.cur("qrd"))])
                sc.op("tensor", lambda e: e.matmul(
                    pnrm[0:1, 0:512], onesc[0:64, :], sq_junk[0:64, 512:1024],
                    start=False, stop=True), inc="qnm")
                sc.op("scalar", lambda e: e.activation(
                    qrow[0:1, :], pnrm[0:1, 0:512], AF.Sqrt),
                    wait=[("qnm", sc.cur("qnm"))], inc="qrt")
                for sh in range(2):
                    s_ = cc * 2 + sh
                    qc2 = h * NOWN + s_ * CHUNK
                    ks = h * 8 + half * 4 + s_
                    sc.op("vector", lambda e, ks=ks, qc2=qc2, sh=sh: e.tensor_scalar(
                        rqx[64:65, qc2:qc2 + CHUNK],
                        qrow[0:1, sh * CHUNK:(sh + 1) * CHUNK],
                        Km[0:1, ks:ks + 1], COFF / SC0,
                        ALU.mult, ALU.add),
                        wait=[("qrt", sc.cur("qrt"))], inc="qrd")
            rr_wq.done(ot, "pm2")
        sc.flush("p1b_uq_h" + str(half))
        es2.close()

        es25 = ExitStack()
        attnT = es25.enter_context(nc.sbuf_tensor("attnT", [128, 16, NOWN], BF16, allow_name_mangling=True))
        msk0 = es25.enter_context(nc.sbuf_tensor("msk0", [128, 8 * 4 * CHUNK], BF16, allow_name_mangling=True))
        sc.dma("sync", msk0.ap().rearrange("p (a q) -> p a q", q=CHUNK),
               cmask0.rearrange("s r p q -> p (s r) q"), "mskl")
        es3 = ExitStack()
        T3 = lambda n, s, d: es3.enter_context(nc.sbuf_tensor(n, s, d, side="right", allow_name_mangling=True))
        P3s = lambda n, s, d: es3.enter_context(nc.psum_tensor(n, s, d, allow_name_mangling=True))
        kbuf = [T3(f"kbuf{i}", [128, S], BF16) for i in range(2)]
        vbuf = [T3(f"vbuf{i}", [128, 32, 128], BF16) for i in range(2)]
        ptsb = [T3(f"ptsb{i}", [128, CHUNK], BF16) for i in range(2)]
        lsb = T3("lsb", [1, CHUNK], F32)
        rbsb = T3("rbsb", [128, CHUNK], F32)
        psc = [P3s(f"psc{i}", [128, CHUNK], F32) for i in range(2)]
        pout = [P3s(f"pout{i}", [128, CHUNK], F32) for i in range(2)]
        pl = [P3s(f"pl{i}", [1, CHUNK], F32) for i in range(2)]
        pbc = P3s("pbc", [128, CHUNK], F32)
        rr_kv = RR(sc, 2)
        rr_scB = RR(sc, 2)
        rr_ptB = RR(sc, 2)
        kvl_at = {}
        sc.op("vector", lambda e: e.engine_nop(), wait=[("mskl", sc.cur("mskl"))],
              inc="mskb")

        for h in range(16):
            hb = h % 2
            sc.dma("sync", kbuf[hb][:, :], k_sp[h, :, :], "kvl", wait=[rr_kv.use(h)])
            sc.dma("sync", vbuf[hb][:, :, :], v_sp[:, :, h, :], "kvl")
            kvl_at[h] = sc.cur("kvl")

            def kf(e, bank, hh, s_, t, hb=hb, h=h):
                e.matmul(bank[:, :], kbuf[hb][:, t * 128:(t + 1) * 128],
                         qx[:, h, s_ * CHUNK:(s_ + 1) * CHUNK],
                         start=True, stop=False)
                return e.matmul(bank[:, :], ropek[0:65, t * 128:(t + 1) * 128],
                                rqx[0:65, h * NOWN + s_ * CHUNK: h * NOWN + (s_ + 1) * CHUNK],
                                start=False, stop=True)

            def qxf(e, dst, srcbank):
                return e.activation(dst[:, :], srcbank[:, :], AF.Exp, scale=SC0)

            def vf(e, bank, hh, t, pb, st_, sp_, hb=hb):
                return e.matmul(bank[:, :], vbuf[hb][:, t, :], ptsb[pb][:, :],
                                start=st_, stop=sp_)

            def outf(e, hh, s_, ob, h=h):
                return e.tensor_mul(attnT[:, h, s_ * CHUNK:(s_ + 1) * CHUNK],
                                    pout[ob][:, :], rbsb[:, :])

            def hook(hh, t=None, s_=None, h=h):
                if t is None:
                    return None
                if s_ == 0 and t < 2:
                    return ("kvl", kvl_at[h])
                return None

            emit_attention(sc, 1, PADL0_HALF[half], kf, qxf, vf,
                           lambda s_, rt: msk0[:, ((half * 4 + s_) * 4 + rt) * CHUNK:
                                               ((half * 4 + s_) * 4 + rt + 1) * CHUNK],
                           ptsb, psc, pout, pl,
                           pbc, lsb, rbsb, onesc, ones1f, outf, l_in_out=False,
                           head_hook=hook, nmask=4, rr_sc=rr_scB, rr_pt=rr_ptB)
            rr_kv.done(h, "pvd")
        sc.flush(f"p2_h{half}")

        es3.close()
        esRH.close()
        es3b = ExitStack()
        T3b = lambda n, s, d: es3b.enter_context(nc.sbuf_tensor(n, s, d, allow_name_mangling=True))
        P3b = lambda n, s, d: es3b.enter_context(nc.psum_tensor(n, s, d, allow_name_mangling=True))
        wo_sb = [T3b(f"wo_sb{i}", [128, 16, 128], BF16) for i in range(2)]
        woT = T3b("woT", [128, 512], F32)
        atok = [T3b(f"atok{i}", [128, D], F32) for i in range(4)]
        ptrf = [P3b(f"ptrf{i}", [128, 128], F32) for i in range(2)]
        pw = P3b("pw", [128, 512], F32)
        rr_wo = RR(sc, 2)
        rr_pw = RR(sc, 1)
        rr_trf = RR(sc, 2)
        rr_woT = RR(sc, 1)

        def proj_transpose_add(sc, w_src, n_kt, rhs_fn, out_dram, tag, scale_sb=None,
                               bias_sb=None, add_wait=None, wdt=BF16, h_kt=16):
            for cc in range(2):
                atok_guard = ("rda", sc.cur("rda"))
                for ot in range(16):
                    bw = ot % 2
                    sc.dma("gpsimd", wo_sb[bw][:, 0:h_kt, :] if wdt == BF16 else None,
                           kview(w_src)[:, :, ot * 128:(ot + 1) * 128], tag + "w",
                           wait=[rr_wo.use(cc * 16 + ot)])
                    i = cc * 16 + ot
                    for kt in range(n_kt):
                        w = [(tag + "w", sc.cur(tag + "w")), rr_pw.use(0)] if kt == 0 else []
                        sc.op("tensor", lambda e, cc=cc, ot=ot, kt=kt, bw=bw: e.matmul(
                            pw[:, :], wo_sb[bw][:, kt, :], rhs_fn(kt, cc),
                            start=(kt == 0), stop=(kt == n_kt - 1)),
                            wait=w, inc=tag + "m" if kt == n_kt - 1 else None)
                    rr_wo.done(i, tag + "m")
                    if bias_sb is not None:
                        sc.op("vector", lambda e, ot=ot: e.tensor_scalar(
                            woT[:, :], pw[:, :], bias_sb[:, ot:ot + 1], None, ALU.add),
                            wait=[(tag + "m", sc.cur(tag + "m")), rr_woT.use(0)],
                            inc=tag + "e")
                    else:
                        sc.op("vector", lambda e, ot=ot: e.tensor_scalar(
                            woT[:, :], pw[:, :], scale_sb[:, ot:ot + 1], None, ALU.mult),
                            wait=[(tag + "m", sc.cur(tag + "m")), rr_woT.use(0)],
                            inc=tag + "e")
                    rr_pw.done(0, tag + "e")
                    for tt in range(4):
                        i2 = sc.cur(tag + "tr")
                        w3 = [(tag + "e", sc.cur(tag + "e")), rr_trf.use(i2)]
                        if ot == 0 and tt == 0:
                            w3.append(atok_guard)
                        sc.op("tensor", lambda e, tt=tt, i2=i2: e.transpose(
                            ptrf[i2 % 2][:, :], woT[:, tt * 128:(tt + 1) * 128],
                            idf[:, :]), wait=w3, inc=tag + "tr")
                        sc.op("scalar", lambda e, tt=tt, ot=ot, i2=i2: e.copy(
                            atok[tt][:, ot * 128:(ot + 1) * 128], ptrf[i2 % 2][:, :]),
                            wait=[(tag + "tr", sc.cur(tag + "tr"))], inc=tag + "te")
                        rr_trf.done(i2, tag + "te")
                    rr_woT.done(0, tag + "te")
                te_all = sc.cur(tag + "te")
                for tt in range(4):
                    sc.dma("gpsimd",
                           out_dram[cc * 512 + tt * 128: cc * 512 + (tt + 1) * 128, :],
                           atok[tt][:, :], "rda",
                           wait=([(tag + "te", te_all)] + (add_wait or [])),
                           accum_op=ALU.add)
                sc.flush(tag + f"_{cc}")

        proj_transpose_add(sc, w_o, 16,
                           lambda kt, cc: attnT[:, kt, cc * 512:(cc + 1) * 512],
                           x1_out[half * NOWN:(half + 1) * NOWN, :], "wo",
                       bias_sb=bias_o, add_wait=[("x1c", 32)])

        es3b.close()
        es25.close()
        es4 = ExitStack()
        T4 = lambda n, s, d: es4.enter_context(nc.sbuf_tensor(n, s, d, allow_name_mangling=True))
        P4s = lambda n, s, d: es4.enter_context(nc.psum_tensor(n, s, d, allow_name_mangling=True))
        woT = T4("woT4", [128, 512], F32)
        ptrf = [P4s(f"ptrf4_{i}", [128, 128], F32) for i in range(2)]
        pw = P4s("pw4", [128, 512], F32)
        rr_pw = RR(sc, 1)
        rr_trf = RR(sc, 2)
        rr_woT = RR(sc, 1)
        gub = [T4(f"gub{i}", [128, 16, 128], FP8) for i in range(4)]
        h_sb = T4("h_sb", [128, 48, NOWN], BF16)
        gtmp = es4.enter_context(nc.sbuf_tensor(
            "gtmp", [128, 512], BF16, side="right", allow_name_mangling=True))
        pg = P4s("pg", [128, 512], F32)
        pu = P4s("pu", [128, 512], F32)
        rr_gu = RR(sc, 2)
        rr_pg = RR(sc, 1)
        rr_pu = RR(sc, 1)
        rr_gt = RR(sc, 1)
        rr_ut = RR(sc, 1)
        for j in range(48):
            bw = j % 2
            sc.dma("gpsimd", gub[bw][:, :, :], kview(gw)[:, :, j * 128:(j + 1) * 128],
                   "guw", wait=[rr_gu.use(j)], max_dma_last_dim=4096)
            sc.dma("gpsimd", gub[2 + bw][:, :, :], kview(uw)[:, :, j * 128:(j + 1) * 128],
                   "guw", max_dma_last_dim=4096)
            for cc in range(2):
                for kt in range(16):
                    w = [("guw", sc.cur("guw")), rr_pg.use(0)] if kt == 0 else []
                    sc.op("tensor", lambda e, j=j, cc=cc, kt=kt, bw=bw: e.matmul(
                        pg[:, :], gub[bw][:, kt, :],
                        xnT_own[:, kt, cc * 512:(cc + 1) * 512],
                        start=(kt == 0), stop=(kt == 15)),
                        wait=w, inc="gm" if kt == 15 else None)
                for kt in range(16):
                    w = [rr_pu.use(0)] if kt == 0 else []
                    sc.op("tensor", lambda e, j=j, cc=cc, kt=kt, bw=bw: e.matmul(
                        pu[:, :], gub[2 + bw][:, kt, :],
                        xnT_own[:, kt, cc * 512:(cc + 1) * 512],
                        start=(kt == 0), stop=(kt == 15)),
                        wait=w, inc="um" if kt == 15 else None)
                sc.op("scalar", lambda e, j=j: e.activation(
                    gtmp[:, :], pg[:, :], AF.Silu, scale=gsc_sb[:, j:j + 1]),
                    wait=[("gm", sc.cur("gm")), rr_gt.use(0)], inc="gsl")
                rr_pg.done(0, "gsl")
                sc.op("vector", lambda e, j=j, cc=cc: e.tensor_scalar(
                    h_sb[:, j, cc * 512:(cc + 1) * 512], pu[:, :],
                    usc_sb[:, j:j + 1], None, ALU.mult),
                    wait=[("um", sc.cur("um"))], inc="uev")
                rr_pu.done(0, "uev")
                sc.op("vector", lambda e, j=j, cc=cc: e.tensor_mul(
                    h_sb[:, j, cc * 512:(cc + 1) * 512], gtmp[:, :],
                    h_sb[:, j, cc * 512:(cc + 1) * 512]),
                    wait=[("gsl", sc.cur("gsl")), ("uev", sc.cur("uev"))], inc="hc")
                rr_gt.done(0, "hc")
            rr_gu.done(j, "um")
        sc.flush("p4_gu_h" + str(half))
        dwb = [es4.enter_context(nc.sbuf_tensor(
            f"dwb{i}", [128, 48, 128], FP8, side="right",
            allow_name_mangling=True)) for i in range(2)]
        dtile = [T4(f"dtile{i}", [128, 128], F32) for i in range(2)]
        rr_dw = RR(sc, 2)
        rr_dt = RR(sc, 2)
        for ot in range(16):
            bw = ot % 2
            sc.dma("gpsimd", dwb[bw][:, :, :], kview(dw)[:, :, ot * 128:(ot + 1) * 128],
                   "dww", wait=[rr_dw.use(ot)], max_dma_last_dim=4096)
            for cc in range(2):
                for kt in range(48):
                    w = [("dww", sc.cur("dww")), rr_pw.use(0)] if kt == 0 else []
                    sc.op("tensor", lambda e, cc=cc, ot=ot, kt=kt, bw=bw: e.matmul(
                        pw[:, :], dwb[bw][:, kt, :],
                        h_sb[:, kt, cc * 512:(cc + 1) * 512],
                        start=(kt == 0), stop=(kt == 47)),
                        wait=w, inc="dm" if kt == 47 else None)
                sc.op("vector", lambda e, ot=ot: e.tensor_scalar(
                    woT[:, :], pw[:, :], dsc_sb[:, ot:ot + 1], None, ALU.mult),
                    wait=[("dm", sc.cur("dm")), rr_woT.use(0)], inc="de")
                rr_pw.done(0, "de")
                for tt in range(4):
                    i2 = sc.cur("ftr")
                    w3 = [("de", sc.cur("de")), rr_trf.use(i2)]
                    sc.op("tensor", lambda e, tt=tt, i2=i2: e.transpose(
                        ptrf[i2 % 2][:, :], woT[:, tt * 128:(tt + 1) * 128],
                        idf[:, :]), wait=w3, inc="ftr")
                    sc.op("scalar", lambda e, i2=i2: e.copy(
                        dtile[i2 % 2][:, :], ptrf[i2 % 2][:, :]),
                        wait=[("ftr", sc.cur("ftr")), rr_dt.use(i2)], inc="fte")
                    rr_trf.done(i2, "fte")
                    sc.dma("gpsimd",
                           x1_out[half * NOWN + cc * 512 + tt * 128:
                                  half * NOWN + cc * 512 + (tt + 1) * 128,
                                  ot * 128:(ot + 1) * 128],
                           dtile[i2 % 2][:, :], "rda",
                           wait=[("fte", sc.cur("fte"))], accum_op=ALU.add)
                    rr_dt.done(i2, "rda")
                rr_woT.done(0, "fte")
            rr_dw.done(ot, "dm")
        sc.flush(f"p4d_h{half}")

        es4.close()
        es25.close()

    esL.close()
    sc.op("gpsimd", lambda e: e.engine_nop(),
          wait=[("rda", sc.cur("rda")), ("x1c", 32),
                ("ksp", sc.cur("ksp")), ("vsp", sc.cur("vsp"))])
    sc.flush("l0_fin")


    esR.close()
    es.close()

    esX = ExitStack()
    TX = lambda n, s, d: esX.enter_context(nc.sbuf_tensor(n, s, d, allow_name_mangling=True))
    sidx = TX("sidx", [128, 16], U32)
    gidx = TX("gidx", [128, 8], U32)
    fidx = TX("fidx", [2, 1], U32)
    nnc = TX("nnc", [2, 1], U32)
    fpoll = TX("fpoll", [1, 2], U32)
    xtile = [TX(f"xtile{i}", [128, D], F32) for i in range(2)]
    sc.dma("sync", sidx[:, :], scat_idx, "xgl")
    sc.dma("sync", gidx[:, :], gath_idx, "xgl")
    sc.dma("sync", fidx[:, :], flag_idx, "xgl")
    sc.dma("sync", nnc[:, :], nonce_in, "xgl")
    for t in range(16):
        sc.dma("sync", xtile[t % 2][:, :],
               x1_out[t * 128:(t + 1) * 128, :], "xtl",
               wait=[("xsc", sc.cur("xsc") - 16)] if t >= 2 else [("xgl", sc.cur("xgl"))])
        sc.op("gpsimd", lambda e, t=t: e.indirect_dma_start(
            x1_sh[:, :], bass.IndirectOffsetOnAxis(sidx[:, t:t + 1], 0),
            xtile[t % 2][:, :], None),
            wait=[("xtl", sc.cur("xtl"))], inc="xsc", inc_amt=16)
    sc.op("gpsimd", lambda e: e.indirect_dma_start(
        shf[:, :], bass.IndirectOffsetOnAxis(fidx[0:2, 0:1], 0),
        nnc[:, :], None), wait=[("xsc", sc.cur("xsc"))], inc="xsc", inc_amt=16)
    sc.flush("x1_scatter")

    with nc.Block("x1_poll") as blk:
        @blk.gpsimd
        def _(g):
            with g.register("rn") as rn, g.register("done") as done, \
                 g.register("plc") as plc, g.register("f0") as f0, \
                 g.register("f1") as f1:
                g.wait_ge(sc.sem("xsc"), sc.cur("xsc"))
                g.reg_load(rn, nnc[0:1, 0:1])
                g.reg_mov(done, 0)
                g.reg_mov(plc, 0)
                with g.Fori(0, 400):
                    with g.If_ne(done, 1):
                        with g.Fori(0, 64):
                            with g.If_ne(done, 1):
                                g.dma_start(fpoll[0:1, 0:2],
                                            shf.rearrange("r c -> c (r)")[0:1, 0:2]
                                            ).then_inc(sc.sem("plf"), 16)
                                g.reg_add(plc, plc, 16)
                                g.wait_ge(sc.sem("plf"), plc)
                                g.reg_load(f0, fpoll[0:1, 0:1])
                                g.reg_load(f1, fpoll[0:1, 1:2])
                                with g.If_eq(f0, rn):
                                    with g.If_eq(f1, rn):
                                        g.reg_mov(done, 1)

    for t in range(8):
        sc.op("gpsimd", lambda e, t=t: e.indirect_dma_start(
            xtile[t % 2][:, :], None, x1_sh[:, :],
            bass.IndirectOffsetOnAxis(gidx[:, t:t + 1], 0)),
            wait=[("x2c", sc.cur("x2c") - 16)] if t >= 2 else [],
            inc="xga", inc_amt=16)
        sc.dma("sync", out_own[t * 128:(t + 1) * 128, :], xtile[t % 2][:, :],
               "x2c", wait=[("xga", sc.cur("xga"))])
    sc.flush("x1_gather")
    esX.close()

    x1_full = x1_sh
    cos1 = DT("cos1", [32, S], F32, EI)
    sin1 = DT("sin1", [32, S], F32, EI)
    cos1q = DT("cos1q", [32, NOWN], F32, EI)
    sin1q = DT("sin1q", [32, NOWN], F32, EI)
    w_qkvT = DT("w_qkvT", [D, 2560], F32, EI)
    w_oT = DT("w_oT", [D, D], F32, EI)
    gate_wT = DT("gate_wT", [D, 8], F32, EI)
    w_gate = DT("w_gate", [8, D, 1408], F32, EI)
    w_up = DT("w_up", [8, D, 1408], F32, EI)
    w_down = DT("w_down", [8, 1408, D], F32, EI)
    sw_gate = DT("sw_gate", [D, 2816], F32, EI)
    sw_up = DT("sw_up", [D, 2816], F32, EI)
    sw_down = DT("sw_down", [2816, D], F32, EI)

    es = ExitStack()
    esR = ExitStack()
    T = lambda n, s, d: es.enter_context(nc.sbuf_tensor(n, s, d, allow_name_mangling=True))
    TR = lambda n, s, d: esR.enter_context(
        nc.sbuf_tensor(n, s, d, side="right", allow_name_mangling=True))

    kmax = T("kmax", [1, NCH * 4], F32)
    Km = T("Km", [1, 4 * 4], F32)
    xn1T_own = T("xn1T_own", [128, 16, NOWN], BF16)
    c_row = T("c_row", [1, 8 * NOWN], BF16)
    msk = TR("msk", [128, 4 * 8 * CHUNK], BF16)
    k1T = TR("k1T", [128, 4, S], BF16)
    v1 = TR("v1", [128, 32, 4 * 65], BF16)

    sc.dma("sync", msk.ap().rearrange("p (a q) -> p a q", q=CHUNK),
           cmask.rearrange("s r p q -> p (s r) q"), "cld")

    sc.op("vector", lambda e: e.memset(k1T[64:65, :, :], 1.0))
    sc.op("vector", lambda e: e.memset(v1[:, :, :], 0.0))
    for g in range(4):
        sc.op("vector", lambda e, g=g: e.memset(
            v1[:, :, g * 65 + 64:g * 65 + 65], 1.0), inc="vset")
    sc.flush("q0")

    NT = 2
    esL = ExitStack()
    TL = lambda n, s, d: esL.enter_context(nc.sbuf_tensor(n, s, d, allow_name_mangling=True))
    PL = lambda n, s, d: esL.enter_context(nc.psum_tensor(n, s, d, allow_name_mangling=True))
    es1 = ExitStack()
    T1 = lambda n, s, d: es1.enter_context(nc.sbuf_tensor(n, s, d, allow_name_mangling=True))
    P1s = lambda n, s, d: es1.enter_context(nc.psum_tensor(n, s, d, allow_name_mangling=True))
    sq_junk = TL("sq_junk", [128, D], BF16)
    tr1 = TL("tr1", [32, CHUNK], F32)
    tr2 = TL("tr2", [32, CHUNK], F32)
    sqbuf = TL("sqbuf", [32, 2 * CHUNK], BF16)
    x1b = TL("x1b", [32, CHUNK], F32)
    x2b = TL("x2b", [32, CHUNK], F32)
    qrow = TL("qrow", [1, 512], F32)
    pproj = [PL(f"pproj{i}", [128, CHUNK], F32) for i in range(2)]
    pnrm = PL("pnrm", [1, 512], F32)
    xc = [T1(f"xc{i}", [128, NT, D], F32) for i in range(2)]
    xntok = [T1(f"xntok{i}", [128, NT, D], BF16) for i in range(2)]
    stt = T1("stt", [128, 8], F32)
    c1c = T1("c1c", [32, CHUNK], F32)
    s1c = T1("s1c", [32, CHUNK], F32)
    wkv_sb = T1("wkv_sb", [128, 16, 512], BF16)
    xnT_c = [T1(f"xnTc{i}", [128, 16, CHUNK], BF16) for i in range(2)]
    ptp = [P1s(f"ptp{i}", [128, 128], BF16) for i in range(2)]
    pv = P1s("pvv", [128, 512], F32)
    rr_tp = RR(sc, 2)
    rr_pp = RR(sc, 2)
    rr_xc = RR(sc, 2)
    rr_pv = RR(sc, 1)
    rr_pn = RR(sc, 1)
    rr_sqb = RR(sc, 1)
    sc.dma("gpsimd", wkv_sb[:, :, :], kview(w_qkvT)[:, :, 2048:2560], "wld")

    wait_start = [("cld", sc.cur("cld")), ("wld", sc.cur("wld")),
                  ("vset", sc.cur("vset"))]
    for c in range(NCH):
        bi = c % 2
        sc.dma("sync", xc[bi][:, :, :],
               x1_full.rearrange("(n t p) d -> n p t d", t=NT, p=128)[c], "xld",
               wait=(wait_start if c == 0 else [rr_xc.use(c)]))
        emit_ln(sc, xc[bi], xntok[bi], sq_junk, stt, ("xld", sc.cur("xld")), "L", epsb)
        rr_xc.done(c, "Lxn")
        emit_tposes(sc, xntok[bi],
                    lambda f, tt, bi=bi: xnT_c[bi][:, f, tt * 128:(tt + 1) * 128],
                    idb, ptp, rr_tp, "L")
        xt_done = sc.cur("Lte")
        sc.dma("sync", c1c[:, :], cos1[:, c * CHUNK:(c + 1) * CHUNK], "cs",
               wait=[("krp", sc.cur("krp"))])
        sc.dma("sync", s1c[:, :], sin1[:, c * CHUNK:(c + 1) * CHUNK], "cs")
        for g in range(4):
            i = sc.cur("km")
            for kt in range(16):
                w = [rr_pp.use(i)] if kt == 0 else []
                if kt == 0 and g == 0:
                    w.append(("Lte", xt_done))
                sc.op("tensor", lambda e, g=g, kt=kt, i=i, bi=bi: e.matmul(
                    pproj[i % 2][0:64, :], wkv_sb[:, kt, g * 64:(g + 1) * 64],
                    xnT_c[bi][:, kt, :], start=(kt == 0), stop=(kt == 15)),
                    wait=w, inc="km" if kt == 15 else None)
            sc.op("vector", lambda e, i=i: e.tensor_copy(x1b[:, :], pproj[i % 2][0:32, :]),
                  wait=[("km", sc.cur("km")), ("krp", sc.cur("krp"))])
            sc.op("vector", lambda e, i=i: e.tensor_copy(x2b[:, :], pproj[i % 2][32:64, :]),
                  inc="kre")
            rr_pp.done(i, "kre")
            sc.op("scalar", lambda e: e.activation(
                sqbuf[:, 0:CHUNK], x1b[:, :], AF.Square),
                wait=[("kre", sc.cur("kre")), rr_sqb.use(0)], inc="ksq")
            sc.op("scalar", lambda e: e.activation(
                sqbuf[:, CHUNK:2 * CHUNK], x2b[:, :], AF.Square), inc="ksq")
            sc.op("tensor", lambda e: e.matmul(
                pnrm[0:1, 0:CHUNK], onesc[0:32, :], sqbuf[:, 0:CHUNK],
                start=True, stop=False),
                wait=[("ksq", sc.cur("ksq")), rr_pn.use(0)])
            sc.op("tensor", lambda e: e.matmul(
                pnrm[0:1, 0:CHUNK], onesc[0:32, :], sqbuf[:, CHUNK:2 * CHUNK],
                start=False, stop=True), inc="nm")
            rr_sqb.done(0, "nm")
            sc.op("vector", lambda e, g=g, c=c: e.tensor_reduce(
                kmax[0:1, c * 4 + g:c * 4 + g + 1], pnrm[0:1, 0:CHUNK],
                axis=AX.X, op=ALU.max), wait=[("nm", sc.cur("nm"))], inc="nrd")
            rr_pn.done(0, "nrd")
            emit_rope(sc, (tr1, tr2), x1b[:, :], x2b[:, :],
                      k1T[0:32, g, c * CHUNK:(c + 1) * CHUNK],
                      k1T[32:64, g, c * CHUNK:(c + 1) * CHUNK],
                      c1c[0:32, :], s1c[0:32, :], "krp",
                      wait=[("kre", sc.cur("kre")), ("cs", sc.cur("cs"))])
        for ktk in range(NT):
            for kt in range(16):
                w = [rr_pv.use(0)] if kt == 0 else []
                sc.op("tensor", lambda e, ktk=ktk, kt=kt, bi=bi: e.matmul(
                    pv[:, 0:256], xnT_c[bi][:, kt, ktk * 128:(ktk + 1) * 128],
                    wkv_sb[:, kt, 256:512], start=(kt == 0), stop=(kt == 15)),
                    wait=w, inc="vm" if kt == 15 else None)
            sc.op("vector", lambda e, ktk=ktk, c=c: e.tensor_copy(
                v1.ap().rearrange("p t (g d) -> p t g d", d=65)[:, 2 * c + ktk, :, 0:64],
                pv.ap()[:, 0:256].rearrange("p (g d) -> p g d", d=64)),
                wait=[("vm", sc.cur("vm"))], inc="ve")
            rr_pv.done(0, "ve")
    sc.flush("q1_end")
    for g in range(4):
        for s_ in range(4):
            npre = 4 * (s_ + 1)
            sc.op("vector", lambda e, g=g, s_=s_, npre=npre: e.tensor_reduce(
                Km[0:1, g * 4 + s_: g * 4 + s_ + 1],
                kmax.ap().rearrange("p (c h) -> p h c", h=4)[0:1, g, 0:npre],
                axis=AX.X, op=ALU.max), inc="kmf")
    sc.op("scalar", lambda e: e.activation(Km[0:1, :], Km[0:1, :], AF.Sqrt),
          wait=[("kmf", sc.cur("kmf"))], inc="kms")
    sc.op("vector", lambda e: e.tensor_scalar(
        Km[0:1, :], Km[0:1, :], -1.0, None, ALU.mult),
          wait=[("kms", sc.cur("kms"))], inc="kmn")
    sc.flush("q1_km")

    es2 = ExitStack()
    T2 = lambda n, s, d: es2.enter_context(nc.sbuf_tensor(n, s, d, allow_name_mangling=True))
    P2s = lambda n, s, d: es2.enter_context(nc.psum_tensor(n, s, d, allow_name_mangling=True))
    gwT_sb = T2("gwT_sb", [128, 16, 8], BF16)
    glog = T2("glog", [128, 8], F32)
    gst = T2("gst", [128, 4], F32)
    comb = T2("comb", [128, 8], F32)
    ptiny = P2s("ptiny", [1, 128], F32)
    pgl = P2s("pgl", [128, 8], F32)
    rr_pgl = RR(sc, 1)
    rr_pti = RR(sc, 1)
    sc.dma("gpsimd", gwT_sb[:, :, :], kview(gate_wT), "wld")
    wait_q1b = [("wld", sc.cur("wld"))]
    for oc in range(4):
        bi = oc % 2
        sc.dma("sync", xc[bi][:, :, :],
               out_own_rd.rearrange("(n t p) d -> n p t d", t=NT, p=128)[oc], "xld",
               wait=[rr_xc.use(oc)] + (wait_q1b if oc == 0 else []))
        emit_ln(sc, xc[bi], xntok[bi], sq_junk, stt, ("xld", sc.cur("xld")), "L", epsb)
        rr_xc.done(oc, "Lxn")
        emit_tposes(sc, xntok[bi],
                    lambda f, tt, oc=oc: xn1T_own[:, f, oc * CHUNK + tt * 128:
                                                  oc * CHUNK + (tt + 1) * 128],
                    idb, ptp, rr_tp, "L")
    sc.flush("q1b_ln")
    xt_own = sc.cur("Lte")
    for tt in range(8):
        for kt in range(16):
            w = [rr_pgl.use(0)] if kt == 0 else []
            if kt == 0 and tt == 0:
                w.append(("Lte", xt_own))
            sc.op("tensor", lambda e, tt=tt, kt=kt: e.matmul(
                pgl[:, :], xn1T_own[:, kt, tt * 128:(tt + 1) * 128],
                gwT_sb[:, kt, :], start=(kt == 0), stop=(kt == 15)),
                wait=w, inc="glm" if kt == 15 else None)
        def gch(fn, extra=(), name=None):
            w = [("gch", sc.cur("gch"))] + list(extra)
            sc.op("vector", fn, wait=w, inc=(name or "gch"))
        gch(lambda e: e.reduce_max(
            gst[:, 1:2], pgl[:, :], axis=AX.X, negate=True),
            extra=[("glm", sc.cur("glm"))], name="gmx")
        sc.op("scalar", lambda e: e.activation(
            glog[:, :], pgl[:, :], AF.Exp, bias=gst[:, 1:2]),
            wait=[("gmx", sc.cur("gmx"))], inc="gex")
        rr_pgl.done(0, "gex")
        gch(lambda e: e.reduce_sum(gst[:, 2:3], glog[:, :], axis=AX.X),
            extra=[("gex", sc.cur("gex")), ("gmx", sc.cur("gmx"))])
        gch(lambda e: e.reciprocal(gst[:, 3:4], gst[:, 2:3]))
        gch(lambda e: e.tensor_scalar(
            glog[:, :], glog[:, :], gst[:, 3:4], None, ALU.mult))
        gch(lambda e: e.reduce_max(gst[:, 0:1], glog[:, :], axis=AX.X))
        gch(lambda e: e.tensor_scalar(
            comb[:, :], glog[:, :], gst[:, 0:1], None, ALU.is_ge))
        gch(lambda e: e.tensor_scalar(
            comb[:, :], comb[:, :], -2.0, None, ALU.mult))
        gch(lambda e: e.tensor_add(comb[:, :], comb[:, :], glog[:, :]))
        gch(lambda e: e.reduce_max(gst[:, 1:2], comb[:, :], axis=AX.X))
        gch(lambda e: e.tensor_scalar(
            comb[:, :], glog[:, :], gst[:, 1:2], None, ALU.is_ge))
        gch(lambda e: e.tensor_mul(comb[:, :], comb[:, :], glog[:, :]),
            name="gcb")
        sc.op("vector", lambda e: e.engine_nop(),
              wait=[("gcb", sc.cur("gcb"))], inc="gch")
        for e_ in range(8):
            sc.op("tensor", lambda e, e_=e_: e.transpose(
                ptiny[0:1, :], comb[:, e_:e_ + 1], idf[:, :]),
                wait=[("gcb", sc.cur("gcb")), rr_pti.use(0)], inc="gtr")
            sc.op("vector", lambda e, e_=e_, tt=tt: e.tensor_copy(
                c_row[0:1, e_ * NOWN + tt * 128: e_ * NOWN + (tt + 1) * 128],
                ptiny[0:1, :]), wait=[("gtr", sc.cur("gtr"))], inc="gte")
            rr_pti.done(0, "gte")
    sc.flush("q1c")
    es2.close()
    es1.close()

    esR2 = ExitStack()
    attnT = esR2.enter_context(
        nc.sbuf_tensor("attnT_l1", [128, 16, NOWN], BF16, side="right"))
    esR3 = ExitStack()
    q1x = esR3.enter_context(
        nc.sbuf_tensor("q1x", [128, 16 * NOWN], BF16, side="right", allow_name_mangling=True))
    es2b = ExitStack()
    T2b = lambda n, s, d: es2b.enter_context(nc.sbuf_tensor(n, s, d, allow_name_mangling=True))
    P2b = lambda n, s, d: es2b.enter_context(nc.psum_tensor(n, s, d, allow_name_mangling=True))
    c1q = T2b("c1q", [32, NOWN], F32)
    s1q = T2b("s1q", [32, NOWN], F32)
    wq_sb = [T2b("wq_sb0", [128, 16, 128], BF16)]
    ptsb = [T2b(f"ptsb{i}", [128, CHUNK], BF16) for i in range(2)]
    lsb = T2b("lsb", [1, CHUNK], F32)
    rbsb = T2b("rbsb", [128, CHUNK], F32)
    psc = [P2b(f"psc{i}", [128, CHUNK], F32) for i in range(2)]
    pout = [P2b(f"pout{i}", [65, CHUNK], F32) for i in range(2)]
    pbc = P2b("pbc", [128, CHUNK], F32)
    rr_wq = RR(sc, 1)
    sc.dma("sync", c1q[:, :], cos1q, "cld")
    sc.dma("sync", s1q[:, :], sin1q, "cld")

    def q2_qproj(gh):
        g = gh // 8
        hl = gh % 16
        bw = 0
        if gh % 2 == 0:
            sc.dma("gpsimd", wq_sb[bw][:, :, :],
                   kview(w_qkvT)[:, :, gh * 64:(gh + 2) * 64], "qw",
                   wait=[rr_wq.use(0)])
        for oc in range(4):
            i = sc.cur("qm")
            for kt in range(16):
                w = [("qw", sc.cur("qw")), rr_pp.use(i)] if kt == 0 else []
                sc.op("tensor", lambda e, gh=gh, oc=oc, kt=kt, bw=bw, i=i: e.matmul(
                    pproj[i % 2][0:64, :],
                    wq_sb[bw][:, kt, (gh % 2) * 64:(gh % 2) * 64 + 64],
                    xn1T_own[:, kt, oc * CHUNK:(oc + 1) * CHUNK],
                    start=(kt == 0), stop=(kt == 15)),
                    wait=w, inc="qm" if kt == 15 else None)
            sc.op("vector", lambda e, i=i: e.tensor_copy(x1b[:, :], pproj[i % 2][0:32, :]),
                  wait=[("qm", sc.cur("qm")), ("qrp", sc.cur("qrp"))])
            sc.op("vector", lambda e, i=i: e.tensor_copy(x2b[:, :], pproj[i % 2][32:64, :]),
                  inc="qre")
            rr_pp.done(i, "qre")
            qc = hl * NOWN + oc * CHUNK
            sc.op("scalar", lambda e: e.activation(
                sqbuf[:, 0:CHUNK], x1b[:, :], AF.Square),
                wait=[("qre", sc.cur("qre")), rr_sqb.use(0)], inc="qsq")
            sc.op("scalar", lambda e: e.activation(
                sqbuf[:, CHUNK:2 * CHUNK], x2b[:, :], AF.Square), inc="qsq")
            sc.op("tensor", lambda e: e.matmul(
                pnrm[0:1, 0:CHUNK], onesc[0:32, :], sqbuf[:, 0:CHUNK],
                start=True, stop=False),
                wait=[("qsq", sc.cur("qsq")), rr_pn.use(0)])
            sc.op("tensor", lambda e: e.matmul(
                pnrm[0:1, 0:CHUNK], onesc[0:32, :], sqbuf[:, CHUNK:2 * CHUNK],
                start=False, stop=True), inc="qnm")
            rr_sqb.done(0, "qnm")
            sc.op("scalar", lambda e: e.activation(
                qrow[0:1, 0:CHUNK], pnrm[0:1, 0:CHUNK], AF.Sqrt),
                wait=[("qnm", sc.cur("qnm"))], inc="qrt")
            rr_pn.done(0, "qrt")
            s_ = oc
            sc.op("vector", lambda e, g=g, s_=s_, qc=qc: e.tensor_scalar(
                q1x[64:65, qc:qc + CHUNK], qrow[0:1, 0:CHUNK],
                Km[0:1, g * 4 + s_:g * 4 + s_ + 1], COFF / SC1,
                ALU.mult, ALU.add),
                wait=[("qrt", sc.cur("qrt")), ("kmn", sc.cur("kmn"))],
                inc="brow")
            emit_rope(sc, (tr1, tr2), x1b[:, :], x2b[:, :],
                      q1x[0:32, qc:qc + CHUNK], q1x[32:64, qc:qc + CHUNK],
                      c1q[0:32, oc * CHUNK:(oc + 1) * CHUNK],
                      s1q[0:32, oc * CHUNK:(oc + 1) * CHUNK], "qrp",
                      wait=[("qre", sc.cur("qre"))])
        if gh % 2 == 1:
            rr_wq.done(0, "qm")

    def qxf1(e, dst, srcbank):
        return e.activation(dst[:, :], srcbank[:, :], AF.Exp, scale=SC1)

    rr_scQ, rr_ptQ, rr_slotQ = RR(sc, 2), RR(sc, 2), RR(sc, 2)

    def q2_att(gh):
        g = gh // 8
        hl = gh % 16
        hh = gh % 2

        def kf1(e, bank, _h, s_, t):
            return e.matmul(bank[:, :], k1T[0:65, g, t * 128:(t + 1) * 128],
                            q1x[0:65, hl * NOWN + s_ * CHUNK:
                                hl * NOWN + (s_ + 1) * CHUNK],
                            start=True, stop=True)

        def vf1(e, bank, _h, t, pb, st_, sp_):
            return e.matmul(bank[:, :], v1[:, t, g * 65:(g + 1) * 65],
                            ptsb[pb][:, :], start=st_, stop=sp_)

        def outf1(e, _h, s_, ob):
            return e.tensor_mul(
                attnT[64 * hh:64 * hh + 64, gh // 2, s_ * CHUNK:(s_ + 1) * CHUNK],
                pout[ob][0:64, :], rbsb[0:64, :])

        emit_attention(sc, 1, PADT, kf1, qxf1, vf1,
                       lambda s_, rt: msk[:, (s_ * 8 + rt) * CHUNK:
                                          (s_ * 8 + rt + 1) * CHUNK],
                       ptsb, psc, pout, None,
                       pbc, lsb, rbsb, onesc, ones1f, outf1, l_in_out=True,
                       rr_sc=rr_scQ, rr_pt=rr_ptQ, rr_slot=rr_slotQ)

    for gh in range(32):
        q2_qproj(gh)
        if gh >= 1:
            q2_att(gh - 1)
    q2_att(31)
    sc.flush("q2")
    es2b.close()
    esR3.close()
    esL.close()

    es3 = ExitStack()
    T3 = lambda n, s, d: es3.enter_context(nc.sbuf_tensor(n, s, d, allow_name_mangling=True))
    P3s = lambda n, s, d: es3.enter_context(nc.psum_tensor(n, s, d, allow_name_mangling=True))
    wo_sb = [T3(f"wo_sb{i}", [128, 16, 128], BF16) for i in range(2)]
    woT = T3("woT", [128, 512], F32)
    atok = [T3(f"atok{i}", [128, D], F32) for i in range(4)]
    ptrf = [P3s(f"ptrf{i}", [128, 128], F32) for i in range(2)]
    pw3 = P3s("pw3", [128, 512], F32)
    rr_wo = RR(sc, 2)
    rr_pw = RR(sc, 1)
    rr_trf = RR(sc, 2)
    rr_woT = RR(sc, 1)
    for cc in range(2):
        atok_guard = ("rda", sc.cur("rda"))
        for ot in range(16):
            bw = ot % 2
            i = cc * 16 + ot
            sc.dma("gpsimd", wo_sb[bw][:, :, :],
                   kview(w_oT)[:, :, ot * 128:(ot + 1) * 128], "wow",
                   wait=[rr_wo.use(i)])
            for kt in range(16):
                w = [("wow", sc.cur("wow")), rr_pw.use(0)] if kt == 0 else []
                sc.op("tensor", lambda e, cc=cc, ot=ot, kt=kt, bw=bw: e.matmul(
                    pw3[:, :], wo_sb[bw][:, kt, :],
                    attnT[:, kt, cc * 512:(cc + 1) * 512],
                    start=(kt == 0), stop=(kt == 15)),
                    wait=w, inc="wm" if kt == 15 else None)
            rr_wo.done(i, "wm")
            sc.op("vector", lambda e: e.tensor_copy(woT[:, :], pw3[:, :]),
                  wait=[("wm", sc.cur("wm")), rr_woT.use(0)], inc="we")
            rr_pw.done(0, "we")
            for tt in range(4):
                i2 = sc.cur("wtr")
                w3 = [("we", sc.cur("we")), rr_trf.use(i2)]
                if ot == 0 and tt == 0:
                    w3.append(atok_guard)
                sc.op("tensor", lambda e, tt=tt, i2=i2: e.transpose(
                    ptrf[i2 % 2][:, :], woT[:, tt * 128:(tt + 1) * 128],
                    idf[:, :]), wait=w3, inc="wtr")
                sc.op("scalar", lambda e, tt=tt, ot=ot, i2=i2: e.copy(
                    atok[tt][:, ot * 128:(ot + 1) * 128], ptrf[i2 % 2][:, :]),
                    wait=[("wtr", sc.cur("wtr"))], inc="wte")
                rr_trf.done(i2, "wte")
            rr_woT.done(0, "wte")
        te_all = sc.cur("wte")
        for tt in range(4):
            sc.dma("gpsimd",
                   out_own[cc * 512 + tt * 128: cc * 512 + (tt + 1) * 128, :],
                   atok[tt][:, :], "rda",
                   wait=[("wte", te_all), ("x2c", sc.cur("x2c"))], accum_op=ALU.add)
        sc.flush(f"q3_{cc}")
    es3.close()
    esR2.close()
    esR.close()

    es4 = ExitStack()
    T4 = lambda n, s, d: es4.enter_context(nc.sbuf_tensor(n, s, d, allow_name_mangling=True))
    P4s = lambda n, s, d: es4.enter_context(nc.psum_tensor(n, s, d, allow_name_mangling=True))
    ySB = T4("ySB", [128, 16, NOWN], F32)
    es5 = ExitStack()
    T5 = lambda n, s, d: es5.enter_context(nc.sbuf_tensor(n, s, d, allow_name_mangling=True))
    P5s = lambda n, s, d: es5.enter_context(nc.psum_tensor(n, s, d, allow_name_mangling=True))
    hbig = T5("hbig", [128, 22, NOWN], BF16)
    web = [T5(f"web{i}", [128, 16, 128], BF16) for i in range(4)]
    cb = T5("cb", [128, NOWN], F32)
    gtmp = T5("gtmp", [128, 512], BF16)
    pg = P5s("pg", [128, 512], F32)
    pu = P5s("pu", [128, 512], F32)
    pcb = P5s("pcb", [128, 512], F32)
    pw4 = P5s("pw4", [128, 512], F32)
    h_e = [hbig[:, 0:11, :], hbig[:, 11:22, :]]
    rr_we = RR(sc, 2)
    rr_pg = RR(sc, 1)
    rr_pu = RR(sc, 1)
    rr_gt = RR(sc, 1)
    rr_pcb = RR(sc, 1)
    rr_pw = RR(sc, 1)
    rr_dwe = RR(sc, 2)
    gucnt = [0]
    for pair in range(4):
        for ei in range(2):
            e_ = pair * 2 + ei
            for hf in range(2):
                sc.op("tensor", lambda e, e_=e_, hf=hf: e.matmul(
                    pcb[:, :], ones1b[:, :],
                    c_row[0:1, e_ * NOWN + hf * 512: e_ * NOWN + (hf + 1) * 512],
                    start=True, stop=True),
                    wait=[rr_pcb.use(0), ("uevm", sc.cur("uevm"))], inc="cbm")
                sc.op("vector", lambda e, hf=hf: e.tensor_copy(
                    cb[:, hf * 512:(hf + 1) * 512], pcb[:, :]),
                    wait=[("cbm", sc.cur("cbm"))], inc="cbe")
                rr_pcb.done(0, "cbe")
            for j in range(11):
                bw = gucnt[0] % 2
                sc.dma("gpsimd", web[bw][:, :, :],
                       kview(w_gate[e_])[:, :, j * 128:(j + 1) * 128], "guw",
                       wait=[rr_we.use(gucnt[0])])
                sc.dma("gpsimd", web[2 + bw][:, :, :],
                       kview(w_up[e_])[:, :, j * 128:(j + 1) * 128], "guw")
                for cc in range(2):
                    for kt in range(16):
                        w = [("guw", sc.cur("guw")), rr_pg.use(0)] if kt == 0 else []
                        sc.op("tensor", lambda e, kt=kt, cc=cc, bw=bw: e.matmul(
                            pg[:, :], web[bw][:, kt, :],
                            xn1T_own[:, kt, cc * 512:(cc + 1) * 512],
                            start=(kt == 0), stop=(kt == 15)),
                            wait=w, inc="gm" if kt == 15 else None)
                    for kt in range(16):
                        w = [rr_pu.use(0)] if kt == 0 else []
                        sc.op("tensor", lambda e, kt=kt, cc=cc, bw=bw: e.matmul(
                            pu[:, :], web[2 + bw][:, kt, :],
                            xn1T_own[:, kt, cc * 512:(cc + 1) * 512],
                            start=(kt == 0), stop=(kt == 15)),
                            wait=w, inc="um" if kt == 15 else None)
                    sc.op("scalar", lambda e: e.activation(
                        gtmp[:, :], pg[:, :], AF.Silu),
                        wait=[("gm", sc.cur("gm")), rr_gt.use(0),
                              ("cbe", sc.cur("cbe"))], inc="gsl")
                    rr_pg.done(0, "gsl")
                    sc.op("vector", lambda e, cc=cc, j=j, ei=ei: e.tensor_tensor(
                        h_e[ei][:, j, cc * 512:(cc + 1) * 512], pu[:, :],
                        cb[:, cc * 512:(cc + 1) * 512], ALU.mult),
                        wait=[("um", sc.cur("um"))], inc="uevm")
                    rr_pu.done(0, "uevm")
                    sc.op("vector", lambda e, cc=cc, j=j, ei=ei: e.tensor_mul(
                        h_e[ei][:, j, cc * 512:(cc + 1) * 512], gtmp[:, :],
                        h_e[ei][:, j, cc * 512:(cc + 1) * 512]),
                        wait=[("gsl", sc.cur("gsl"))], inc="hc")
                    rr_gt.done(0, "hc")
                rr_we.done(gucnt[0], "um")
                gucnt[0] += 1
        sc.flush(f"q4g_{pair}")
        for ot in range(16):
            for ei in range(2):
                e_ = pair * 2 + ei
                sc.dma("gpsimd", web[ei][:, 0:11, :],
                       kview(w_down[e_])[:, :, ot * 128:(ot + 1) * 128], "dww",
                       wait=[rr_dwe.use(ot * 2 + ei)])
            for cc in range(2):
                for ei in range(2):
                    for kt in range(11):
                        w = []
                        if ei == 0 and kt == 0:
                            w = [("dww", sc.cur("dww")), rr_pw.use(0)]
                        sc.op("tensor", lambda e, kt=kt, cc=cc, ei=ei: e.matmul(
                            pw4[:, :], web[ei][:, kt, :],
                            h_e[ei][:, kt, cc * 512:(cc + 1) * 512],
                            start=(kt == 0 and ei == 0),
                            stop=(kt == 10 and ei == 1)),
                            wait=w, inc="dm" if (kt == 10 and ei == 1) else None)
                if pair == 0:
                    sc.op("vector", lambda e, ot=ot, cc=cc: e.tensor_copy(
                        ySB[:, ot, cc * 512:(cc + 1) * 512], pw4[:, :]),
                        wait=[("dm", sc.cur("dm"))], inc="de")
                else:
                    sc.op("vector", lambda e, ot=ot, cc=cc: e.tensor_add(
                        ySB[:, ot, cc * 512:(cc + 1) * 512],
                        ySB[:, ot, cc * 512:(cc + 1) * 512], pw4[:, :]),
                        wait=[("dm", sc.cur("dm"))], inc="de")
                rr_pw.done(0, "de")
            rr_dwe.done(ot * 2, "dm")
            rr_dwe.done(ot * 2 + 1, "dm")
        sc.flush(f"q4d_{pair}")

    for j in range(22):
        bw = j % 2
        sc.dma("gpsimd", web[bw][:, :, :],
               kview(sw_gate)[:, :, j * 128:(j + 1) * 128], "guw",
               wait=[rr_we.use(j)])
        sc.dma("gpsimd", web[2 + bw][:, :, :],
               kview(sw_up)[:, :, j * 128:(j + 1) * 128], "guw")
        for cc in range(2):
            for kt in range(16):
                w = [("guw", sc.cur("guw")), rr_pg.use(0)] if kt == 0 else []
                sc.op("tensor", lambda e, kt=kt, cc=cc, bw=bw: e.matmul(
                    pg[:, :], web[bw][:, kt, :],
                    xn1T_own[:, kt, cc * 512:(cc + 1) * 512],
                    start=(kt == 0), stop=(kt == 15)),
                    wait=w, inc="gm" if kt == 15 else None)
            for kt in range(16):
                w = [rr_pu.use(0)] if kt == 0 else []
                sc.op("tensor", lambda e, kt=kt, cc=cc, bw=bw: e.matmul(
                    pu[:, :], web[2 + bw][:, kt, :],
                    xn1T_own[:, kt, cc * 512:(cc + 1) * 512],
                    start=(kt == 0), stop=(kt == 15)),
                    wait=w, inc="um" if kt == 15 else None)
            sc.op("scalar", lambda e: e.activation(gtmp[:, :], pg[:, :], AF.Silu),
                  wait=[("gm", sc.cur("gm")), rr_gt.use(0)], inc="gsl")
            rr_pg.done(0, "gsl")
            sc.op("vector", lambda e, cc=cc, j=j: e.tensor_copy(
                hbig[:, j, cc * 512:(cc + 1) * 512], pu[:, :]),
                wait=[("um", sc.cur("um"))], inc="uevm")
            rr_pu.done(0, "uevm")
            sc.op("vector", lambda e, cc=cc, j=j: e.tensor_mul(
                hbig[:, j, cc * 512:(cc + 1) * 512], gtmp[:, :],
                hbig[:, j, cc * 512:(cc + 1) * 512]),
                wait=[("gsl", sc.cur("gsl"))], inc="hc")
            rr_gt.done(0, "hc")
        rr_we.done(j, "um")
        if j % 8 == 7:
            sc.flush(f"q4s_{j}")
    sc.flush("q4_sh")
    rr_sd1 = RR(sc, 2)
    rr_sd2 = RR(sc, 2)
    for cc in range(2):
        for ot in range(16):
            bw = (cc * 16 + ot) % 2
            i = cc * 16 + ot
            sc.dma("gpsimd", web[bw][:, :, :],
                   kview(sw_down)[:, 0:16, ot * 128:(ot + 1) * 128], "sdw",
                   wait=[rr_sd1.use(i)])
            sc.dma("gpsimd", web[2 + bw][:, 0:6, :],
                   kview(sw_down)[:, 16:22, ot * 128:(ot + 1) * 128], "sdw",
                   wait=[rr_sd2.use(i)])
            for kt in range(22):
                w = [("sdw", sc.cur("sdw")), rr_pw.use(0)] if kt == 0 else []
                wsrc = web[bw] if kt < 16 else web[2 + bw]
                ktl = kt if kt < 16 else kt - 16
                sc.op("tensor", lambda e, kt=kt, cc=cc, wsrc=wsrc, ktl=ktl: e.matmul(
                    pw4[:, :], wsrc[:, ktl, :], hbig[:, kt, cc * 512:(cc + 1) * 512],
                    start=(kt == 0), stop=(kt == 21)),
                    wait=w, inc="sm" if kt == 21 else None)
            rr_sd1.done(i, "sm")
            rr_sd2.done(i, "sm")
            sc.op("vector", lambda e, ot=ot, cc=cc: e.tensor_add(
                ySB[:, ot, cc * 512:(cc + 1) * 512],
                ySB[:, ot, cc * 512:(cc + 1) * 512], pw4[:, :]),
                wait=[("sm", sc.cur("sm"))], inc="de")
            rr_pw.done(0, "de")
    sc.flush("q4_sd")
    es5.close()
    es6 = ExitStack()
    T6 = lambda n, s, d: es6.enter_context(
        nc.sbuf_tensor(n, s, d, side="right"))
    P6s = lambda n, s, d: es6.enter_context(nc.psum_tensor(n, s, d, allow_name_mangling=True))
    atok = [T6(f"atok2_{i}", [128, D], F32) for i in range(4)]
    ptrf = [P6s(f"ptrf2_{i}", [128, 128], F32) for i in range(2)]
    rr_trf = RR(sc, 2)
    for cc in range(2):
        atok_guard = ("rda", sc.cur("rda"))
        for ot in range(16):
            for tt in range(4):
                i2 = sc.cur("ytr")
                w3 = [rr_trf.use(i2)]
                if ot == 0 and tt == 0:
                    w3.append(atok_guard)
                sc.op("tensor", lambda e, ot=ot, tt=tt, cc=cc, i2=i2: e.transpose(
                    ptrf[i2 % 2][:, :],
                    ySB[:, ot, cc * 512 + tt * 128: cc * 512 + (tt + 1) * 128],
                    idf[:, :]), wait=w3, inc="ytr")
                sc.op("scalar", lambda e, tt=tt, ot=ot, i2=i2: e.copy(
                    atok[tt][:, ot * 128:(ot + 1) * 128], ptrf[i2 % 2][:, :]),
                    wait=[("ytr", sc.cur("ytr"))], inc="yte")
                rr_trf.done(i2, "yte")
        te_all = sc.cur("yte")
        for tt in range(4):
            sc.dma("gpsimd",
                   out_own[cc * 512 + tt * 128: cc * 512 + (tt + 1) * 128, :],
                   atok[tt][:, :], "rda", wait=[("yte", te_all)],
                   accum_op=ALU.add)
        sc.flush(f"q5_{cc}")
    sc.op("gpsimd", lambda e: e.engine_nop(),
          wait=[("rda", sc.cur("rda")), ("x2c", sc.cur("x2c"))])
    sc.flush("fused_fin")
    es6.close()
    es4.close()
    es.close()
    esG.close()
    return nc




def zig_chunks(par):
    return [par, 15 - par, 2 + par, 13 - par, 4 + par, 11 - par, 6 + par, 9 - par]


def build_masks_l0(par):
    m = np.zeros((8, 4, 128, CHUNK), np.float32)
    zz = zig_chunks(par)
    kk = np.arange(128)[:, None]
    qq = np.arange(CHUNK)[None, :]
    for s in range(8):
        for rt in range(4):
            t = PADL0_ALL[s] - 4 + rt
            m[s, rt] = ((128 * t + kk) <= (CHUNK * zz[s] + qq)).astype(np.float32)
    return m.astype(mld.bfloat16)


def kernel(**inp):
    cos0, sin0 = rope_tables_L0()
    cos1, sin1 = rope_tables_L1()
    ident = np.eye(128, dtype=mld.bfloat16)
    wrq = nb(inp["L0_attn_w_rq"])
    brq = nb(inp["L0_attn_b_rq"])
    wrq_p = np.concatenate([wrq[:, h * 64 + PERM64] for h in range(16)], axis=1)
    brq_p = np.concatenate([brq[h * 64 + PERM64] for h in range(16)])
    wrk_p = nb(inp["L0_attn_w_rk"])[:, PERM64]
    brk_p = nb(inp["L0_attn_b_rk"])[PERM64]

    common = {
        "ident": ident, "cos0": cos0, "sin0": sin0,
        "cos1": cos1, "sin1": sin1,
        "w_dkv": nb(inp["L0_attn_w_dkv"]), "b_dkv": to128(inp["L0_attn_b_dkv"]),
        "w_rk": np.ascontiguousarray(wrk_p),
        "b_rk": np.ascontiguousarray(brk_p.reshape(2, 32).T),
        "w_dq": nb(inp["L0_attn_w_dq"]), "b_dq": to128(inp["L0_attn_b_dq"]),
        "w_rq": np.ascontiguousarray(wrq_p),
        "b_rq": np.ascontiguousarray(brq_p.reshape(32, 32).T),
        "w_uk": nb(inp["L0_attn_w_uk"]), "b_uk": to128(inp["L0_attn_b_uk"]),
        "w_uv": nb(inp["L0_attn_w_uv"]),
        "b_uv": nb(inp["L0_attn_b_uv"]).reshape(1, D),
        "w_uq": nb(inp["L0_attn_w_uq"]), "b_uq": to128(inp["L0_attn_b_uq"]),
        "w_o": nb(inp["L0_attn_w_o"]), "b_o": to128(inp["L0_attn_b_o"]),
        "gw": nb(inp["L0_ffn_gate_w"]), "uw": nb(inp["L0_ffn_up_w"]),
        "dw": nb(inp["L0_ffn_down_w"]),
        "gsc": to128(inp["L0_ffn_gate_scale"]),
        "usc": to128(inp["L0_ffn_up_scale"]),
        "dsc": to128(inp["L0_ffn_down_scale"]),
        "w_qkvT": np.ascontiguousarray(nb(inp["L1_attn_w_qkv"]).T),
        "w_oT": np.ascontiguousarray(nb(inp["L1_attn_w_o"]).T),
        "gate_wT": np.ascontiguousarray(nb(inp["L1_ffn_gate_w"]).T),
        "w_gate": nb(inp["L1_ffn_w_gate"]), "w_up": nb(inp["L1_ffn_w_up"]),
        "w_down": nb(inp["L1_ffn_w_down"]),
        "sw_gate": nb(inp["L1_ffn_sw_gate"]), "sw_up": nb(inp["L1_ffn_sw_up"]),
        "sw_down": nb(inp["L1_ffn_sw_down"]),
    }
    x = nb(inp["L0_attn_x"])
    tokidx = [own_token_index(j) for j in range(4)]
    masks = [build_masks(j) for j in range(4)]
    l0_tok = [np.concatenate([np.arange(z * CHUNK, (z + 1) * CHUNK)
                              for z in zig_chunks(par)]) for par in range(2)]
    cmask0s = [build_masks_l0(par) for par in range(2)]
    nonce = np.uint32(np.random.default_rng().integers(1, 2**32 - 1))
    in_maps = []
    for c in range(8):
        b, j, par = c // 4, c % 4, c % 2
        m = dict(common)
        m["x_full"] = np.ascontiguousarray(x[b])
        m["x_own"] = np.ascontiguousarray(x[b][l0_tok[par]])
        m["cmask0"] = cmask0s[par]
        m["cmask"] = masks[j]
        m["cos0q"] = np.ascontiguousarray(cos0[:, l0_tok[par]])
        m["sin0q"] = np.ascontiguousarray(sin0[:, l0_tok[par]])
        m["cos1q"] = np.ascontiguousarray(cos1[:, tokidx[j]])
        m["sin1q"] = np.ascontiguousarray(sin1[:, tokidx[j]])
        m["scat_idx"] = np.ascontiguousarray(
            l0_tok[par].astype(np.uint32).reshape(16, 128).T)
        m["gath_idx"] = np.ascontiguousarray(
            tokidx[j].astype(np.uint32).reshape(8, 128).T)
        m["flag_idx"] = np.array([[par], [par + 2]], np.uint32)
        m["nonce_in"] = np.array([[nonce], [nonce]], np.uint32)
        in_maps.append(m)
    nc = build_fused()
    res = run_bass_kernel_spmd(nc, in_maps, core_ids=list(range(8)))
    out = np.zeros((B, S, D), np.float32)
    for c in range(8):
        b, j = c // 4, c % 4
        out[b][tokidx[j]] = res.results[c]["out_own"]
    return out


# version 09
# speedup vs baseline: 1.0800x; 1.0295x over previous
import math
import numpy as np
import ml_dtypes as mld
import concourse.bass as bass
import concourse.mybir as mybir
from concourse.bass_utils import run_bass_kernel_spmd
from contextlib import ExitStack

F32 = mybir.dt.float32
BF16 = mybir.dt.bfloat16
FP8 = mybir.dt.float8e4
AF = mybir.ActivationFunctionType
ALU = mybir.AluOpType
AX = mybir.AxisListType
U32 = mybir.dt.uint32

B, S, D = 2, 4096, 2048
NCH, CHUNK = 16, 256
NOWN = 1024
PADT = [8, 16, 24, 32]
PADL0_ALL = [4, 32, 8, 28, 12, 24, 16, 20]
PADL0_HALF = [PADL0_ALL[0:4], PADL0_ALL[4:8]]
SC0 = 1.0 / math.sqrt(192.0)
SC1 = 1.0 / math.sqrt(64.0)
COFF = 40.0
EPS = 1e-5
ENGS = ("sync", "gpsimd", "vector", "scalar", "tensor")


def own_chunks(j):
    return [j, 7 - j, 8 + j, 15 - j]


class Sched:

    def __init__(self, nc):
        self.nc = nc
        self.items = []
        self.val = {}
        self.sems = {}

    ALIAS = {"km": "pm", "kre": "pe", "krp": "rkr", "nm": "nmm", "qw": "wq",
             "qm": "pm2", "qre": "rqe", "qrp": "rqr", "wm": "wom", "we": "woe",
             "wtr": "wotr", "wte": "wote", "uevm": "uev", "cbm": "vm",
             "cbe": "ve", "sdw": "dww", "sm": "dm", "ytr": "ftr", "yte": "fte",
             "glm": "rkm", "gmx": "rke", "gex": "rsq", "gch": "Ls2",
             "gcb": "Lbv", "gtr": "Lsd", "gte": "Lxn"}

    def _n(self, name):
        return self.ALIAS.get(name, name)

    def sem(self, name):
        name = self._n(name)
        if name not in self.sems:
            self.sems[name] = self.nc.alloc_semaphore(name)
            self.val[name] = 0
        return self.sems[name]

    def op(self, eng, fn, wait=(), inc=None, inc_amt=1):
        wait = [(self._n(w[0]), w[1]) for w in wait if w is not None]
        for s, _ in wait:
            self.sem(s)
        v = None
        if inc is not None:
            inc = self._n(inc)
            self.sem(inc)
            self.val[inc] += inc_amt
            v = self.val[inc]
        self.items.append((eng, fn, tuple(wait), (inc, inc_amt) if inc else None))
        return v

    def dma(self, eng, out_ap, in_ap, inc, wait=(), **kw):
        def fn(e, o=out_ap, i=in_ap, kw=kw):
            return e.dma_start(o, i, **kw)
        return self.op(eng, fn, wait=wait, inc=inc, inc_amt=16)

    def cur(self, name):
        return self.val.get(self._n(name), 0)

    def flush(self, name=None):
        items, self.items = self.items, []
        if not items:
            return
        self._blk = getattr(self, "_blk", 0) + 1
        with self.nc.Block(f"{name or 'blk'}_b{self._blk}") as blk:
            for eng_name in ENGS:
                mine = [(fn, w, i) for (en, fn, w, i) in items if en == eng_name]
                if not mine:
                    continue

                def body(eng, mine=mine):
                    for fn, waits, inc in mine:
                        for s, v in waits:
                            if v > 0:
                                eng.wait_ge(self.sems[s], v)
                        inst = fn(eng)
                        if inc is not None:
                            inst.then_inc(self.sems[inc[0]], inc[1])
                getattr(blk, eng_name)(body)


class Box:
    def __init__(self, d, k):
        self._d = d
        self._k = k

    def __getitem__(self, sl):
        return self._d[self._k][sl]

    def __getattr__(self, a):
        return getattr(self._d[self._k], a)


class RR:

    def __init__(self, sc, n):
        self.sc = sc
        self.n = n
        self.rel = [None] * n

    def use(self, i):
        return self.rel[i % self.n]

    def done(self, i, semname):
        self.rel[i % self.n] = (semname, self.sc.cur(semname))


def emit_ln(sc, xc, xntok, sq_junk, st, wait_x, tag, epsb=None):
    NT = xc.shape[1]

    def vch(fn, extra=(), name=None):
        w = [(tag + "ch", sc.cur(tag + "ch"))] + list(extra)
        sc.op("vector", fn, wait=w, inc=(name or (tag + "ch")))

    for tt in range(NT):
        w0 = [wait_x] if tt == 0 else ()
        sc.op("vector", lambda e, tt=tt: e.reduce_sum(
            st[:, 0:1], xc[:, tt, :], axis=AX.X, negate=True),
            wait=list(w0) + [(tag + "ch", sc.cur(tag + "ch"))],
            inc=tag + "ch")
        sc.op("scalar", lambda e, tt=tt: e.activation(
            sq_junk[:, :], xc[:, tt, :], AF.Square, accum_out=st[:, 1:2]),
            wait=w0, inc=tag + "s2")
        vch(lambda e: e.tensor_scalar(
            st[:, 2:3], st[:, 0:1], 1.0 / D, None, ALU.mult),
            extra=[(tag + "s2", sc.cur(tag + "s2"))])
        vch(lambda e: e.tensor_mul(st[:, 3:4], st[:, 0:1], st[:, 2:3]))
        vch(lambda e: e.tensor_sub(st[:, 4:5], st[:, 1:2], st[:, 3:4]),
            name=tag + "bv")
        sc.op("scalar", lambda e: e.activation(
            st[:, 5:6], st[:, 4:5], AF.Sqrt, bias=epsb[:, 0:1], scale=1.0 / D),
            wait=[(tag + "bv", sc.cur(tag + "bv"))], inc=tag + "sd")
        sc.op("vector", lambda e: e.reciprocal(st[:, 6:7], st[:, 5:6]),
              wait=[(tag + "sd", sc.cur(tag + "sd")),
                    (tag + "ch", sc.cur(tag + "ch"))], inc=tag + "ch")
        vch(lambda e: e.tensor_scalar(
            st[:, 7:8], st[:, 2:3], 1.0, st[:, 6:7], ALU.mult, ALU.mult))
        vch(lambda e, tt=tt: e.tensor_scalar(
            xntok[:, tt, :], xc[:, tt, :], st[:, 6:7], st[:, 7:8],
            ALU.mult, ALU.add), name=tag + "xn")
    sc.op("vector", lambda e: e.engine_nop(),
          wait=[(tag + "xn", sc.cur(tag + "xn"))], inc=tag + "ch")


def emit_tposes(sc, xntok, dst_fn, idb, ptp, rr_tp, tag, n_f=16):
    NT = xntok.shape[1]
    xn_done = sc.cur(tag + "xn")
    n_base = sc.cur(tag + "tr")
    for tt in range(NT):
        for f in range(n_f):
            i = sc.cur(tag + "tr")
            w = [(tag + "xn", xn_done), rr_tp.use(i)]
            sc.op("tensor", lambda e, tt=tt, f=f, i=i: e.transpose(
                ptp[i % 2][:, :], xntok[:, tt, f * 128:(f + 1) * 128], idb[:, :]),
                wait=w, inc=tag + "tr")
            sc.op("scalar", lambda e, tt=tt, f=f, i=i: e.copy(
                dst_fn(f, tt), ptp[i % 2][:, :]),
                wait=[(tag + "tr", sc.cur(tag + "tr"))], inc=tag + "te")
            rr_tp.done(i, tag + "te")


def emit_rope(sc, tmp, x1b, x2b, out1, out2, cos_ap, sin_ap, tag, wait):
    t1, t2 = tmp
    sc.op("vector", lambda e: e.tensor_mul(t1[:, :], x1b, cos_ap), wait=wait)
    sc.op("vector", lambda e: e.tensor_mul(t2[:, :], x2b, sin_ap))
    sc.op("vector", lambda e: e.tensor_sub(out1, t1[:, :], t2[:, :]))
    sc.op("vector", lambda e: e.tensor_mul(t1[:, :], x1b, sin_ap))
    sc.op("vector", lambda e: e.tensor_mul(t2[:, :], x2b, cos_ap))
    sc.op("vector", lambda e: e.tensor_add(out2, t1[:, :], t2[:, :]), inc=tag)


def emit_attention(sc, nh, padt, kf_fn, qx_fn, v_fn, mask_fn, ptsb, psc, pout, pl,
                   pbc, lsb, rbsb, ones_l, ones1_f, out_fn, l_in_out,
                   head_hook=None, nmask=8, rr_sc=None, rr_pt=None, rr_slot=None):
    as_list = lambda x: x if isinstance(x, list) else [x]
    pout, pl = as_list(pout), as_list(pl)
    nbk = len(pout)
    if rr_sc is None:
        rr_sc = RR(sc, 2)
    if rr_pt is None:
        rr_pt = RR(sc, 2)
    if rr_slot is None:
        rr_slot = RR(sc, nbk)
    for h in range(nh):
        if head_hook:
            head_hook(h)
        for s_ in range(4):
            ntile = padt[s_]
            so = sc.cur("aev")
            ob = so % nbk
            rdy = {}

            def scores(t, h=h, s_=s_):
                i = sc.cur("scm")
                w = [rr_sc.use(i)]
                if head_hook:
                    w.append(head_hook(h, t, s_))
                sc.op("tensor", lambda e, t=t, h=h, s_=s_, i=i: kf_fn(
                    e, psc[i % len(psc)], h, s_, t), wait=w, inc="scm")

            def expmask(t, s_=s_, h=h):
                i = sc.cur("scm") - 1
                j = sc.cur("ex")
                w = [("scm", sc.cur("scm")), rr_pt.use(j)]
                sc.op("scalar", lambda e, i=i, j=j: qx_fn(
                    e, ptsb[j % len(ptsb)], psc[i % len(psc)]), wait=w, inc="ex")
                rr_sc.done(i, "ex")
                if t >= padt[s_] - nmask:
                    rt = t - (padt[s_] - nmask)
                    sc.op("vector", lambda e, j=j, rt=rt, s_=s_: e.tensor_mul(
                        ptsb[j % len(ptsb)][:, :], ptsb[j % len(ptsb)][:, :], mask_fn(s_, rt)),
                        wait=[("ex", sc.cur("ex"))], inc="mk")
                    return j, ("mk", sc.cur("mk"))
                return j, ("ex", sc.cur("ex"))

            def pvl(t, h=h, s_=s_, ntile=ntile, so=so, ob=ob):
                j, rd = rdy[t]
                w = [rd]
                if t == 0:
                    w.append(rr_slot.use(so))
                sc.op("tensor", lambda e, t=t, h=h, j=j, ntile=ntile, ob=ob: v_fn(
                    e, pout[ob], h, t, j % len(ptsb), t == 0, t == ntile - 1),
                    wait=w, inc="pvd" if l_in_out else None)
                if not l_in_out:
                    sc.op("tensor", lambda e, t=t, j=j, ntile=ntile, ob=ob: e.matmul(
                        pl[ob][:, :], ones_l[:, :], ptsb[j % len(ptsb)][:, :],
                        start=(t == 0), stop=(t == ntile - 1)), inc="pvd")
                rr_pt.done(j, "pvd")

            scores(0)
            rdy[0] = expmask(0)
            if ntile > 1:
                scores(1)
                rdy[1] = expmask(1)
            for t in range(ntile):
                pvl(t)
                if t + 2 < ntile:
                    scores(t + 2)
                    rdy[t + 2] = expmask(t + 2)
            lsrc = pout[ob][64:65, :] if l_in_out else pl[ob][0:1, :]
            sc.op("vector", lambda e, lsrc=lsrc: e.tensor_copy(lsb[0:1, :], lsrc),
                  wait=[("pvd", sc.cur("pvd"))], inc="lcp")
            sc.op("vector", lambda e: e.reciprocal(lsb[0:1, :], lsb[0:1, :]),
                  wait=[("lcp", sc.cur("lcp"))], inc="rc")
            sc.op("tensor", lambda e: e.matmul(
                pbc[:, :], ones1_f[:, :], lsb[0:1, :], start=True, stop=True),
                wait=[("rc", sc.cur("rc")), ("rbu", sc.cur("rbu"))], inc="bc")
            sc.op("vector", lambda e: e.tensor_copy(rbsb[:, :], pbc[:, :]),
                  wait=[("bc", sc.cur("bc"))], inc="rbu")
            sc.op("vector", lambda e, h=h, s_=s_, ob=ob: out_fn(e, h, s_, ob),
                  wait=[("rbu", sc.cur("rbu"))], inc="aev")
            rr_slot.done(so, "aev")


def nb(x):
    return np.ascontiguousarray(np.asarray(x, np.float32))


def to128(v):
    v = np.asarray(v, np.float32)
    return np.ascontiguousarray(v.reshape(-1, 128).T)


PERM64 = np.concatenate([np.arange(0, 64, 2), np.arange(1, 64, 2)])


def rope_tables_L0():
    pos = np.arange(S, dtype=np.float64)[:, None]
    inv = np.exp(np.arange(0, 128, 2, dtype=np.float64) * (-math.log(1e4) / 128))
    ang = pos * inv[None, :]
    return (np.ascontiguousarray(np.cos(ang)[:, :32].astype(np.float32).T),
            np.ascontiguousarray(np.sin(ang)[:, :32].astype(np.float32).T))


def rope_tables_L1():
    pos = np.arange(S, dtype=np.float64)[:, None]
    inv = 1.0 / (1e4 ** (np.arange(0, 64, 2, dtype=np.float64) / 64))
    ang = pos * inv[None, :]
    return (np.ascontiguousarray(np.cos(ang).astype(np.float32).T),
            np.ascontiguousarray(np.sin(ang).astype(np.float32).T))


def own_token_index(j):
    idx = []
    for c in own_chunks(j):
        idx.extend(range(c * CHUNK, (c + 1) * CHUNK))
    return np.asarray(idx)


def build_masks(j):
    m = np.zeros((4, 8, 128, CHUNK), np.float32)
    ch = own_chunks(j)
    kk = np.arange(128)[:, None]
    qq = np.arange(CHUNK)[None, :]
    for s in range(4):
        for rt in range(8):
            t = 8 * s + rt
            m[s, rt] = ((128 * t + kk) <= (CHUNK * ch[s] + qq)).astype(np.float32)
    return m.astype(mld.bfloat16)


def kview(w_ap, p=128):
    return w_ap.rearrange("(k p) o -> p k o", p=p)


def build_fused():
    nc = bass.Bass("TRN2", target_bir_lowering=False)
    EI, EO = "ExternalInput", "ExternalOutput"
    DT = lambda n, s, d, k="Internal": nc.dram_tensor(n, s, d, kind=k).ap()
    x1_sh = nc.dram_tensor("x1_sh", [S, D], F32, addr_space="Shared").ap()
    shf = nc.dram_tensor("shf", [4, 1], U32, addr_space="Shared").ap()
    scat_idx = DT("scat_idx", [128, 16], U32, EI)
    gath_idx = DT("gath_idx", [128, 8], U32, EI)
    flag_idx = DT("flag_idx", [2, 1], U32, EI)
    nonce_in = DT("nonce_in", [2, 1], U32, EI)
    out_own = DT("out_own", [NOWN, D], F32, EO)
    out_own_rd = out_own
    x_full = DT("x_full", [S, D], F32, EI)
    x_own = DT("x_own", [2 * NOWN, D], F32, EI)
    ident = DT("ident", [128, 128], BF16, EI)
    cmask0 = DT("cmask0", [8, 4, 128, CHUNK], BF16, EI)
    cmask = DT("cmask", [4, 8, 128, CHUNK], BF16, EI)
    cos0 = DT("cos0", [32, S], F32, EI)
    sin0 = DT("sin0", [32, S], F32, EI)
    cos0q = DT("cos0q", [32, 2 * NOWN], F32, EI)
    sin0q = DT("sin0q", [32, 2 * NOWN], F32, EI)
    w_dkv = DT("w_dkv", [D, 512], F32, EI)
    b_dkv = DT("b_dkv", [128, 4], F32, EI)
    w_rk = DT("w_rk", [D, 64], F32, EI)
    b_rk = DT("b_rk", [32, 2], F32, EI)
    w_dq = DT("w_dq", [D, 1536], F32, EI)
    b_dq = DT("b_dq", [128, 12], F32, EI)
    w_rq = DT("w_rq", [1536, 1024], F32, EI)
    b_rq = DT("b_rq", [32, 32], F32, EI)
    w_uk = DT("w_uk", [512, D], F32, EI)
    b_uk = DT("b_uk", [128, 16], F32, EI)
    w_uv = DT("w_uv", [512, D], F32, EI)
    b_uv = DT("b_uv", [1, D], F32, EI)
    w_uq = DT("w_uq", [1536, D], F32, EI)
    b_uq = DT("b_uq", [128, 16], F32, EI)
    w_o = DT("w_o", [D, D], F32, EI)
    b_o = DT("b_o", [128, 16], F32, EI)
    gw = DT("gw", [D, 6144], F32, EI)
    uw = DT("uw", [D, 6144], F32, EI)
    dw = DT("dw", [6144, D], F32, EI)
    gsc = DT("gsc", [128, 48], F32, EI)
    usc = DT("usc", [128, 48], F32, EI)
    dsc = DT("dsc", [128, 16], F32, EI)
    x1_out = DT("x1_local", [2 * NOWN, D], F32)
    k_sp = DT("k_sp", [16, 128, S], BF16)
    v_sp = DT("v_sp", [128, 32, 16, 128], BF16)

    esG = ExitStack()
    TG = lambda n, s, d: esG.enter_context(nc.sbuf_tensor(n, s, d))
    es = ExitStack()
    esR = ExitStack()
    T = lambda n, s, d: es.enter_context(nc.sbuf_tensor(n, s, d))
    TR = lambda n, s, d: esR.enter_context(nc.sbuf_tensor(n, s, d, side="right"))
    sc = Sched(nc)

    idb = TG("idb", [128, 128], BF16)
    idf = TG("idf", [128, 128], F32)
    onesc = TG("onesc", [128, 1], BF16)
    epsb = TG("epsb", [128, 1], F32)
    ones1b = TG("ones1b", [1, 128], BF16)
    ones1f = TG("ones1f", [1, 128], F32)
    kmax = T("kmax", [1, NCH * 16], F32)
    Km = T("Km", [1, 16 * 8], F32)
    bias_dkv = T("bias_dkv", [128, 4], F32)
    bias_rk = T("bias_rk", [32, 2], F32)
    bias_dq = T("bias_dq", [128, 12], F32)
    bias_rq = T("bias_rq", [32, 32], F32)
    bias_uk = T("bias_uk", [128, 16], F32)

    bias_uq = T("bias_uq", [128, 16], F32)
    bias_o = T("bias_o", [128, 16], F32)
    gsc_sb = T("gsc_sb", [128, 48], F32)
    usc_sb = T("usc_sb", [128, 48], F32)
    dsc_sb = T("dsc_sb", [128, 16], F32)
    xnT_own = T("xnT_own", [128, 16, NOWN], BF16)
    ropek = TR("ropek", [128, S], BF16)
    c0q = TR("c0q", [32, NOWN], F32)
    s0q = TR("s0q", [32, NOWN], F32)

    for dst, src in [(idb, ident), (bias_dkv, b_dkv), (bias_rk, b_rk),
                     (bias_dq, b_dq), (bias_rq, b_rq), (bias_uk, b_uk),
                     (bias_uq, b_uq), (bias_o, b_o), (gsc_sb, gsc),
                     (usc_sb, usc), (dsc_sb, dsc)]:
        sc.dma("sync", dst[:, :], src, "cld")
    sc.dma("gpsimd", idf[:, :], ident, "cld")
    sc.dma("sync", x1_out[0:NOWN, :], x_own[0:NOWN, :], "x1c")
    sc.dma("sync", x1_out[NOWN:2 * NOWN, :], x_own[NOWN:2 * NOWN, :], "x1c")
    sc.op("vector", lambda e: e.memset(onesc[:, :], 1.0))
    sc.op("vector", lambda e: e.memset(epsb[:, :], EPS))
    sc.op("vector", lambda e: e.memset(ones1b[:, :], 1.0))
    sc.op("vector", lambda e: e.memset(ones1f[:, :], 1.0))
    sc.op("vector", lambda e: e.memset(ropek[64:65, :], 1.0))
    sc.flush("p0")

    NT = 2
    esL = ExitStack()
    TL = lambda n, s, d: esL.enter_context(nc.sbuf_tensor(n, s, d))
    TLR = lambda n, s, d: esL.enter_context(nc.sbuf_tensor(n, s, d, side="right"))
    PL = lambda n, s, d: esL.enter_context(nc.psum_tensor(n, s, d))
    es1 = ExitStack()
    T1 = lambda n, s, d: es1.enter_context(nc.sbuf_tensor(n, s, d))
    _scr = {}
    c0c = T1("c0c", [32, CHUNK], F32)
    s0c = T1("s0c", [32, CHUNK], F32)
    xnT_c = [T1(f"xnTc{i}", [128, 16, CHUNK], BF16) for i in range(2)]
    lkvT_c = [T1(f"lkvTc{i}", [128, 4, CHUNK], BF16) for i in range(2)]
    kc_sb = T1("kc_sb", [128, 16, CHUNK], BF16)
    vc_sb = T1("vc_sb", [128, NT, D], BF16)
    P1p = lambda n, s, d: es1.enter_context(nc.psum_tensor(n, s, d))
    ptp = [P1p(f"ptp{i}", [128, 128], BF16) for i in range(2)]
    pproj = [P1p(f"pproj{i}", [128, CHUNK], F32) for i in range(2)]
    prk = P1p("prk", [64, CHUNK], F32)
    pv = P1p("pvv", [128, 512], F32)
    pnrm = P1p("pnrm", [1, 512], F32)
    rr_tp = RR(sc, 2)
    rr_pp = RR(sc, 2)
    rr_xc = RR(sc, 2)
    rr_lkv = RR(sc, 2)
    rr_prk = RR(sc, 1)
    rr_pv = RR(sc, 1)
    rr_pn = RR(sc, 1)
    rr_kc = RR(sc, 1)
    rr_vc = RR(sc, 1)

    for _n, _s, _d in [("sq_junk", [128, D], BF16), ("tr1", [32, CHUNK], F32),
                       ("tr2", [32, CHUNK], F32), ("x1b", [32, CHUNK], F32),
                       ("x2b", [32, CHUNK], F32), ("qrow", [1, 512], F32)]:
        _scr[_n] = es1.enter_context(nc.sbuf_tensor(_n, _s, _d, allow_name_mangling=True))
    sq_junk = Box(_scr, "sq_junk"); tr1 = Box(_scr, "tr1"); tr2 = Box(_scr, "tr2")
    x1b = Box(_scr, "x1b"); x2b = Box(_scr, "x2b"); qrow = Box(_scr, "qrow")
    sqbuf = T1("sqbuf", [128, CHUNK], BF16)
    sqrope = T1("sqrope", [64, CHUNK], BF16)
    xc = [T1(f"xc{i}", [128, NT, D], F32) for i in range(2)]
    xntok = [T1(f"xntok{i}", [128, NT, D], BF16) for i in range(2)]
    sttb = [T1("stt", [128, 8], F32)]
    bias_uv = T1("bias_uv", [1, D], BF16)
    sc.dma("gpsimd", bias_uv[:, :], b_uv, "cld")
    wdkv_sb = T1("wdkv_sb", [128, 16, 512], BF16)
    wrk_sb = T1("wrk_sb", [128, 16, 64], BF16)
    wuk_sb = T1("wuk_sb", [128, 4, D], BF16)
    wuv_sb = T1("wuv_sb", [128, 4, D], BF16)
    sc.dma("gpsimd", wdkv_sb[:, :, :], kview(w_dkv), "wld")
    sc.dma("gpsimd", wrk_sb[:, :, :], kview(w_rk), "wld")
    sc.dma("gpsimd", wuk_sb[:, :, :], kview(w_uk), "wld")
    sc.dma("gpsimd", wuv_sb[:, :, :], kview(w_uv), "wld")
    wait_start = [("cld", sc.cur("cld")), ("wld", sc.cur("wld"))]

    def l0_chunk(c, x_src, is_own, oc=None):
        bi = c % 2
        sc.dma("sync", xc[bi][:, :, :],
               x_src.rearrange("(n t p) d -> n p t d", t=NT, p=128)[c if not is_own else oc],
               "xld", wait=(wait_start if (c == 0 and not is_own) else [rr_xc.use(c)]))
        emit_ln(sc, xc[bi], xntok[bi], sq_junk, sttb[0], ("xld", sc.cur("xld")), "L", epsb)
        rr_xc.done(c, "Lxn")

    for c in range(NCH):
        bi = c % 2
        l0_chunk(c, x_full, False)
        emit_tposes(sc, xntok[bi],
                    lambda f, tt, bi=bi: xnT_c[bi][:, f, tt * 128:(tt + 1) * 128],
                    idb, ptp, rr_tp, "L")
        xt_done = sc.cur("Lte")
        for ot in range(4):
            i = sc.cur("pm")
            for kt in range(16):
                w = [rr_pp.use(i)] if kt == 0 else []
                if kt == 0 and ot == 0:
                    w.append(("Lte", xt_done))
                sc.op("tensor", lambda e, ot=ot, kt=kt, i=i, bi=bi: e.matmul(
                    pproj[i % 2][:, :], wdkv_sb[:, kt, ot * 128:(ot + 1) * 128],
                    xnT_c[bi][:, kt, :], start=(kt == 0), stop=(kt == 15)),
                    wait=w, inc="pm" if kt == 15 else None)
            lw = [("pm", sc.cur("pm")), rr_lkv.use(c)] if ot == 0 else \
                 [("pm", sc.cur("pm"))]
            sc.op("vector", lambda e, ot=ot, i=i, bi=bi: e.tensor_scalar(
                lkvT_c[bi][:, ot, :], pproj[i % 2][:, :],
                bias_dkv[:, ot:ot + 1], None, ALU.add), wait=lw, inc="pe")
            rr_pp.done(i, "pe")
        for kt in range(16):
            w = [rr_prk.use(0)] if kt == 0 else ()
            sc.op("tensor", lambda e, kt=kt, bi=bi: e.matmul(
                prk[:, :], wrk_sb[:, kt, :], xnT_c[bi][:, kt, :],
                start=(kt == 0), stop=(kt == 15)),
                wait=w, inc="rkm" if kt == 15 else None)
        sc.op("vector", lambda e: e.tensor_scalar(
            x1b[:, :], prk[0:32, :], bias_rk[:, 0:1], None, ALU.add),
            wait=[("rkm", sc.cur("rkm")), ("rkr", sc.cur("rkr"))])
        sc.op("vector", lambda e: e.tensor_scalar(
            x2b[:, :], prk[32:64, :], bias_rk[:, 1:2], None, ALU.add),
            inc="rke")
        rr_prk.done(0, "rke")
        sc.dma("sync", c0c[:, :], cos0[:, c * CHUNK:(c + 1) * CHUNK], "cs",
               wait=[("rkr", sc.cur("rkr"))])
        sc.dma("sync", s0c[:, :], sin0[:, c * CHUNK:(c + 1) * CHUNK], "cs")
        emit_rope(sc, (tr1, tr2), x1b[:, :], x2b[:, :],
                  ropek[0:32, c * CHUNK:(c + 1) * CHUNK],
                  ropek[32:64, c * CHUNK:(c + 1) * CHUNK],
                  c0c[0:32, :], s0c[0:32, :], "rkr",
                  wait=[("rke", sc.cur("rke")), ("cs", sc.cur("cs"))])
        lkv_done = sc.cur("pe")
        for h in range(16):
            i = sc.cur("pm")
            for kt in range(4):
                w = []
                if kt == 0:
                    w = [rr_pp.use(i), ("pe", lkv_done)]
                    if h == 0:
                        w.append(rr_kc.use(0))
                sc.op("tensor", lambda e, h=h, kt=kt, i=i, bi=bi: e.matmul(
                    pproj[i % 2][:, :], wuk_sb[:, kt, h * 128:(h + 1) * 128],
                    lkvT_c[bi][:, kt, :], start=(kt == 0), stop=(kt == 3)),
                    wait=w, inc="pm" if kt == 3 else None)
            sc.op("vector", lambda e, h=h, i=i: e.tensor_scalar(
                kc_sb[:, h, :], pproj[i % 2][:, :], bias_uk[:, h:h + 1],
                None, ALU.add), wait=[("pm", sc.cur("pm"))], inc="pe")
            rr_pp.done(i, "pe")
        sc.op("scalar", lambda e, c=c: e.activation(
            sqrope[:, :], ropek[0:64, c * CHUNK:(c + 1) * CHUNK], AF.Square),
            wait=[("rkr", sc.cur("rkr")), ("nmm", sc.cur("nmm"))], inc="rsq")
        for h in range(16):
            sc.op("scalar", lambda e, h=h: e.activation(
                sqbuf[:, :], kc_sb[:, h, :], AF.Square),
                wait=[("pe", sc.cur("pe")), ("nmm", sc.cur("nmm"))], inc="nsq")
            sc.op("tensor", lambda e: e.matmul(
                pnrm[0:1, 0:CHUNK], onesc[:, :], sqbuf[:, :], start=True, stop=False),
                wait=[("nsq", sc.cur("nsq")), rr_pn.use(0),
                      ("rsq", sc.cur("rsq"))])
            sc.op("tensor", lambda e: e.matmul(
                pnrm[0:1, 0:CHUNK], onesc[0:64, :], sqrope[:, :], start=False, stop=True),
                inc="nmm")
            sc.op("vector", lambda e, h=h, c=c: e.tensor_reduce(
                kmax[0:1, c * 16 + h:c * 16 + h + 1], pnrm[0:1, 0:CHUNK],
                axis=AX.X, op=ALU.max),
                wait=[("nmm", sc.cur("nmm"))], inc="nrd")
            rr_pn.done(0, "nrd")
        sc.dma("sync", k_sp.rearrange("h p (n t) -> p h n t", t=CHUNK)[:, :, c, :],
               kc_sb[:, :, :], "ksp", wait=[("nsq", sc.cur("nsq"))])
        rr_kc.done(0, "ksp")
        for g in range(4):
            for ktk in range(NT):
                for lt in range(4):
                    w = []
                    if lt == 0:
                        w = [rr_pv.use(0)]
                        if g == 0 and ktk == 0:
                            w.append(rr_vc.use(0))
                    sc.op("tensor", lambda e, g=g, ktk=ktk, lt=lt, bi=bi: e.matmul(
                        pv[:, :], lkvT_c[bi][:, lt, ktk * 128:(ktk + 1) * 128],
                        wuv_sb[:, lt, g * 512:(g + 1) * 512],
                        start=(lt == 0), stop=False), wait=w)
                sc.op("tensor", lambda e, g=g: e.matmul(
                    pv[:, :], ones1b[:, :], bias_uv[0:1, g * 512:(g + 1) * 512],
                    start=False, stop=True), inc="vm")
                sc.op("vector", lambda e, g=g, ktk=ktk: e.tensor_copy(
                    vc_sb[:, ktk, g * 512:(g + 1) * 512], pv[:, :]),
                    wait=[("vm", sc.cur("vm"))], inc="ve")
                rr_pv.done(0, "ve")
        sc.dma("sync",
               v_sp.rearrange("p t h d -> p t (h d)")[:, 2 * c:2 * c + 2, :],
               vc_sb[:, :, :], "vsp", wait=[("ve", sc.cur("ve"))])
        rr_vc.done(0, "vsp")
        rr_lkv.done(c, "ve")
    sc.flush("p1_end")
    for h in range(16):
        for s_ in range(8):
            npre = PADL0_ALL[s_] // 2
            sc.op("vector", lambda e, h=h, s_=s_, npre=npre: e.tensor_reduce(
                Km[0:1, h * 8 + s_: h * 8 + s_ + 1],
                kmax.ap().rearrange("p (c h) -> p h c", h=16)[0:1, h, 0:npre],
                axis=AX.X, op=ALU.max), inc="kmf")
    sc.op("scalar", lambda e: e.activation(Km[0:1, :], Km[0:1, :], AF.Sqrt),
          wait=[("kmf", sc.cur("kmf"))], inc="kms")
    sc.op("vector", lambda e: e.tensor_scalar(
        Km[0:1, :], Km[0:1, :], -1.0, None, ALU.mult),
          wait=[("kms", sc.cur("kms"))], inc="kmn")
    sc.flush("p1_km")
    es1.close()

    for half in range(2):
        sc.dma("sync", c0q[:, :], cos0q[:, half * NOWN:(half + 1) * NOWN], "csq")
        sc.dma("sync", s0q[:, :], sin0q[:, half * NOWN:(half + 1) * NOWN], "csq")
        esRH = ExitStack()
        TRH = lambda n, s, d: esRH.enter_context(
            nc.sbuf_tensor(n, s, d, side="right", allow_name_mangling=True))
        qx = TRH("qx", [128, 16, NOWN], BF16)
        rqx = TRH("rqx", [128, 16 * NOWN], BF16)
        es2 = ExitStack()
        T2 = lambda n, s, d: es2.enter_context(nc.sbuf_tensor(n, s, d, allow_name_mangling=True))
        P2p = lambda n, s, d: es2.enter_context(
            nc.psum_tensor(n, s, d, allow_name_mangling=True))
        ptp = [P2p(f"ptpH{i}", [128, 128], BF16) for i in range(2)]
        pproj = [P2p(f"pprojH{i}", [128, CHUNK], F32) for i in range(2)]
        pv = P2p("pvvH", [128, 512], F32)
        pnrm = P2p("pnrmH", [1, 512], F32)
        xc[0] = T2("xcH0", [128, NT, D], F32)
        xc[1] = T2("xcH1", [128, NT, D], F32)
        xntok[0] = T2("xntokH0", [128, NT, D], BF16)
        xntok[1] = T2("xntokH1", [128, NT, D], BF16)
        sttb[0] = T2("sttH", [128, 8], F32)
        for _n, _s, _d in [("sq_junk", [128, D], BF16), ("tr1", [32, CHUNK], F32),
                           ("tr2", [32, CHUNK], F32), ("x1b", [32, CHUNK], F32),
                           ("x2b", [32, CHUNK], F32), ("qrow", [1, 512], F32)]:
            _scr[_n] = es2.enter_context(nc.sbuf_tensor(_n + "H", _s, _d, allow_name_mangling=True))
        for oc in range(4):
            bi = oc % 2
            l0_chunk(oc, x_own, True, half * 4 + oc)
            emit_tposes(sc, xntok[bi],
                        lambda f, tt, oc=oc: xnT_own[:, f, oc * CHUNK + tt * 128:
                                                     oc * CHUNK + (tt + 1) * 128],
                        idb, ptp, rr_tp, "L")
        sc.flush("p1b_ln_h" + str(half))

        wq_sb = [T2(f"wq_sb{i}", [128, 16, 128], BF16) for i in range(2)]
        lqT = T2("lqT", [128, 12, NOWN], BF16)
        rr_wq = RR(sc, 2)
        xt_own = sc.cur("Lte")
        for ot in range(12):
            bw = ot % 2
            sc.dma("gpsimd", wq_sb[bw][:, :, :],
                   kview(w_dq)[:, :, ot * 128:(ot + 1) * 128], "wq", wait=[rr_wq.use(ot)])
            for cc in range(2):
                i = sc.cur("pm2")
                for kt in range(16):
                    w = []
                    if kt == 0:
                        w = [("wq", sc.cur("wq")), rr_pv.use(0)]
                        if ot == 0 and cc == 0:
                            w.append(("Lte", xt_own))
                    sc.op("tensor", lambda e, ot=ot, cc=cc, kt=kt, bw=bw: e.matmul(
                        pv[:, 0:512], wq_sb[bw][:, kt, :],
                        xnT_own[:, kt, cc * 512:(cc + 1) * 512],
                        start=(kt == 0), stop=(kt == 15)),
                        wait=w, inc="pm2" if kt == 15 else None)
                sc.op("vector", lambda e, ot=ot, cc=cc: e.tensor_scalar(
                    lqT[:, ot, cc * 512:(cc + 1) * 512], pv[:, 0:512],
                    bias_dq[:, ot:ot + 1], None, ALU.add),
                    wait=[("pm2", sc.cur("pm2"))], inc="pe2")
                rr_pv.done(0, "pe2")
            rr_wq.done(ot, "pm2")
        sc.flush("p1b_lq_h" + str(half))
        for h in range(16):
            bw = (h // 2) % 2
            if h % 2 == 0:
                sc.dma("gpsimd", wq_sb[bw][:, 0:12, :],
                       kview(w_rq)[:, :, h * 64:(h + 2) * 64], "wq",
                       wait=[rr_wq.use(h // 2)])
            for oc in range(4):
                i = sc.cur("pm2")
                for kt in range(12):
                    w = [("wq", sc.cur("wq")), rr_pp.use(i)] if kt == 0 else []
                    sc.op("tensor", lambda e, h=h, oc=oc, kt=kt, bw=bw, i=i: e.matmul(
                        pproj[i % 2][0:64, :],
                        wq_sb[bw][:, kt, (h % 2) * 64:(h % 2) * 64 + 64],
                        lqT[:, kt, oc * CHUNK:(oc + 1) * CHUNK],
                        start=(kt == 0), stop=(kt == 11)),
                        wait=w, inc="pm2" if kt == 11 else None)
                sc.op("vector", lambda e, h=h, i=i: e.tensor_scalar(
                    x1b[:, :], pproj[i % 2][0:32, :], bias_rq[:, 2 * h:2 * h + 1],
                    None, ALU.add),
                    wait=[("pm2", sc.cur("pm2")), ("rqr", sc.cur("rqr")),
                          ("csq", sc.cur("csq"))])
                sc.op("vector", lambda e, h=h, i=i: e.tensor_scalar(
                    x2b[:, :], pproj[i % 2][32:64, :], bias_rq[:, 2 * h + 1:2 * h + 2],
                    None, ALU.add), inc="rqe")
                rr_pp.done(i, "rqe")
                qc = h * NOWN + oc * CHUNK
                emit_rope(sc, (tr1, tr2), x1b[:, :], x2b[:, :],
                          rqx[0:32, qc:qc + CHUNK], rqx[32:64, qc:qc + CHUNK],
                          c0q[0:32, oc * CHUNK:(oc + 1) * CHUNK],
                          s0q[0:32, oc * CHUNK:(oc + 1) * CHUNK], "rqr",
                          wait=[("rqe", sc.cur("rqe"))])
            if h % 2 == 1:
                rr_wq.done(h // 2, "pm2")
        sc.flush("p1b_rq_h" + str(half))
        for ot in range(16):
            bw = ot % 2
            h = ot
            sc.dma("gpsimd", wq_sb[bw][:, 0:12, :],
                   kview(w_uq)[:, :, ot * 128:(ot + 1) * 128], "wq", wait=[rr_wq.use(ot)])
            for cc in range(2):
                for kt in range(12):
                    w = [("wq", sc.cur("wq")), rr_pv.use(0)] if kt == 0 else []
                    sc.op("tensor", lambda e, ot=ot, cc=cc, kt=kt, bw=bw: e.matmul(
                        pv[:, 0:512], wq_sb[bw][:, kt, :],
                        lqT[:, kt, cc * 512:(cc + 1) * 512],
                        start=(kt == 0), stop=(kt == 11)),
                        wait=w, inc="pm2" if kt == 11 else None)
                sc.op("vector", lambda e, ot=ot, cc=cc: e.tensor_scalar(
                    qx[:, ot, cc * 512:(cc + 1) * 512], pv[:, 0:512],
                    bias_uq[:, ot:ot + 1], None, ALU.add),
                    wait=[("pm2", sc.cur("pm2"))], inc="pe2")
                rr_pv.done(0, "pe2")
                qc = h * NOWN + cc * 512
                sc.op("scalar", lambda e, h=h, cc=cc: e.activation(
                    sq_junk[:, 0:512], qx[:, h, cc * 512:(cc + 1) * 512], AF.Square),
                    wait=[("pe2", sc.cur("pe2")), ("qnm", sc.cur("qnm"))], inc="qsq")
                sc.op("scalar", lambda e, qc=qc: e.activation(
                    sq_junk[0:64, 512:1024], rqx[0:64, qc:qc + 512], AF.Square),
                    inc="qsq")
                sc.op("tensor", lambda e: e.matmul(
                    pnrm[0:1, 0:512], onesc[:, :], sq_junk[:, 0:512],
                    start=True, stop=False),
                    wait=[("qsq", sc.cur("qsq")), ("qrd", sc.cur("qrd"))])
                sc.op("tensor", lambda e: e.matmul(
                    pnrm[0:1, 0:512], onesc[0:64, :], sq_junk[0:64, 512:1024],
                    start=False, stop=True), inc="qnm")
                sc.op("scalar", lambda e: e.activation(
                    qrow[0:1, :], pnrm[0:1, 0:512], AF.Sqrt),
                    wait=[("qnm", sc.cur("qnm"))], inc="qrt")
                for sh in range(2):
                    s_ = cc * 2 + sh
                    qc2 = h * NOWN + s_ * CHUNK
                    ks = h * 8 + half * 4 + s_
                    sc.op("vector", lambda e, ks=ks, qc2=qc2, sh=sh: e.tensor_scalar(
                        rqx[64:65, qc2:qc2 + CHUNK],
                        qrow[0:1, sh * CHUNK:(sh + 1) * CHUNK],
                        Km[0:1, ks:ks + 1], COFF / SC0,
                        ALU.mult, ALU.add),
                        wait=[("qrt", sc.cur("qrt"))], inc="qrd")
            rr_wq.done(ot, "pm2")
        sc.flush("p1b_uq_h" + str(half))
        es2.close()

        es25 = ExitStack()
        attnT = es25.enter_context(nc.sbuf_tensor("attnT", [128, 16, NOWN], BF16, allow_name_mangling=True))
        msk0 = es25.enter_context(nc.sbuf_tensor("msk0", [128, 8 * 4 * CHUNK], BF16, allow_name_mangling=True))
        sc.dma("sync", msk0.ap().rearrange("p (a q) -> p a q", q=CHUNK),
               cmask0.rearrange("s r p q -> p (s r) q"), "mskl")
        es3 = ExitStack()
        T3 = lambda n, s, d: es3.enter_context(nc.sbuf_tensor(n, s, d, side="right", allow_name_mangling=True))
        P3s = lambda n, s, d: es3.enter_context(nc.psum_tensor(n, s, d, allow_name_mangling=True))
        kbuf = [T3(f"kbuf{i}", [128, S], BF16) for i in range(2)]
        vbuf = [T3(f"vbuf{i}", [128, 32, 128], BF16) for i in range(2)]
        ptsb = [T3(f"ptsb{i}", [128, CHUNK], BF16) for i in range(3)]
        lsb = T3("lsb", [1, CHUNK], F32)
        rbsb = T3("rbsb", [128, CHUNK], F32)
        psc = [P3s(f"psc{i}", [128, CHUNK], F32) for i in range(3)]
        pout = [P3s(f"pout{i}", [128, CHUNK], F32) for i in range(2)]
        pl = [P3s(f"pl{i}", [1, CHUNK], F32) for i in range(2)]
        pbc = P3s("pbc", [128, CHUNK], F32)
        rr_kv = RR(sc, 2)
        rr_scB = RR(sc, 3)
        rr_ptB = RR(sc, 3)
        kvl_at = {}
        sc.op("vector", lambda e: e.engine_nop(), wait=[("mskl", sc.cur("mskl"))],
              inc="mskb")

        for h in range(16):
            hb = h % 2
            sc.dma("sync", kbuf[hb][:, :], k_sp[h, :, :], "kvl", wait=[rr_kv.use(h)])
            sc.dma("sync", vbuf[hb][:, :, :], v_sp[:, :, h, :], "kvl")
            kvl_at[h] = sc.cur("kvl")

            def kf(e, bank, hh, s_, t, hb=hb, h=h):
                e.matmul(bank[:, :], kbuf[hb][:, t * 128:(t + 1) * 128],
                         qx[:, h, s_ * CHUNK:(s_ + 1) * CHUNK],
                         start=True, stop=False)
                return e.matmul(bank[:, :], ropek[0:65, t * 128:(t + 1) * 128],
                                rqx[0:65, h * NOWN + s_ * CHUNK: h * NOWN + (s_ + 1) * CHUNK],
                                start=False, stop=True)

            def qxf(e, dst, srcbank):
                return e.activation(dst[:, :], srcbank[:, :], AF.Exp, scale=SC0)

            def vf(e, bank, hh, t, pb, st_, sp_, hb=hb):
                return e.matmul(bank[:, :], vbuf[hb][:, t, :], ptsb[pb][:, :],
                                start=st_, stop=sp_)

            def outf(e, hh, s_, ob, h=h):
                return e.tensor_mul(attnT[:, h, s_ * CHUNK:(s_ + 1) * CHUNK],
                                    pout[ob][:, :], rbsb[:, :])

            def hook(hh, t=None, s_=None, h=h):
                if t is None:
                    return None
                if s_ == 0 and t < 2:
                    return ("kvl", kvl_at[h])
                return None

            emit_attention(sc, 1, PADL0_HALF[half], kf, qxf, vf,
                           lambda s_, rt: msk0[:, ((half * 4 + s_) * 4 + rt) * CHUNK:
                                               ((half * 4 + s_) * 4 + rt + 1) * CHUNK],
                           ptsb, psc, pout, pl,
                           pbc, lsb, rbsb, onesc, ones1f, outf, l_in_out=False,
                           head_hook=hook, nmask=4, rr_sc=rr_scB, rr_pt=rr_ptB)
            rr_kv.done(h, "pvd")
        sc.flush(f"p2_h{half}")

        es3.close()
        esRH.close()
        es3b = ExitStack()
        T3b = lambda n, s, d: es3b.enter_context(nc.sbuf_tensor(n, s, d, allow_name_mangling=True))
        P3b = lambda n, s, d: es3b.enter_context(nc.psum_tensor(n, s, d, allow_name_mangling=True))
        wo_sb = [T3b(f"wo_sb{i}", [128, 16, 128], BF16) for i in range(2)]
        woT = T3b("woT", [128, 512], F32)
        atok = [T3b(f"atok{i}", [128, D], F32) for i in range(4)]
        ptrf = [P3b(f"ptrf{i}", [128, 128], F32) for i in range(2)]
        pw = P3b("pw", [128, 512], F32)
        rr_wo = RR(sc, 2)
        rr_pw = RR(sc, 1)
        rr_trf = RR(sc, 2)
        rr_woT = RR(sc, 1)

        def proj_transpose_add(sc, w_src, n_kt, rhs_fn, out_dram, tag, scale_sb=None,
                               bias_sb=None, add_wait=None, wdt=BF16, h_kt=16):
            for cc in range(2):
                atok_guard = ("rda", sc.cur("rda"))
                for ot in range(16):
                    bw = ot % 2
                    sc.dma("gpsimd", wo_sb[bw][:, 0:h_kt, :] if wdt == BF16 else None,
                           kview(w_src)[:, :, ot * 128:(ot + 1) * 128], tag + "w",
                           wait=[rr_wo.use(cc * 16 + ot)])
                    i = cc * 16 + ot
                    for kt in range(n_kt):
                        w = [(tag + "w", sc.cur(tag + "w")), rr_pw.use(0)] if kt == 0 else []
                        sc.op("tensor", lambda e, cc=cc, ot=ot, kt=kt, bw=bw: e.matmul(
                            pw[:, :], wo_sb[bw][:, kt, :], rhs_fn(kt, cc),
                            start=(kt == 0), stop=(kt == n_kt - 1)),
                            wait=w, inc=tag + "m" if kt == n_kt - 1 else None)
                    rr_wo.done(i, tag + "m")
                    if bias_sb is not None:
                        sc.op("vector", lambda e, ot=ot: e.tensor_scalar(
                            woT[:, :], pw[:, :], bias_sb[:, ot:ot + 1], None, ALU.add),
                            wait=[(tag + "m", sc.cur(tag + "m")), rr_woT.use(0)],
                            inc=tag + "e")
                    else:
                        sc.op("vector", lambda e, ot=ot: e.tensor_scalar(
                            woT[:, :], pw[:, :], scale_sb[:, ot:ot + 1], None, ALU.mult),
                            wait=[(tag + "m", sc.cur(tag + "m")), rr_woT.use(0)],
                            inc=tag + "e")
                    rr_pw.done(0, tag + "e")
                    for tt in range(4):
                        i2 = sc.cur(tag + "tr")
                        w3 = [(tag + "e", sc.cur(tag + "e")), rr_trf.use(i2)]
                        if ot == 0 and tt == 0:
                            w3.append(atok_guard)
                        sc.op("tensor", lambda e, tt=tt, i2=i2: e.transpose(
                            ptrf[i2 % 2][:, :], woT[:, tt * 128:(tt + 1) * 128],
                            idf[:, :]), wait=w3, inc=tag + "tr")
                        sc.op("scalar", lambda e, tt=tt, ot=ot, i2=i2: e.copy(
                            atok[tt][:, ot * 128:(ot + 1) * 128], ptrf[i2 % 2][:, :]),
                            wait=[(tag + "tr", sc.cur(tag + "tr"))], inc=tag + "te")
                        rr_trf.done(i2, tag + "te")
                    rr_woT.done(0, tag + "te")
                te_all = sc.cur(tag + "te")
                for tt in range(4):
                    sc.dma("gpsimd",
                           out_dram[cc * 512 + tt * 128: cc * 512 + (tt + 1) * 128, :],
                           atok[tt][:, :], "rda",
                           wait=([(tag + "te", te_all)] + (add_wait or [])),
                           accum_op=ALU.add)
                sc.flush(tag + f"_{cc}")

        proj_transpose_add(sc, w_o, 16,
                           lambda kt, cc: attnT[:, kt, cc * 512:(cc + 1) * 512],
                           x1_out[half * NOWN:(half + 1) * NOWN, :], "wo",
                       bias_sb=bias_o, add_wait=[("x1c", 32)])

        es3b.close()
        es25.close()
        es4 = ExitStack()
        T4 = lambda n, s, d: es4.enter_context(nc.sbuf_tensor(n, s, d, allow_name_mangling=True))
        P4s = lambda n, s, d: es4.enter_context(nc.psum_tensor(n, s, d, allow_name_mangling=True))
        woT = T4("woT4", [128, 512], F32)
        ptrf = [P4s(f"ptrf4_{i}", [128, 128], F32) for i in range(2)]
        pw = P4s("pw4", [128, 512], F32)
        rr_pw = RR(sc, 1)
        rr_trf = RR(sc, 2)
        rr_woT = RR(sc, 1)
        gub = [T4(f"gub{i}", [128, 16, 128], FP8) for i in range(4)]
        h_sb = T4("h_sb", [128, 48, NOWN], BF16)
        gtmp = es4.enter_context(nc.sbuf_tensor(
            "gtmp", [128, 512], BF16, side="right", allow_name_mangling=True))
        pg = P4s("pg", [128, 512], F32)
        pu = P4s("pu", [128, 512], F32)
        rr_gu = RR(sc, 2)
        rr_pg = RR(sc, 1)
        rr_pu = RR(sc, 1)
        rr_gt = RR(sc, 1)
        rr_ut = RR(sc, 1)
        for j in range(48):
            bw = j % 2
            sc.dma("gpsimd", gub[bw][:, :, :], kview(gw)[:, :, j * 128:(j + 1) * 128],
                   "guw", wait=[rr_gu.use(j)], max_dma_last_dim=4096)
            sc.dma("gpsimd", gub[2 + bw][:, :, :], kview(uw)[:, :, j * 128:(j + 1) * 128],
                   "guw", max_dma_last_dim=4096)
            for cc in range(2):
                for kt in range(16):
                    w = [("guw", sc.cur("guw")), rr_pg.use(0)] if kt == 0 else []
                    sc.op("tensor", lambda e, j=j, cc=cc, kt=kt, bw=bw: e.matmul(
                        pg[:, :], gub[bw][:, kt, :],
                        xnT_own[:, kt, cc * 512:(cc + 1) * 512],
                        start=(kt == 0), stop=(kt == 15)),
                        wait=w, inc="gm" if kt == 15 else None)
                for kt in range(16):
                    w = [rr_pu.use(0)] if kt == 0 else []
                    sc.op("tensor", lambda e, j=j, cc=cc, kt=kt, bw=bw: e.matmul(
                        pu[:, :], gub[2 + bw][:, kt, :],
                        xnT_own[:, kt, cc * 512:(cc + 1) * 512],
                        start=(kt == 0), stop=(kt == 15)),
                        wait=w, inc="um" if kt == 15 else None)
                sc.op("scalar", lambda e, j=j: e.activation(
                    gtmp[:, :], pg[:, :], AF.Silu, scale=gsc_sb[:, j:j + 1]),
                    wait=[("gm", sc.cur("gm")), rr_gt.use(0)], inc="gsl")
                rr_pg.done(0, "gsl")
                sc.op("vector", lambda e, j=j, cc=cc: e.tensor_scalar(
                    h_sb[:, j, cc * 512:(cc + 1) * 512], pu[:, :],
                    usc_sb[:, j:j + 1], None, ALU.mult),
                    wait=[("um", sc.cur("um"))], inc="uev")
                rr_pu.done(0, "uev")
                sc.op("vector", lambda e, j=j, cc=cc: e.tensor_mul(
                    h_sb[:, j, cc * 512:(cc + 1) * 512], gtmp[:, :],
                    h_sb[:, j, cc * 512:(cc + 1) * 512]),
                    wait=[("gsl", sc.cur("gsl")), ("uev", sc.cur("uev"))], inc="hc")
                rr_gt.done(0, "hc")
            rr_gu.done(j, "um")
        sc.flush("p4_gu_h" + str(half))
        dwb = [es4.enter_context(nc.sbuf_tensor(
            f"dwb{i}", [128, 48, 128], FP8, side="right",
            allow_name_mangling=True)) for i in range(2)]
        dtile = [T4(f"dtile{i}", [128, 128], F32) for i in range(2)]
        rr_dw = RR(sc, 2)
        rr_dt = RR(sc, 2)
        for ot in range(16):
            bw = ot % 2
            sc.dma("gpsimd", dwb[bw][:, :, :], kview(dw)[:, :, ot * 128:(ot + 1) * 128],
                   "dww", wait=[rr_dw.use(ot)], max_dma_last_dim=4096)
            for cc in range(2):
                for kt in range(48):
                    w = [("dww", sc.cur("dww")), rr_pw.use(0)] if kt == 0 else []
                    sc.op("tensor", lambda e, cc=cc, ot=ot, kt=kt, bw=bw: e.matmul(
                        pw[:, :], dwb[bw][:, kt, :],
                        h_sb[:, kt, cc * 512:(cc + 1) * 512],
                        start=(kt == 0), stop=(kt == 47)),
                        wait=w, inc="dm" if kt == 47 else None)
                sc.op("vector", lambda e, ot=ot: e.tensor_scalar(
                    woT[:, :], pw[:, :], dsc_sb[:, ot:ot + 1], None, ALU.mult),
                    wait=[("dm", sc.cur("dm")), rr_woT.use(0)], inc="de")
                rr_pw.done(0, "de")
                for tt in range(4):
                    i2 = sc.cur("ftr")
                    w3 = [("de", sc.cur("de")), rr_trf.use(i2)]
                    sc.op("tensor", lambda e, tt=tt, i2=i2: e.transpose(
                        ptrf[i2 % 2][:, :], woT[:, tt * 128:(tt + 1) * 128],
                        idf[:, :]), wait=w3, inc="ftr")
                    sc.op("scalar", lambda e, i2=i2: e.copy(
                        dtile[i2 % 2][:, :], ptrf[i2 % 2][:, :]),
                        wait=[("ftr", sc.cur("ftr")), rr_dt.use(i2)], inc="fte")
                    rr_trf.done(i2, "fte")
                    sc.dma("gpsimd",
                           x1_out[half * NOWN + cc * 512 + tt * 128:
                                  half * NOWN + cc * 512 + (tt + 1) * 128,
                                  ot * 128:(ot + 1) * 128],
                           dtile[i2 % 2][:, :], "rda",
                           wait=[("fte", sc.cur("fte"))], accum_op=ALU.add)
                    rr_dt.done(i2, "rda")
                rr_woT.done(0, "fte")
            rr_dw.done(ot, "dm")
        sc.flush(f"p4d_h{half}")

        es4.close()
        es25.close()

    esL.close()
    sc.op("gpsimd", lambda e: e.engine_nop(),
          wait=[("rda", sc.cur("rda")), ("x1c", 32),
                ("ksp", sc.cur("ksp")), ("vsp", sc.cur("vsp"))])
    sc.flush("l0_fin")


    esR.close()
    es.close()

    esX = ExitStack()
    TX = lambda n, s, d: esX.enter_context(nc.sbuf_tensor(n, s, d, allow_name_mangling=True))
    sidx = TX("sidx", [128, 16], U32)
    gidx = TX("gidx", [128, 8], U32)
    fidx = TX("fidx", [2, 1], U32)
    nnc = TX("nnc", [2, 1], U32)
    fpoll = TX("fpoll", [1, 2], U32)
    xtile = [TX(f"xtile{i}", [128, D], F32) for i in range(2)]
    sc.dma("sync", sidx[:, :], scat_idx, "xgl")
    sc.dma("sync", gidx[:, :], gath_idx, "xgl")
    sc.dma("sync", fidx[:, :], flag_idx, "xgl")
    sc.dma("sync", nnc[:, :], nonce_in, "xgl")
    for t in range(16):
        sc.dma("sync", xtile[t % 2][:, :],
               x1_out[t * 128:(t + 1) * 128, :], "xtl",
               wait=[("xsc", sc.cur("xsc") - 16)] if t >= 2 else [("xgl", sc.cur("xgl"))])
        sc.op("gpsimd", lambda e, t=t: e.indirect_dma_start(
            x1_sh[:, :], bass.IndirectOffsetOnAxis(sidx[:, t:t + 1], 0),
            xtile[t % 2][:, :], None),
            wait=[("xtl", sc.cur("xtl"))], inc="xsc", inc_amt=16)
    sc.op("gpsimd", lambda e: e.indirect_dma_start(
        shf[:, :], bass.IndirectOffsetOnAxis(fidx[0:2, 0:1], 0),
        nnc[:, :], None), wait=[("xsc", sc.cur("xsc"))], inc="xsc", inc_amt=16)
    sc.flush("x1_scatter")

    with nc.Block("x1_poll") as blk:
        @blk.gpsimd
        def _(g):
            with g.register("rn") as rn, g.register("done") as done, \
                 g.register("plc") as plc, g.register("f0") as f0, \
                 g.register("f1") as f1:
                g.wait_ge(sc.sem("xsc"), sc.cur("xsc"))
                g.reg_load(rn, nnc[0:1, 0:1])
                g.reg_mov(done, 0)
                g.reg_mov(plc, 0)
                with g.Fori(0, 400):
                    with g.If_ne(done, 1):
                        with g.Fori(0, 64):
                            with g.If_ne(done, 1):
                                g.dma_start(fpoll[0:1, 0:2],
                                            shf.rearrange("r c -> c (r)")[0:1, 0:2]
                                            ).then_inc(sc.sem("plf"), 16)
                                g.reg_add(plc, plc, 16)
                                g.wait_ge(sc.sem("plf"), plc)
                                g.reg_load(f0, fpoll[0:1, 0:1])
                                g.reg_load(f1, fpoll[0:1, 1:2])
                                with g.If_eq(f0, rn):
                                    with g.If_eq(f1, rn):
                                        g.reg_mov(done, 1)

    for t in range(8):
        sc.op("gpsimd", lambda e, t=t: e.indirect_dma_start(
            xtile[t % 2][:, :], None, x1_sh[:, :],
            bass.IndirectOffsetOnAxis(gidx[:, t:t + 1], 0)),
            wait=[("x2c", sc.cur("x2c") - 16)] if t >= 2 else [],
            inc="xga", inc_amt=16)
        sc.dma("sync", out_own[t * 128:(t + 1) * 128, :], xtile[t % 2][:, :],
               "x2c", wait=[("xga", sc.cur("xga"))])
    sc.flush("x1_gather")
    esX.close()

    x1_full = x1_sh
    cos1 = DT("cos1", [32, S], F32, EI)
    sin1 = DT("sin1", [32, S], F32, EI)
    cos1q = DT("cos1q", [32, NOWN], F32, EI)
    sin1q = DT("sin1q", [32, NOWN], F32, EI)
    w_qkvT = DT("w_qkvT", [D, 2560], F32, EI)
    w_oT = DT("w_oT", [D, D], F32, EI)
    gate_wT = DT("gate_wT", [D, 8], F32, EI)
    w_gate = DT("w_gate", [8, D, 1408], F32, EI)
    w_up = DT("w_up", [8, D, 1408], F32, EI)
    w_down = DT("w_down", [8, 1408, D], F32, EI)
    sw_gate = DT("sw_gate", [D, 2816], F32, EI)
    sw_up = DT("sw_up", [D, 2816], F32, EI)
    sw_down = DT("sw_down", [2816, D], F32, EI)

    es = ExitStack()
    esR = ExitStack()
    T = lambda n, s, d: es.enter_context(nc.sbuf_tensor(n, s, d, allow_name_mangling=True))
    TR = lambda n, s, d: esR.enter_context(
        nc.sbuf_tensor(n, s, d, side="right", allow_name_mangling=True))

    kmax = T("kmax", [1, NCH * 4], F32)
    Km = T("Km", [1, 4 * 4], F32)
    xn1T_own = T("xn1T_own", [128, 16, NOWN], BF16)
    c_row = T("c_row", [1, 8 * NOWN], BF16)
    msk = TR("msk", [128, 4 * 8 * CHUNK], BF16)
    k1T = TR("k1T", [128, 4, S], BF16)
    v1 = TR("v1", [128, 32, 4 * 65], BF16)

    sc.dma("sync", msk.ap().rearrange("p (a q) -> p a q", q=CHUNK),
           cmask.rearrange("s r p q -> p (s r) q"), "cld")

    sc.op("vector", lambda e: e.memset(k1T[64:65, :, :], 1.0))
    sc.op("vector", lambda e: e.memset(v1[:, :, :], 0.0))
    for g in range(4):
        sc.op("vector", lambda e, g=g: e.memset(
            v1[:, :, g * 65 + 64:g * 65 + 65], 1.0), inc="vset")
    sc.flush("q0")

    NT = 2
    esL = ExitStack()
    TL = lambda n, s, d: esL.enter_context(nc.sbuf_tensor(n, s, d, allow_name_mangling=True))
    PL = lambda n, s, d: esL.enter_context(nc.psum_tensor(n, s, d, allow_name_mangling=True))
    es1 = ExitStack()
    T1 = lambda n, s, d: es1.enter_context(nc.sbuf_tensor(n, s, d, allow_name_mangling=True))
    P1s = lambda n, s, d: es1.enter_context(nc.psum_tensor(n, s, d, allow_name_mangling=True))
    sq_junk = TL("sq_junk", [128, D], BF16)
    tr1 = TL("tr1", [32, CHUNK], F32)
    tr2 = TL("tr2", [32, CHUNK], F32)
    sqbuf = TL("sqbuf", [32, 2 * CHUNK], BF16)
    x1b = TL("x1b", [32, CHUNK], F32)
    x2b = TL("x2b", [32, CHUNK], F32)
    qrow = TL("qrow", [1, 512], F32)
    pproj = [PL(f"pproj{i}", [128, CHUNK], F32) for i in range(2)]
    pnrm = PL("pnrm", [1, 512], F32)
    xc = [T1(f"xc{i}", [128, NT, D], F32) for i in range(2)]
    xntok = [T1(f"xntok{i}", [128, NT, D], BF16) for i in range(2)]
    stt = T1("stt", [128, 8], F32)
    c1c = T1("c1c", [32, CHUNK], F32)
    s1c = T1("s1c", [32, CHUNK], F32)
    wkv_sb = T1("wkv_sb", [128, 16, 512], BF16)
    xnT_c = [T1(f"xnTc{i}", [128, 16, CHUNK], BF16) for i in range(2)]
    ptp = [P1s(f"ptp{i}", [128, 128], BF16) for i in range(2)]
    pv = P1s("pvv", [128, 512], F32)
    rr_tp = RR(sc, 2)
    rr_pp = RR(sc, 2)
    rr_xc = RR(sc, 2)
    rr_pv = RR(sc, 1)
    rr_pn = RR(sc, 1)
    rr_sqb = RR(sc, 1)
    sc.dma("gpsimd", wkv_sb[:, :, :], kview(w_qkvT)[:, :, 2048:2560], "wld")

    wait_start = [("cld", sc.cur("cld")), ("wld", sc.cur("wld")),
                  ("vset", sc.cur("vset"))]
    for c in range(NCH):
        bi = c % 2
        sc.dma("sync", xc[bi][:, :, :],
               x1_full.rearrange("(n t p) d -> n p t d", t=NT, p=128)[c], "xld",
               wait=(wait_start if c == 0 else [rr_xc.use(c)]))
        emit_ln(sc, xc[bi], xntok[bi], sq_junk, stt, ("xld", sc.cur("xld")), "L", epsb)
        rr_xc.done(c, "Lxn")
        emit_tposes(sc, xntok[bi],
                    lambda f, tt, bi=bi: xnT_c[bi][:, f, tt * 128:(tt + 1) * 128],
                    idb, ptp, rr_tp, "L")
        xt_done = sc.cur("Lte")
        sc.dma("sync", c1c[:, :], cos1[:, c * CHUNK:(c + 1) * CHUNK], "cs",
               wait=[("krp", sc.cur("krp"))])
        sc.dma("sync", s1c[:, :], sin1[:, c * CHUNK:(c + 1) * CHUNK], "cs")
        for g in range(4):
            i = sc.cur("km")
            for kt in range(16):
                w = [rr_pp.use(i)] if kt == 0 else []
                if kt == 0 and g == 0:
                    w.append(("Lte", xt_done))
                sc.op("tensor", lambda e, g=g, kt=kt, i=i, bi=bi: e.matmul(
                    pproj[i % 2][0:64, :], wkv_sb[:, kt, g * 64:(g + 1) * 64],
                    xnT_c[bi][:, kt, :], start=(kt == 0), stop=(kt == 15)),
                    wait=w, inc="km" if kt == 15 else None)
            sc.op("vector", lambda e, i=i: e.tensor_copy(x1b[:, :], pproj[i % 2][0:32, :]),
                  wait=[("km", sc.cur("km")), ("krp", sc.cur("krp"))])
            sc.op("vector", lambda e, i=i: e.tensor_copy(x2b[:, :], pproj[i % 2][32:64, :]),
                  inc="kre")
            rr_pp.done(i, "kre")
            sc.op("scalar", lambda e: e.activation(
                sqbuf[:, 0:CHUNK], x1b[:, :], AF.Square),
                wait=[("kre", sc.cur("kre")), rr_sqb.use(0)], inc="ksq")
            sc.op("scalar", lambda e: e.activation(
                sqbuf[:, CHUNK:2 * CHUNK], x2b[:, :], AF.Square), inc="ksq")
            sc.op("tensor", lambda e: e.matmul(
                pnrm[0:1, 0:CHUNK], onesc[0:32, :], sqbuf[:, 0:CHUNK],
                start=True, stop=False),
                wait=[("ksq", sc.cur("ksq")), rr_pn.use(0)])
            sc.op("tensor", lambda e: e.matmul(
                pnrm[0:1, 0:CHUNK], onesc[0:32, :], sqbuf[:, CHUNK:2 * CHUNK],
                start=False, stop=True), inc="nm")
            rr_sqb.done(0, "nm")
            sc.op("vector", lambda e, g=g, c=c: e.tensor_reduce(
                kmax[0:1, c * 4 + g:c * 4 + g + 1], pnrm[0:1, 0:CHUNK],
                axis=AX.X, op=ALU.max), wait=[("nm", sc.cur("nm"))], inc="nrd")
            rr_pn.done(0, "nrd")
            emit_rope(sc, (tr1, tr2), x1b[:, :], x2b[:, :],
                      k1T[0:32, g, c * CHUNK:(c + 1) * CHUNK],
                      k1T[32:64, g, c * CHUNK:(c + 1) * CHUNK],
                      c1c[0:32, :], s1c[0:32, :], "krp",
                      wait=[("kre", sc.cur("kre")), ("cs", sc.cur("cs"))])
        for ktk in range(NT):
            for kt in range(16):
                w = [rr_pv.use(0)] if kt == 0 else []
                sc.op("tensor", lambda e, ktk=ktk, kt=kt, bi=bi: e.matmul(
                    pv[:, 0:256], xnT_c[bi][:, kt, ktk * 128:(ktk + 1) * 128],
                    wkv_sb[:, kt, 256:512], start=(kt == 0), stop=(kt == 15)),
                    wait=w, inc="vm" if kt == 15 else None)
            sc.op("vector", lambda e, ktk=ktk, c=c: e.tensor_copy(
                v1.ap().rearrange("p t (g d) -> p t g d", d=65)[:, 2 * c + ktk, :, 0:64],
                pv.ap()[:, 0:256].rearrange("p (g d) -> p g d", d=64)),
                wait=[("vm", sc.cur("vm"))], inc="ve")
            rr_pv.done(0, "ve")
    sc.flush("q1_end")
    for g in range(4):
        for s_ in range(4):
            npre = 4 * (s_ + 1)
            sc.op("vector", lambda e, g=g, s_=s_, npre=npre: e.tensor_reduce(
                Km[0:1, g * 4 + s_: g * 4 + s_ + 1],
                kmax.ap().rearrange("p (c h) -> p h c", h=4)[0:1, g, 0:npre],
                axis=AX.X, op=ALU.max), inc="kmf")
    sc.op("scalar", lambda e: e.activation(Km[0:1, :], Km[0:1, :], AF.Sqrt),
          wait=[("kmf", sc.cur("kmf"))], inc="kms")
    sc.op("vector", lambda e: e.tensor_scalar(
        Km[0:1, :], Km[0:1, :], -1.0, None, ALU.mult),
          wait=[("kms", sc.cur("kms"))], inc="kmn")
    sc.flush("q1_km")

    es2 = ExitStack()
    T2 = lambda n, s, d: es2.enter_context(nc.sbuf_tensor(n, s, d, allow_name_mangling=True))
    P2s = lambda n, s, d: es2.enter_context(nc.psum_tensor(n, s, d, allow_name_mangling=True))
    gwT_sb = T2("gwT_sb", [128, 16, 8], BF16)
    glog = T2("glog", [128, 8], F32)
    gst = T2("gst", [128, 4], F32)
    comb = T2("comb", [128, 8], F32)
    ptiny = P2s("ptiny", [1, 128], F32)
    pgl = P2s("pgl", [128, 8], F32)
    rr_pgl = RR(sc, 1)
    rr_pti = RR(sc, 1)
    sc.dma("gpsimd", gwT_sb[:, :, :], kview(gate_wT), "wld")
    wait_q1b = [("wld", sc.cur("wld"))]
    for oc in range(4):
        bi = oc % 2
        sc.dma("sync", xc[bi][:, :, :],
               out_own_rd.rearrange("(n t p) d -> n p t d", t=NT, p=128)[oc], "xld",
               wait=[rr_xc.use(oc)] + (wait_q1b if oc == 0 else []))
        emit_ln(sc, xc[bi], xntok[bi], sq_junk, stt, ("xld", sc.cur("xld")), "L", epsb)
        rr_xc.done(oc, "Lxn")
        emit_tposes(sc, xntok[bi],
                    lambda f, tt, oc=oc: xn1T_own[:, f, oc * CHUNK + tt * 128:
                                                  oc * CHUNK + (tt + 1) * 128],
                    idb, ptp, rr_tp, "L")
    sc.flush("q1b_ln")
    xt_own = sc.cur("Lte")
    for tt in range(8):
        for kt in range(16):
            w = [rr_pgl.use(0)] if kt == 0 else []
            if kt == 0 and tt == 0:
                w.append(("Lte", xt_own))
            sc.op("tensor", lambda e, tt=tt, kt=kt: e.matmul(
                pgl[:, :], xn1T_own[:, kt, tt * 128:(tt + 1) * 128],
                gwT_sb[:, kt, :], start=(kt == 0), stop=(kt == 15)),
                wait=w, inc="glm" if kt == 15 else None)
        def gch(fn, extra=(), name=None):
            w = [("gch", sc.cur("gch"))] + list(extra)
            sc.op("vector", fn, wait=w, inc=(name or "gch"))
        gch(lambda e: e.reduce_max(
            gst[:, 1:2], pgl[:, :], axis=AX.X, negate=True),
            extra=[("glm", sc.cur("glm"))], name="gmx")
        sc.op("scalar", lambda e: e.activation(
            glog[:, :], pgl[:, :], AF.Exp, bias=gst[:, 1:2]),
            wait=[("gmx", sc.cur("gmx"))], inc="gex")
        rr_pgl.done(0, "gex")
        gch(lambda e: e.reduce_sum(gst[:, 2:3], glog[:, :], axis=AX.X),
            extra=[("gex", sc.cur("gex")), ("gmx", sc.cur("gmx"))])
        gch(lambda e: e.reciprocal(gst[:, 3:4], gst[:, 2:3]))
        gch(lambda e: e.tensor_scalar(
            glog[:, :], glog[:, :], gst[:, 3:4], None, ALU.mult))
        gch(lambda e: e.reduce_max(gst[:, 0:1], glog[:, :], axis=AX.X))
        gch(lambda e: e.tensor_scalar(
            comb[:, :], glog[:, :], gst[:, 0:1], None, ALU.is_ge))
        gch(lambda e: e.tensor_scalar(
            comb[:, :], comb[:, :], -2.0, None, ALU.mult))
        gch(lambda e: e.tensor_add(comb[:, :], comb[:, :], glog[:, :]))
        gch(lambda e: e.reduce_max(gst[:, 1:2], comb[:, :], axis=AX.X))
        gch(lambda e: e.tensor_scalar(
            comb[:, :], glog[:, :], gst[:, 1:2], None, ALU.is_ge))
        gch(lambda e: e.tensor_mul(comb[:, :], comb[:, :], glog[:, :]),
            name="gcb")
        sc.op("vector", lambda e: e.engine_nop(),
              wait=[("gcb", sc.cur("gcb"))], inc="gch")
        for e_ in range(8):
            sc.op("tensor", lambda e, e_=e_: e.transpose(
                ptiny[0:1, :], comb[:, e_:e_ + 1], idf[:, :]),
                wait=[("gcb", sc.cur("gcb")), rr_pti.use(0)], inc="gtr")
            sc.op("vector", lambda e, e_=e_, tt=tt: e.tensor_copy(
                c_row[0:1, e_ * NOWN + tt * 128: e_ * NOWN + (tt + 1) * 128],
                ptiny[0:1, :]), wait=[("gtr", sc.cur("gtr"))], inc="gte")
            rr_pti.done(0, "gte")
    sc.flush("q1c")
    es2.close()
    es1.close()

    esR2 = ExitStack()
    attnT = esR2.enter_context(
        nc.sbuf_tensor("attnT_l1", [128, 16, NOWN], BF16, side="right"))
    esR3 = ExitStack()
    q1x = esR3.enter_context(
        nc.sbuf_tensor("q1x", [128, 16 * NOWN], BF16, side="right", allow_name_mangling=True))
    es2b = ExitStack()
    T2b = lambda n, s, d: es2b.enter_context(nc.sbuf_tensor(n, s, d, allow_name_mangling=True))
    P2b = lambda n, s, d: es2b.enter_context(nc.psum_tensor(n, s, d, allow_name_mangling=True))
    c1q = T2b("c1q", [32, NOWN], F32)
    s1q = T2b("s1q", [32, NOWN], F32)
    wq_sb = [T2b("wq_sb0", [128, 16, 128], BF16)]
    ptsb = [T2b(f"ptsb{i}", [128, 2 * CHUNK], BF16) for i in range(2)]
    lsb = T2b("lsb", [1, 2 * CHUNK], F32)
    rbsb = T2b("rbsb", [128, 2 * CHUNK], F32)
    psc = [P2b(f"psc{i}", [128, 2 * CHUNK], F32) for i in range(2)]
    pout = [P2b(f"pout{i}", [65, 2 * CHUNK], F32) for i in range(2)]
    pbc = P2b("pbc", [128, 2 * CHUNK], F32)
    rr_wq = RR(sc, 1)
    sc.dma("sync", c1q[:, :], cos1q, "cld")
    sc.dma("sync", s1q[:, :], sin1q, "cld")

    def q2_qproj(gh):
        g = gh // 8
        hl = gh % 16
        bw = 0
        if gh % 2 == 0:
            sc.dma("gpsimd", wq_sb[bw][:, :, :],
                   kview(w_qkvT)[:, :, gh * 64:(gh + 2) * 64], "qw",
                   wait=[rr_wq.use(0)])
        for oc in range(4):
            i = sc.cur("qm")
            for kt in range(16):
                w = [("qw", sc.cur("qw")), rr_pp.use(i)] if kt == 0 else []
                sc.op("tensor", lambda e, gh=gh, oc=oc, kt=kt, bw=bw, i=i: e.matmul(
                    pproj[i % 2][0:64, :],
                    wq_sb[bw][:, kt, (gh % 2) * 64:(gh % 2) * 64 + 64],
                    xn1T_own[:, kt, oc * CHUNK:(oc + 1) * CHUNK],
                    start=(kt == 0), stop=(kt == 15)),
                    wait=w, inc="qm" if kt == 15 else None)
            sc.op("vector", lambda e, i=i: e.tensor_copy(x1b[:, :], pproj[i % 2][0:32, :]),
                  wait=[("qm", sc.cur("qm")), ("qrp", sc.cur("qrp"))])
            sc.op("vector", lambda e, i=i: e.tensor_copy(x2b[:, :], pproj[i % 2][32:64, :]),
                  inc="qre")
            rr_pp.done(i, "qre")
            qc = oc * (16 * CHUNK) + hl * CHUNK
            sc.op("scalar", lambda e: e.activation(
                sqbuf[:, 0:CHUNK], x1b[:, :], AF.Square),
                wait=[("qre", sc.cur("qre")), rr_sqb.use(0)], inc="qsq")
            sc.op("scalar", lambda e: e.activation(
                sqbuf[:, CHUNK:2 * CHUNK], x2b[:, :], AF.Square), inc="qsq")
            sc.op("tensor", lambda e: e.matmul(
                pnrm[0:1, 0:CHUNK], onesc[0:32, :], sqbuf[:, 0:CHUNK],
                start=True, stop=False),
                wait=[("qsq", sc.cur("qsq")), rr_pn.use(0)])
            sc.op("tensor", lambda e: e.matmul(
                pnrm[0:1, 0:CHUNK], onesc[0:32, :], sqbuf[:, CHUNK:2 * CHUNK],
                start=False, stop=True), inc="qnm")
            rr_sqb.done(0, "qnm")
            sc.op("scalar", lambda e: e.activation(
                qrow[0:1, 0:CHUNK], pnrm[0:1, 0:CHUNK], AF.Sqrt),
                wait=[("qnm", sc.cur("qnm"))], inc="qrt")
            rr_pn.done(0, "qrt")
            s_ = oc
            sc.op("vector", lambda e, g=g, s_=s_, qc=qc: e.tensor_scalar(
                q1x[64:65, qc:qc + CHUNK], qrow[0:1, 0:CHUNK],
                Km[0:1, g * 4 + s_:g * 4 + s_ + 1], COFF / SC1,
                ALU.mult, ALU.add),
                wait=[("qrt", sc.cur("qrt")), ("kmn", sc.cur("kmn"))],
                inc="brow")
            emit_rope(sc, (tr1, tr2), x1b[:, :], x2b[:, :],
                      q1x[0:32, qc:qc + CHUNK], q1x[32:64, qc:qc + CHUNK],
                      c1q[0:32, oc * CHUNK:(oc + 1) * CHUNK],
                      s1q[0:32, oc * CHUNK:(oc + 1) * CHUNK], "qrp",
                      wait=[("qre", sc.cur("qre"))])
        if gh % 2 == 1:
            rr_wq.done(0, "qm")

    rr_scQ, rr_ptQ, rr_slotQ = RR(sc, 2), RR(sc, 2), RR(sc, 2)

    def q2_att_pair(p):
        g = (2 * p) // 8
        hl0 = (2 * p) % 16
        for s_ in range(4):
            ntile = PADT[s_]
            so = sc.cur("aev")
            ob = so % 2
            rdy = {}
            qcol = s_ * (16 * CHUNK) + hl0 * CHUNK

            def scores(t, g=g, qcol=qcol):
                i = sc.cur("scm")
                w = [rr_scQ.use(i)]
                sc.op("tensor", lambda e, t=t, i=i: e.matmul(
                    psc[i % 2][:, :], k1T[0:65, g, t * 128:(t + 1) * 128],
                    q1x[0:65, qcol:qcol + 2 * CHUNK],
                    start=True, stop=True), wait=w, inc="scm")

            def expmask(t, s_=s_, ntile=ntile):
                i = sc.cur("scm") - 1
                j = sc.cur("ex")
                w = [("scm", sc.cur("scm")), rr_ptQ.use(j)]
                sc.op("scalar", lambda e, i=i, j=j: e.activation(
                    ptsb[j % 2][:, :], psc[i % 2][:, :], AF.Exp, scale=SC1),
                    wait=w, inc="ex")
                rr_scQ.done(i, "ex")
                if t >= ntile - 8:
                    rt = t - (ntile - 8)
                    for hx in range(2):
                        sc.op("vector", lambda e, j=j, rt=rt, s_=s_, hx=hx: e.tensor_mul(
                            ptsb[j % 2][:, hx * CHUNK:(hx + 1) * CHUNK],
                            ptsb[j % 2][:, hx * CHUNK:(hx + 1) * CHUNK],
                            msk[:, (s_ * 8 + rt) * CHUNK:(s_ * 8 + rt + 1) * CHUNK]),
                            wait=[("ex", sc.cur("ex"))] if hx == 0 else (), inc="mk")
                    return j, ("mk", sc.cur("mk"))
                return j, ("ex", sc.cur("ex"))

            def pvl(t, g=g, ntile=ntile, so=so, ob=ob):
                j, rd = rdy[t]
                w = [rd]
                if t == 0:
                    w.append(rr_slotQ.use(so))
                sc.op("tensor", lambda e, t=t, j=j, ob=ob: e.matmul(
                    pout[ob][:, :], v1[:, t, g * 65:(g + 1) * 65], ptsb[j % 2][:, :],
                    start=(t == 0), stop=(t == ntile - 1)), wait=w, inc="pvd")
                rr_ptQ.done(j, "pvd")

            scores(0)
            rdy[0] = expmask(0)
            scores(1)
            rdy[1] = expmask(1)
            for t in range(ntile):
                pvl(t)
                if t + 2 < ntile:
                    scores(t + 2)
                    rdy[t + 2] = expmask(t + 2)
            sc.op("vector", lambda e, ob=ob: e.tensor_copy(
                lsb[0:1, :], pout[ob][64:65, :]),
                wait=[("pvd", sc.cur("pvd"))], inc="lcp")
            sc.op("vector", lambda e: e.reciprocal(lsb[0:1, :], lsb[0:1, :]),
                  wait=[("lcp", sc.cur("lcp"))], inc="rc")
            sc.op("tensor", lambda e: e.matmul(
                pbc[:, :], ones1f[:, :], lsb[0:1, :], start=True, stop=True),
                wait=[("rc", sc.cur("rc")), ("rbu", sc.cur("rbu"))], inc="bc")
            sc.op("vector", lambda e: e.tensor_copy(rbsb[:, :], pbc[:, :]),
                  wait=[("bc", sc.cur("bc"))], inc="rbu")
            for hx in range(2):
                sc.op("vector", lambda e, hx=hx, s_=s_, p=p, ob=ob: e.tensor_mul(
                    attnT[64 * hx:64 * hx + 64, p, s_ * CHUNK:(s_ + 1) * CHUNK],
                    pout[ob][0:64, hx * CHUNK:(hx + 1) * CHUNK],
                    rbsb[0:64, hx * CHUNK:(hx + 1) * CHUNK]),
                    wait=[("rbu", sc.cur("rbu"))] if hx == 0 else (),
                    inc="aev" if hx == 1 else None)
            rr_slotQ.done(so, "aev")

    for p in range(16):
        q2_qproj(2 * p)
        q2_qproj(2 * p + 1)
        if p >= 1:
            q2_att_pair(p - 1)
    q2_att_pair(15)
    sc.flush("q2")
    es2b.close()
    esR3.close()
    esL.close()

    es3 = ExitStack()
    T3 = lambda n, s, d: es3.enter_context(nc.sbuf_tensor(n, s, d, allow_name_mangling=True))
    P3s = lambda n, s, d: es3.enter_context(nc.psum_tensor(n, s, d, allow_name_mangling=True))
    wo_sb = [T3(f"wo_sb{i}", [128, 16, 128], BF16) for i in range(2)]
    woT = T3("woT", [128, 512], F32)
    atok = [T3(f"atok{i}", [128, D], F32) for i in range(4)]
    ptrf = [P3s(f"ptrf{i}", [128, 128], F32) for i in range(2)]
    pw3 = P3s("pw3", [128, 512], F32)
    rr_wo = RR(sc, 2)
    rr_pw = RR(sc, 1)
    rr_trf = RR(sc, 2)
    rr_woT = RR(sc, 1)
    for cc in range(2):
        atok_guard = ("rda", sc.cur("rda"))
        for ot in range(16):
            bw = ot % 2
            i = cc * 16 + ot
            sc.dma("gpsimd", wo_sb[bw][:, :, :],
                   kview(w_oT)[:, :, ot * 128:(ot + 1) * 128], "wow",
                   wait=[rr_wo.use(i)])
            for kt in range(16):
                w = [("wow", sc.cur("wow")), rr_pw.use(0)] if kt == 0 else []
                sc.op("tensor", lambda e, cc=cc, ot=ot, kt=kt, bw=bw: e.matmul(
                    pw3[:, :], wo_sb[bw][:, kt, :],
                    attnT[:, kt, cc * 512:(cc + 1) * 512],
                    start=(kt == 0), stop=(kt == 15)),
                    wait=w, inc="wm" if kt == 15 else None)
            rr_wo.done(i, "wm")
            sc.op("vector", lambda e: e.tensor_copy(woT[:, :], pw3[:, :]),
                  wait=[("wm", sc.cur("wm")), rr_woT.use(0)], inc="we")
            rr_pw.done(0, "we")
            for tt in range(4):
                i2 = sc.cur("wtr")
                w3 = [("we", sc.cur("we")), rr_trf.use(i2)]
                if ot == 0 and tt == 0:
                    w3.append(atok_guard)
                sc.op("tensor", lambda e, tt=tt, i2=i2: e.transpose(
                    ptrf[i2 % 2][:, :], woT[:, tt * 128:(tt + 1) * 128],
                    idf[:, :]), wait=w3, inc="wtr")
                sc.op("scalar", lambda e, tt=tt, ot=ot, i2=i2: e.copy(
                    atok[tt][:, ot * 128:(ot + 1) * 128], ptrf[i2 % 2][:, :]),
                    wait=[("wtr", sc.cur("wtr"))], inc="wte")
                rr_trf.done(i2, "wte")
            rr_woT.done(0, "wte")
        te_all = sc.cur("wte")
        for tt in range(4):
            sc.dma("gpsimd",
                   out_own[cc * 512 + tt * 128: cc * 512 + (tt + 1) * 128, :],
                   atok[tt][:, :], "rda",
                   wait=[("wte", te_all), ("x2c", sc.cur("x2c"))], accum_op=ALU.add)
        sc.flush(f"q3_{cc}")
    es3.close()
    esR2.close()
    esR.close()

    es4 = ExitStack()
    T4 = lambda n, s, d: es4.enter_context(nc.sbuf_tensor(n, s, d, allow_name_mangling=True))
    P4s = lambda n, s, d: es4.enter_context(nc.psum_tensor(n, s, d, allow_name_mangling=True))
    ySB = T4("ySB", [128, 16, NOWN], F32)
    es5 = ExitStack()
    T5 = lambda n, s, d: es5.enter_context(nc.sbuf_tensor(n, s, d, allow_name_mangling=True))
    P5s = lambda n, s, d: es5.enter_context(nc.psum_tensor(n, s, d, allow_name_mangling=True))
    hbig = T5("hbig", [128, 22, NOWN], BF16)
    web = [T5(f"web{i}", [128, 16, 128], BF16) for i in range(4)]
    cb = T5("cb", [128, NOWN], F32)
    gtmp = T5("gtmp", [128, 512], BF16)
    pg = P5s("pg", [128, 512], F32)
    pu = P5s("pu", [128, 512], F32)
    pcb = P5s("pcb", [128, 512], F32)
    pw4 = P5s("pw4", [128, 512], F32)
    h_e = [hbig[:, 0:11, :], hbig[:, 11:22, :]]
    rr_we = RR(sc, 2)
    rr_pg = RR(sc, 1)
    rr_pu = RR(sc, 1)
    rr_gt = RR(sc, 1)
    rr_pcb = RR(sc, 1)
    rr_pw = RR(sc, 1)
    rr_dwe = RR(sc, 2)
    gucnt = [0]
    for pair in range(4):
        for ei in range(2):
            e_ = pair * 2 + ei
            for hf in range(2):
                sc.op("tensor", lambda e, e_=e_, hf=hf: e.matmul(
                    pcb[:, :], ones1b[:, :],
                    c_row[0:1, e_ * NOWN + hf * 512: e_ * NOWN + (hf + 1) * 512],
                    start=True, stop=True),
                    wait=[rr_pcb.use(0), ("uevm", sc.cur("uevm"))], inc="cbm")
                sc.op("vector", lambda e, hf=hf: e.tensor_copy(
                    cb[:, hf * 512:(hf + 1) * 512], pcb[:, :]),
                    wait=[("cbm", sc.cur("cbm"))], inc="cbe")
                rr_pcb.done(0, "cbe")
            for j in range(11):
                bw = gucnt[0] % 2
                sc.dma("gpsimd", web[bw][:, :, :],
                       kview(w_gate[e_])[:, :, j * 128:(j + 1) * 128], "guw",
                       wait=[rr_we.use(gucnt[0])])
                sc.dma("gpsimd", web[2 + bw][:, :, :],
                       kview(w_up[e_])[:, :, j * 128:(j + 1) * 128], "guw")
                for cc in range(2):
                    for kt in range(16):
                        w = [("guw", sc.cur("guw")), rr_pg.use(0)] if kt == 0 else []
                        sc.op("tensor", lambda e, kt=kt, cc=cc, bw=bw: e.matmul(
                            pg[:, :], web[bw][:, kt, :],
                            xn1T_own[:, kt, cc * 512:(cc + 1) * 512],
                            start=(kt == 0), stop=(kt == 15)),
                            wait=w, inc="gm" if kt == 15 else None)
                    for kt in range(16):
                        w = [rr_pu.use(0)] if kt == 0 else []
                        sc.op("tensor", lambda e, kt=kt, cc=cc, bw=bw: e.matmul(
                            pu[:, :], web[2 + bw][:, kt, :],
                            xn1T_own[:, kt, cc * 512:(cc + 1) * 512],
                            start=(kt == 0), stop=(kt == 15)),
                            wait=w, inc="um" if kt == 15 else None)
                    sc.op("scalar", lambda e: e.activation(
                        gtmp[:, :], pg[:, :], AF.Silu),
                        wait=[("gm", sc.cur("gm")), rr_gt.use(0),
                              ("cbe", sc.cur("cbe"))], inc="gsl")
                    rr_pg.done(0, "gsl")
                    sc.op("vector", lambda e, cc=cc, j=j, ei=ei: e.tensor_tensor(
                        h_e[ei][:, j, cc * 512:(cc + 1) * 512], pu[:, :],
                        cb[:, cc * 512:(cc + 1) * 512], ALU.mult),
                        wait=[("um", sc.cur("um"))], inc="uevm")
                    rr_pu.done(0, "uevm")
                    sc.op("vector", lambda e, cc=cc, j=j, ei=ei: e.tensor_mul(
                        h_e[ei][:, j, cc * 512:(cc + 1) * 512], gtmp[:, :],
                        h_e[ei][:, j, cc * 512:(cc + 1) * 512]),
                        wait=[("gsl", sc.cur("gsl"))], inc="hc")
                    rr_gt.done(0, "hc")
                rr_we.done(gucnt[0], "um")
                gucnt[0] += 1
        sc.flush(f"q4g_{pair}")
        for ot in range(16):
            for ei in range(2):
                e_ = pair * 2 + ei
                sc.dma("gpsimd", web[ei][:, 0:11, :],
                       kview(w_down[e_])[:, :, ot * 128:(ot + 1) * 128], "dww",
                       wait=[rr_dwe.use(ot * 2 + ei)])
            for cc in range(2):
                for ei in range(2):
                    for kt in range(11):
                        w = []
                        if ei == 0 and kt == 0:
                            w = [("dww", sc.cur("dww")), rr_pw.use(0)]
                        sc.op("tensor", lambda e, kt=kt, cc=cc, ei=ei: e.matmul(
                            pw4[:, :], web[ei][:, kt, :],
                            h_e[ei][:, kt, cc * 512:(cc + 1) * 512],
                            start=(kt == 0 and ei == 0),
                            stop=(kt == 10 and ei == 1)),
                            wait=w, inc="dm" if (kt == 10 and ei == 1) else None)
                if pair == 0:
                    sc.op("vector", lambda e, ot=ot, cc=cc: e.tensor_copy(
                        ySB[:, ot, cc * 512:(cc + 1) * 512], pw4[:, :]),
                        wait=[("dm", sc.cur("dm"))], inc="de")
                else:
                    sc.op("vector", lambda e, ot=ot, cc=cc: e.tensor_add(
                        ySB[:, ot, cc * 512:(cc + 1) * 512],
                        ySB[:, ot, cc * 512:(cc + 1) * 512], pw4[:, :]),
                        wait=[("dm", sc.cur("dm"))], inc="de")
                rr_pw.done(0, "de")
            rr_dwe.done(ot * 2, "dm")
            rr_dwe.done(ot * 2 + 1, "dm")
        sc.flush(f"q4d_{pair}")

    for j in range(22):
        bw = j % 2
        sc.dma("gpsimd", web[bw][:, :, :],
               kview(sw_gate)[:, :, j * 128:(j + 1) * 128], "guw",
               wait=[rr_we.use(j)])
        sc.dma("gpsimd", web[2 + bw][:, :, :],
               kview(sw_up)[:, :, j * 128:(j + 1) * 128], "guw")
        for cc in range(2):
            for kt in range(16):
                w = [("guw", sc.cur("guw")), rr_pg.use(0)] if kt == 0 else []
                sc.op("tensor", lambda e, kt=kt, cc=cc, bw=bw: e.matmul(
                    pg[:, :], web[bw][:, kt, :],
                    xn1T_own[:, kt, cc * 512:(cc + 1) * 512],
                    start=(kt == 0), stop=(kt == 15)),
                    wait=w, inc="gm" if kt == 15 else None)
            for kt in range(16):
                w = [rr_pu.use(0)] if kt == 0 else []
                sc.op("tensor", lambda e, kt=kt, cc=cc, bw=bw: e.matmul(
                    pu[:, :], web[2 + bw][:, kt, :],
                    xn1T_own[:, kt, cc * 512:(cc + 1) * 512],
                    start=(kt == 0), stop=(kt == 15)),
                    wait=w, inc="um" if kt == 15 else None)
            sc.op("scalar", lambda e: e.activation(gtmp[:, :], pg[:, :], AF.Silu),
                  wait=[("gm", sc.cur("gm")), rr_gt.use(0)], inc="gsl")
            rr_pg.done(0, "gsl")
            sc.op("vector", lambda e, cc=cc, j=j: e.tensor_copy(
                hbig[:, j, cc * 512:(cc + 1) * 512], pu[:, :]),
                wait=[("um", sc.cur("um"))], inc="uevm")
            rr_pu.done(0, "uevm")
            sc.op("vector", lambda e, cc=cc, j=j: e.tensor_mul(
                hbig[:, j, cc * 512:(cc + 1) * 512], gtmp[:, :],
                hbig[:, j, cc * 512:(cc + 1) * 512]),
                wait=[("gsl", sc.cur("gsl"))], inc="hc")
            rr_gt.done(0, "hc")
        rr_we.done(j, "um")
        if j % 8 == 7:
            sc.flush(f"q4s_{j}")
    sc.flush("q4_sh")
    rr_sd1 = RR(sc, 2)
    rr_sd2 = RR(sc, 2)
    for cc in range(2):
        for ot in range(16):
            bw = (cc * 16 + ot) % 2
            i = cc * 16 + ot
            sc.dma("gpsimd", web[bw][:, :, :],
                   kview(sw_down)[:, 0:16, ot * 128:(ot + 1) * 128], "sdw",
                   wait=[rr_sd1.use(i)])
            sc.dma("gpsimd", web[2 + bw][:, 0:6, :],
                   kview(sw_down)[:, 16:22, ot * 128:(ot + 1) * 128], "sdw",
                   wait=[rr_sd2.use(i)])
            for kt in range(22):
                w = [("sdw", sc.cur("sdw")), rr_pw.use(0)] if kt == 0 else []
                wsrc = web[bw] if kt < 16 else web[2 + bw]
                ktl = kt if kt < 16 else kt - 16
                sc.op("tensor", lambda e, kt=kt, cc=cc, wsrc=wsrc, ktl=ktl: e.matmul(
                    pw4[:, :], wsrc[:, ktl, :], hbig[:, kt, cc * 512:(cc + 1) * 512],
                    start=(kt == 0), stop=(kt == 21)),
                    wait=w, inc="sm" if kt == 21 else None)
            rr_sd1.done(i, "sm")
            rr_sd2.done(i, "sm")
            sc.op("vector", lambda e, ot=ot, cc=cc: e.tensor_add(
                ySB[:, ot, cc * 512:(cc + 1) * 512],
                ySB[:, ot, cc * 512:(cc + 1) * 512], pw4[:, :]),
                wait=[("sm", sc.cur("sm"))], inc="de")
            rr_pw.done(0, "de")
    sc.flush("q4_sd")
    es5.close()
    es6 = ExitStack()
    T6 = lambda n, s, d: es6.enter_context(
        nc.sbuf_tensor(n, s, d, side="right"))
    P6s = lambda n, s, d: es6.enter_context(nc.psum_tensor(n, s, d, allow_name_mangling=True))
    atok = [T6(f"atok2_{i}", [128, D], F32) for i in range(4)]
    ptrf = [P6s(f"ptrf2_{i}", [128, 128], F32) for i in range(2)]
    rr_trf = RR(sc, 2)
    for cc in range(2):
        atok_guard = ("rda", sc.cur("rda"))
        for ot in range(16):
            for tt in range(4):
                i2 = sc.cur("ytr")
                w3 = [rr_trf.use(i2)]
                if ot == 0 and tt == 0:
                    w3.append(atok_guard)
                sc.op("tensor", lambda e, ot=ot, tt=tt, cc=cc, i2=i2: e.transpose(
                    ptrf[i2 % 2][:, :],
                    ySB[:, ot, cc * 512 + tt * 128: cc * 512 + (tt + 1) * 128],
                    idf[:, :]), wait=w3, inc="ytr")
                sc.op("scalar", lambda e, tt=tt, ot=ot, i2=i2: e.copy(
                    atok[tt][:, ot * 128:(ot + 1) * 128], ptrf[i2 % 2][:, :]),
                    wait=[("ytr", sc.cur("ytr"))], inc="yte")
                rr_trf.done(i2, "yte")
        te_all = sc.cur("yte")
        for tt in range(4):
            sc.dma("gpsimd",
                   out_own[cc * 512 + tt * 128: cc * 512 + (tt + 1) * 128, :],
                   atok[tt][:, :], "rda", wait=[("yte", te_all)],
                   accum_op=ALU.add)
        sc.flush(f"q5_{cc}")
    sc.op("gpsimd", lambda e: e.engine_nop(),
          wait=[("rda", sc.cur("rda")), ("x2c", sc.cur("x2c"))])
    sc.flush("fused_fin")
    es6.close()
    es4.close()
    es.close()
    esG.close()
    return nc




def zig_chunks(par):
    return [par, 15 - par, 2 + par, 13 - par, 4 + par, 11 - par, 6 + par, 9 - par]


def build_masks_l0(par):
    m = np.zeros((8, 4, 128, CHUNK), np.float32)
    zz = zig_chunks(par)
    kk = np.arange(128)[:, None]
    qq = np.arange(CHUNK)[None, :]
    for s in range(8):
        for rt in range(4):
            t = PADL0_ALL[s] - 4 + rt
            m[s, rt] = ((128 * t + kk) <= (CHUNK * zz[s] + qq)).astype(np.float32)
    return m.astype(mld.bfloat16)


def kernel(**inp):
    cos0, sin0 = rope_tables_L0()
    cos1, sin1 = rope_tables_L1()
    ident = np.eye(128, dtype=mld.bfloat16)
    wrq = nb(inp["L0_attn_w_rq"])
    brq = nb(inp["L0_attn_b_rq"])
    wrq_p = np.concatenate([wrq[:, h * 64 + PERM64] for h in range(16)], axis=1)
    brq_p = np.concatenate([brq[h * 64 + PERM64] for h in range(16)])
    wrk_p = nb(inp["L0_attn_w_rk"])[:, PERM64]
    brk_p = nb(inp["L0_attn_b_rk"])[PERM64]

    common = {
        "ident": ident, "cos0": cos0, "sin0": sin0,
        "cos1": cos1, "sin1": sin1,
        "w_dkv": nb(inp["L0_attn_w_dkv"]), "b_dkv": to128(inp["L0_attn_b_dkv"]),
        "w_rk": np.ascontiguousarray(wrk_p),
        "b_rk": np.ascontiguousarray(brk_p.reshape(2, 32).T),
        "w_dq": nb(inp["L0_attn_w_dq"]), "b_dq": to128(inp["L0_attn_b_dq"]),
        "w_rq": np.ascontiguousarray(wrq_p),
        "b_rq": np.ascontiguousarray(brq_p.reshape(32, 32).T),
        "w_uk": nb(inp["L0_attn_w_uk"]), "b_uk": to128(inp["L0_attn_b_uk"]),
        "w_uv": nb(inp["L0_attn_w_uv"]),
        "b_uv": nb(inp["L0_attn_b_uv"]).reshape(1, D),
        "w_uq": nb(inp["L0_attn_w_uq"]), "b_uq": to128(inp["L0_attn_b_uq"]),
        "w_o": nb(inp["L0_attn_w_o"]), "b_o": to128(inp["L0_attn_b_o"]),
        "gw": nb(inp["L0_ffn_gate_w"]), "uw": nb(inp["L0_ffn_up_w"]),
        "dw": nb(inp["L0_ffn_down_w"]),
        "gsc": to128(inp["L0_ffn_gate_scale"]),
        "usc": to128(inp["L0_ffn_up_scale"]),
        "dsc": to128(inp["L0_ffn_down_scale"]),
        "w_qkvT": np.ascontiguousarray(nb(inp["L1_attn_w_qkv"]).T),
        "w_oT": np.ascontiguousarray(nb(inp["L1_attn_w_o"]).T),
        "gate_wT": np.ascontiguousarray(nb(inp["L1_ffn_gate_w"]).T),
        "w_gate": nb(inp["L1_ffn_w_gate"]), "w_up": nb(inp["L1_ffn_w_up"]),
        "w_down": nb(inp["L1_ffn_w_down"]),
        "sw_gate": nb(inp["L1_ffn_sw_gate"]), "sw_up": nb(inp["L1_ffn_sw_up"]),
        "sw_down": nb(inp["L1_ffn_sw_down"]),
    }
    x = nb(inp["L0_attn_x"])
    tokidx = [own_token_index(j) for j in range(4)]
    masks = [build_masks(j) for j in range(4)]
    l0_tok = [np.concatenate([np.arange(z * CHUNK, (z + 1) * CHUNK)
                              for z in zig_chunks(par)]) for par in range(2)]
    cmask0s = [build_masks_l0(par) for par in range(2)]
    nonce = np.uint32(np.random.default_rng().integers(1, 2**32 - 1))
    in_maps = []
    for c in range(8):
        b, j, par = c // 4, c % 4, c % 2
        m = dict(common)
        m["x_full"] = np.ascontiguousarray(x[b])
        m["x_own"] = np.ascontiguousarray(x[b][l0_tok[par]])
        m["cmask0"] = cmask0s[par]
        m["cmask"] = masks[j]
        m["cos0q"] = np.ascontiguousarray(cos0[:, l0_tok[par]])
        m["sin0q"] = np.ascontiguousarray(sin0[:, l0_tok[par]])
        m["cos1q"] = np.ascontiguousarray(cos1[:, tokidx[j]])
        m["sin1q"] = np.ascontiguousarray(sin1[:, tokidx[j]])
        m["scat_idx"] = np.ascontiguousarray(
            l0_tok[par].astype(np.uint32).reshape(16, 128).T)
        m["gath_idx"] = np.ascontiguousarray(
            tokidx[j].astype(np.uint32).reshape(8, 128).T)
        m["flag_idx"] = np.array([[par], [par + 2]], np.uint32)
        m["nonce_in"] = np.array([[nonce], [nonce]], np.uint32)
        in_maps.append(m)
    nc = build_fused()
    res = run_bass_kernel_spmd(nc, in_maps, core_ids=list(range(8)))
    out = np.zeros((B, S, D), np.float32)
    for c in range(8):
        b, j = c // 4, c % 4
        out[b][tokidx[j]] = res.results[c]["out_own"]
    return out
